# version 04
# baseline (speedup 1.0000x reference)
import numpy as np
import ml_dtypes
import concourse.bass as bass
import concourse.mybir as mybir
from concourse.bass_utils import run_bass_kernel_spmd

F32 = mybir.dt.float32
BF16 = mybir.dt.bfloat16
I32 = mybir.dt.int32
AF = mybir.ActivationFunctionType
ALU = mybir.AluOpType
AX = mybir.AxisListType

SEQ, DM, NT, NO = 2048, 2048, 16, 8
H, HKV, DH, FH = 16, 4, 128, 8192
NEG = -3.0e38
ATT_SCALE = float(np.float32(1.0) / np.sqrt(np.float32(DH)))

TWO_PI = 2.0 * np.pi
P2_1 = 6.28125
_r = TWO_PI - P2_1
P2_2 = float(np.float32(int(_r * 2 ** 22) / 2 ** 22))
P2_3 = float(np.float32(_r - P2_2))
INV_2PI = float(np.float32(1.0 / TWO_PI))
PIH = np.pi / 2.0
PIH_HI = float(np.float32(int(PIH * 2 ** 20) / 2 ** 20))
PIH_LO = float(np.float32(PIH - PIH_HI))
L64 = float(np.log(10000.0) / 64.0)
L64_HI = float(np.float32(int(L64 * 2 ** 18) / 2 ** 18))
L64_LO = float(np.float32(L64 - L64_HI))


class Builder:
    def __init__(self, dbg_names=()):
        self.nc = bass.Bass("TRN2", target_bir_lowering=False)
        self.events = []
        self.cnt = {}
        self.dbg_names = set(dbg_names)
        self.dbg_bufs = []
        self._ctxs = []
        self._ivbufs = []

    def sb(self, name, shape, dtype, phases=None, **kw):
        if phases is not None and name in self.dbg_names:
            phases = (phases[0], 7)
        if phases is None:
            cm = self.nc.sbuf_tensor(name, shape, dtype, **kw)
            t = cm.__enter__()
            self._ctxs.append(cm)
            return t
        t = self.nc.virtual_sbuf_tensor(name, list(shape), dtype)
        bpp = int(np.prod(shape[1:])) * mybir.dt.size(dtype)
        self._ivbufs.append((t, list(shape), dtype, bpp, phases[0], phases[1]))
        return t

    def pack(self):
        free0 = (self.nc.sbuf_base + 511) & ~511
        cap = self.nc.sbuf_bytes_remaining - (free0 - self.nc.sbuf_base)
        placed = []
        assign = {}
        for (t, shape, dtype, bpp, lo, hi) in sorted(
                self._ivbufs, key=lambda x: -x[3]):
            size = (bpp + 511) & ~511
            cands = sorted({0} | {p[0] + p[1] for p in placed})
            off = None
            for c in cands:
                if c + size > cap:
                    continue
                ok = True
                for (p0, p1, p2, p3) in placed:
                    if not (hi < p2 or lo > p3):
                        if not (c + size <= p0 or c >= p0 + p1):
                            ok = False
                            break
                if ok:
                    off = c
                    break
            assert off is not None, \
                f"pack failed for {t.name} size {size} (cap {cap})"
            placed.append((off, size, lo, hi))
            assign[t.name] = (off, shape, dtype, t)
        total = max(p[0] + p[1] for p in placed)
        for name, (off, shape, dtype, t) in assign.items():
            bk = self.nc.alloc_sbuf_tensor_at(
                name + "_bk", shape, dtype, offset=free0 + off)
            self.nc.back_virtual_tensor_with(t, bk)
        return total, cap

    def ps(self, name, shape):
        cm = self.nc.psum_tensor(name, shape, F32)
        t = cm.__enter__()
        self._ctxs.append(cm)
        return t

    def sem(self, name):
        cm = self.nc.semaphore(name)
        s = cm.__enter__()
        self._ctxs.append(cm)
        return s

    def ev(self, tag, fn, waits=(), sem=None, inc=0):
        self.events.append((tag, fn, tuple((s, v) for (s, v) in waits if v > 0),
                            sem, inc))
        if sem is not None and inc:
            self.cnt[sem] = self.cnt.get(sem, 0) + inc
        return self.cnt.get(sem, 0)

    def val(self, sem):
        return self.cnt.get(sem, 0)

    def dbg(self, name, buf):
        if name in self.dbg_names:
            shape = [buf.shape[0], int(np.prod(buf.shape[1:]))]
            d = self.nc.dram_tensor("dbg_" + name, shape, buf.dtype,
                                    kind="ExternalOutput").ap()
            self.dbg_bufs.append((name, buf, d))

    def emit(self):
        P = self
        with self.nc.Block() as block:
            @block.sync
            def _(s):
                run_events(s, P, "s")

            @block.gpsimd
            def _(g):
                run_events(g, P, "g")

            @block.vector
            def _(v):
                run_events(v, P, "v")

            @block.scalar
            def _(sc):
                run_events(sc, P, "sc")

            @block.tensor
            def _(t):
                run_events(t, P, "t")
        for cm in reversed(self._ctxs):
            cm.__exit__(None, None, None)


def run_events(eng, P, tag):
    for (etag, fn, waits, sem, inc) in P.events:
        if etag != tag:
            continue
        for (s, v) in waits:
            eng.wait_ge(s, v)
        inst = fn(eng)
        if sem is not None and inc and inst is not None:
            inst.then_inc(sem, inc)


def build(dbg_names=()):
    B = Builder(dbg_names)
    nc = B.nc

    d_xseq = nc.dram_tensor("xseq", [SEQ, DM], F32, kind="ExternalInput").ap()
    d_xown = nc.dram_tensor("xown", [NO * 128, DM], F32, kind="ExternalInput").ap()
    d_pm = nc.dram_tensor("pm", [128, 1], F32, kind="ExternalInput").ap()
    d_pm1 = nc.dram_tensor("pm1", [128, 1], F32, kind="ExternalInput").ap()
    d_wq = nc.dram_tensor("wq", [DM, DM], F32, kind="ExternalInput").ap()
    d_wk = nc.dram_tensor("wk", [512, DM], F32, kind="ExternalInput").ap()
    d_wv = nc.dram_tensor("wv", [512, DM], F32, kind="ExternalInput").ap()
    d_wo = nc.dram_tensor("wo", [DM, DM], F32, kind="ExternalInput").ap()
    d_wgs = nc.dram_tensor("wgs", [H, DM], F32, kind="ExternalInput").ap()
    d_wgw = nc.dram_tensor("wgw", [H, DM], F32, kind="ExternalInput").ap()
    d_wg = nc.dram_tensor("wg", [FH, DM], F32, kind="ExternalInput").ap()
    d_wu = nc.dram_tensor("wu", [FH, DM], F32, kind="ExternalInput").ap()
    d_wd = nc.dram_tensor("wd", [DM, FH], F32, kind="ExternalInput").ap()
    d_out = nc.dram_tensor("oown", [NO * 128, DM], F32, kind="ExternalOutput").ap()

    DXA = [B.sem("DXA0"), B.sem("DXA1")]
    DW = [B.sem("DW0"), B.sem("DW1")]
    DTN = [B.sem("DTN0"), B.sem("DTN1")]
    DF = [B.sem("DF0"), B.sem("DF1")]
    DQ = [B.sem("DQ0"), B.sem("DQ1")]
    DK = [B.sem("DK0"), B.sem("DK1")]
    DP = [B.sem("DP0"), B.sem("DP1")]
    DS = [B.sem("DS0"), B.sem("DS1")]
    DDB = B.sem("DDB")
    CV = B.sem("CV")
    CS = B.sem("CS")
    CT = B.sem("CT")
    CG = B.sem("CG")

    psA = B.ps("psA", [128, 2048])
    psB = B.ps("psB", [128, 2048])

    ALLSEMS = DXA + DW + DTN + DF + DQ + DK + DP + DS + [CV, CS, CT, CG]

    def barrier(name):
        snap = [(sm, B.val(sm)) for sm in ALLSEMS if B.val(sm) > 0]
        for tag in ("s", "g", "v", "sc", "t"):
            B.ev(tag, (lambda e: None), waits=snap)

    ones_row = B.sb("ones_row", [1, 128], F32)
    ident = B.sb("ident", [128, 128], F32)
    mdiag = B.sb("mdiag", [128, 128], F32)
    mpre = B.sb("mpre", [128, 128], F32)
    mlast = B.sb("mlast", [128, 128], F32)
    pmb = B.sb("pmb", [128, 1], F32)
    pm1b = B.sb("pm1b", [128, 1], F32)
    pm1neg = B.sb("pm1neg", [128, 1], F32)
    wrow = B.sb("wrow", [1, 64], F32)
    omega = B.sb("omega", [128, 64], F32)
    tgi = B.sb("tgi", [128, 64], F32)
    rr1 = B.sb("rr1", [128, 64], F32)
    rr2 = B.sb("rr2", [128, 64], F32)
    rri = B.sb("rri", [128, 64], I32)
    rrk = B.sb("rrk", [128, 64], F32)
    zbQ = B.sb("zbQ", [128, 256], F32)
    sums = B.sb("sums", [128, 24], F32)
    sumsq = B.sb("sumsq", [128, 24], F32)
    mean_ = B.sb("mean_", [128, 24], F32)
    var_ = B.sb("var_", [128, 24], F32)
    rstd = B.sb("rstd", [128, 24], F32)
    nrs = B.sb("nrs", [128, 24], F32)
    bias_ = B.sb("bias_", [128, 24], F32)
    var2 = B.sb("var2", [128, 8], F32)
    rstd2 = B.sb("rstd2", [128, 8], F32)
    n8 = B.sb("n8", [128, 8], F32)
    amax = B.sb("amax", [128, 8], F32)
    zsc = B.sb("zsc", [128, 8], F32)
    ninvs = B.sb("ninvs", [128, 8], F32)
    gtm = B.sb("gtm", [128, NO, H], F32)
    gsig = B.sb("gsig", [128, 32], F32)
    lsum = B.sb("lsum", [128, 1], F32)
    mrow = B.sb("mrow", [128, 2], F32)
    lsum2 = B.sb("lsum2", [128, 2], F32)
    mrow2 = B.sb("mrow2", [128, 6], F32)
    wnat = [B.sb(f"wnat{i}", [128, DM], BF16) for i in range(2)]

    costm = B.sb("costm", [128, NT, 64], F32, phases=(0, 2))
    sintm = B.sb("sintm", [128, NT, 64], F32, phases=(0, 2))
    wkT = B.sb("wkT", [128, 16, 4, 128], BF16, phases=(0, 2))
    wvT = B.sb("wvT", [128, 16, 4, 128], BF16, phases=(0, 2))
    wgT = B.sb("wgT", [128, 16, 32], BF16, phases=(0, 2))
    sqs = B.sb("sqs", [128, DM], F32, phases=(0, 0))
    xsA = [B.sb(f"xsA{i}", [128, DM], F32, phases=(0, 2)) for i in range(2)]
    wqTh = B.sb("wqTh", [128, 16, 8, 128], BF16, phases=(0, 1))
    xnbo = B.sb("xnbo", [128, NO, DM], BF16, phases=(1, 1))
    xfm = [B.sb(f"xfm{i}", [128, NT, 128], BF16, phases=(1, 2)) for i in range(2)]
    qtm = [B.sb(f"qtm{i}", [128, 1024], BF16, phases=(1, 1)) for i in range(2)]
    crep = B.sb("crep", [128, 512], F32, phases=(1, 2))
    srep = B.sb("srep", [128, 512], F32, phases=(1, 2))
    zbW = B.sb("zbW", [128, 2048], F32, phases=(1, 2))
    qfm = B.sb("qfm", [128, NO, H, 128], BF16, phases=(1, 3))
    xnb = [B.sb(f"xnb{i}", [128, DM], BF16, phases=(2, 2)) for i in range(2)]
    ktm = [B.sb(f"ktm{i}", [128, 512], BF16, phases=(2, 2)) for i in range(2)]
    kfm = B.sb("kfm", [128, HKV, NT, 128], BF16, phases=(2, 3))
    vtm = B.sb("vtm", [128, NT, 512], BF16, phases=(2, 3))
    pexp = [B.sb(f"pexp{i}", [128, 2048], F32, phases=(3, 3)) for i in range(2)]
    pbf = [B.sb(f"pbf{i}", [128, 2048], BF16, phases=(3, 3)) for i in range(2)]
    PT = [B.sb(f"PT{i}", [128, 16, 4, 128], BF16, phases=(3, 3)) for i in range(2)]
    attn = B.sb("attn", [128, 16, NO, 128], BF16, phases=(3, 4))
    oacc = B.sb("oacc", [128, NO, 2048], F32, phases=(4, 7))
    woT = B.sb("woT", [128, 16, 4, 128], BF16, phases=(4, 4))
    xsB = [B.sb(f"xsB{i}", [128, DM], F32, phases=(5, 5)) for i in range(2)]
    xnf = B.sb("xnf", [128, DM], F32, phases=(5, 5))
    zb = B.sb("zb", [128, DM], F32, phases=(5, 5))
    zi = B.sb("zi", [128, DM], I32, phases=(5, 5))
    zk = B.sb("zk", [128, DM], F32, phases=(5, 5))
    zd = B.sb("zd", [128, DM], F32, phases=(5, 5))
    xqtm = [B.sb(f"xqtm{i}", [128, DM], BF16, phases=(5, 5)) for i in range(2)]
    rowb = B.sb("rowb", [1, 1024], F32, phases=(5, 5))
    xqfm = B.sb("xqfm", [128, 16, NO, 128], BF16, phases=(5, 6))
    nb_full = B.sb("nb_full", [128, 1024], F32, phases=(5, 6))
    pq = B.sb("pq", [128, 16, NO, 128], BF16, phases=(6, 6))
    wgnT = [B.sb(f"wgnT{i}", [128, 16, 128], BF16, phases=(6, 6)) for i in range(2)]
    wdT = [B.sb(f"wdT{i}", [128, 16, 4, 128], BF16, phases=(6, 6)) for i in range(2)]
    ffa = B.sb("ffa", [128, 1024], F32, phases=(6, 6))
    fft = B.sb("fft", [128, 1024], F32, phases=(6, 6))
    ffu = B.sb("ffu", [128, 1024], F32, phases=(6, 6))
    xsC = [B.sb(f"xsC{i}", [128, DM], F32, phases=(7, 7)) for i in range(2)]
    ost = [B.sb(f"ost{i}", [128, DM], F32, phases=(7, 7)) for i in range(2)]

    def g_const(g):
        g.memset(ones_row[:], 1.0)
        g.memset(mdiag[:], 0.0)
        g.memset(ident[:], 1.0)
        g.drain()
        g.affine_select(mdiag[:], mdiag[:], [[-1, 128]], ALU.is_ge, NEG,
                        base=0, channel_multiplier=1)
        g.affine_select(ident[:], ident[:], [[-1, 128]], ALU.is_equal, 0.0,
                        base=0, channel_multiplier=1)
        g.iota(wrow[:], [[1, 64]], base=0, channel_multiplier=0,
               allow_small_or_imprecise_dtypes=True)
        return g.drain()
    cg_const = B.ev("g", g_const, sem=CG, inc=1)

    def s_pm(s):
        s.dma_start(out=pmb[:], in_=d_pm).then_inc(DXA[0], 16)
        s.dma_start(out=pm1b[:], in_=d_pm1).then_inc(DXA[1], 16)
        return None
    B.ev("s", s_pm)
    B.cnt[DXA[0]] = B.cnt.get(DXA[0], 0) + 16
    B.cnt[DXA[1]] = B.cnt.get(DXA[1], 0) + 16

    def v_masks(v):
        v.tensor_scalar_mul(pm1neg[:], pm1b[:], NEG)
        v.tensor_scalar_mul(mpre[:], mdiag[:], pm1b[:, 0:1])
        v.tensor_scalar_mul(mlast[:], mdiag[:], pmb[:, 0:1])
        v.drain()
        v.tensor_scalar_add(mlast[:], mlast[:], pm1neg[:, 0:1])
        v.tensor_scalar_mul(rr1[0:1, :], wrow[:], -L64_HI)
        v.drain()
        v.scalar_tensor_tensor(rr1[0:1, :], wrow[:], -L64_LO, rr1[0:1, :],
                               ALU.mult, ALU.add)
        return v.drain()
    cv_masks = B.ev("v", v_masks,
                    waits=[(CG, cg_const), (DXA[0], 16), (DXA[1], 16)],
                    sem=CV, inc=1)

    def s_omega(s):
        return s.activation(wrow[:], rr1[0:1, :], AF.Exp)
    cs_omega = B.ev("sc", s_omega, waits=[(CV, cv_masks)], sem=CS, inc=1)

    def t_omega(t):
        return t.matmul(psB[:, 0:64], ones_row[:], wrow[:], start=True, stop=True)
    ct_omega = B.ev("t", t_omega, waits=[(CS, cs_omega)], sem=CT, inc=1)

    def v_omega_ev(v):
        v.tensor_copy(omega[:], psB[:, 0:64])
        return v.drain()
    cv_omega = B.ev("v", v_omega_ev, waits=[(CT, ct_omega)], sem=CV, inc=1)

    def rr_chain(v, src):
        v.tensor_scalar(rr2[:], src, INV_2PI, 0.5, ALU.mult, ALU.add)
        v.drain()
        v.tensor_copy(rri[:], rr2[:])
        v.drain()
        v.tensor_copy(rrk[:], rri[:])
        v.drain()
        v.tensor_sub(rr2[:], rr2[:], rrk[:])
        v.drain()
        v.tensor_scalar(rr2[:], rr2[:], 0.0, None, ALU.is_lt)
        v.drain()
        v.tensor_sub(rrk[:], rrk[:], rr2[:])
        v.drain()
        v.scalar_tensor_tensor(rr2[:], rrk[:], -P2_1, src, ALU.mult, ALU.add)
        v.drain()
        v.scalar_tensor_tensor(rr2[:], rrk[:], -P2_2, rr2[:], ALU.mult, ALU.add)
        v.drain()
        v.scalar_tensor_tensor(rr2[:], rrk[:], -P2_3, rr2[:], ALU.mult, ALU.add)
        v.drain()

    cs_cos_prev = 0
    for g in range(NT):
        def g_iota(gp, g=g):
            gp.iota(tgi[:], [[0, 64]], base=128 * g, channel_multiplier=1,
                    allow_small_or_imprecise_dtypes=True)
            return gp.drain()
        cg_i = B.ev("g", g_iota, waits=[(CV, B.val(CV) if g > 0 else 0)],
                    sem=CG, inc=1)

        def v_theta(v, g=g, csp=cs_cos_prev):
            if csp:
                v.wait_ge(CS, csp)
            v.tensor_mul(rr1[:], tgi[:], omega[:])
            v.drain()
            rr_chain(v, rr1[:])
            return v.drain()
        cv_sin = B.ev("v", v_theta, waits=[(CG, cg_i), (CV, cv_omega)],
                      sem=CV, inc=1)

        def s_sin(s, g=g):
            return s.activation(sintm[:, g, :], rr2[:], AF.Sin)
        cs_sin = B.ev("sc", s_sin, waits=[(CV, cv_sin)], sem=CS, inc=1)

        def v_theta2(v, g=g):
            v.tensor_scalar_add(rr1[:], rr1[:], PIH_HI)
            v.drain()
            v.tensor_scalar_add(rr1[:], rr1[:], PIH_LO)
            v.drain()
            rr_chain(v, rr1[:])
            return v.drain()
        cv_cos = B.ev("v", v_theta2, waits=[(CS, cs_sin)], sem=CV, inc=1)

        def s_cos(s, g=g):
            return s.activation(costm[:, g, :], rr2[:], AF.Sin)
        cs_cos_prev = B.ev("sc", s_cos, waits=[(CV, cv_cos)], sem=CS, inc=1)

    B.dbg("sintm", sintm)
    B.dbg("costm", costm)

    wnat_rel = [0, 0]

    class WS:
        idx = 0

    def wstream(src, rows, tpfn, extra_tp_wait_ct=0, sign=False):
        slot = WS.idx % 2
        WS.idx += 1

        def g_wload(g, src=src, rows=rows, slot=slot, rel=wnat_rel[slot]):
            if rel:
                g.wait_ge(DTN[slot], rel)
            return g.dma_start(out=wnat[slot][0:rows, :], in_=src)
        dw = B.ev("g", g_wload, sem=DW[slot], inc=16)

        cs_sg = 0
        if sign:
            def s_sign(s, slot=slot, rows=rows, dw=dw):
                s.wait_ge(DW[slot], dw)
                return s.activation(wnat[slot][0:rows, :], wnat[slot][0:rows, :],
                                    AF.Sign)
            cs_sg = B.ev("sc", s_sign, sem=CS, inc=1)

        def s_wtp(s, tpfn=tpfn, slot=slot, dw=dw, cs=cs_sg, xw=extra_tp_wait_ct):
            if cs:
                s.wait_ge(CS, cs)
            else:
                s.wait_ge(DW[slot], dw)
            if xw:
                s.wait_ge(CT, xw)
            return tpfn(s, slot)
        wnat_rel[slot] = B.ev("s", s_wtp, sem=DTN[slot], inc=16)
        return (DTN[slot], wnat_rel[slot])

    for b in range(4):
        wstream(d_wk[b * 128:(b + 1) * 128, :], 128,
                lambda s, slot, b=b: s.dma_start(
                    out=wkT[:, :, b, :], in_=wnat[slot][:], transpose=True))
    for b in range(4):
        wstream(d_wv[b * 128:(b + 1) * 128, :], 128,
                lambda s, slot, b=b: s.dma_start(
                    out=wvT[:, :, b, :], in_=wnat[slot][:], transpose=True))
    wstream(d_wgs, 16, lambda s, slot: s.dma_start(
        out=wgT[:, :, 0:16], in_=wnat[slot][0:16, :], transpose=True))
    wstream(d_wgw, 16, lambda s, slot: s.dma_start(
        out=wgT[:, :, 16:32], in_=wnat[slot][0:16, :], transpose=True))
    for b in range(8):
        wstream(d_wq[b * 128:(b + 1) * 128, :], 128,
                lambda s, slot, b=b: s.dma_start(
                    out=wqTh[:, :, b, :], in_=wnat[slot][:], transpose=True))
    DTN_attw = (B.val(DTN[0]), B.val(DTN[1]))
    B.dbg("wkT", wkT)

    xsrc = [(d_xseq, t, t) for t in range(NT)] + \
           [(d_xown, t, NT + t) for t in range(NO)]
    xrel = [(0, 0)] * 2
    for idx, (src, t, col) in enumerate(xsrc):
        slot = idx % 2

        def s_ldx(s, src=src, t=t, slot=slot, rel=xrel[slot]):
            if rel[0]:
                s.wait_ge(CV, rel[0])
            if rel[1]:
                s.wait_ge(CS, rel[1])
            return s.dma_start(out=xsA[slot][:], in_=src[t * 128:(t + 1) * 128, :])
        dx = B.ev("s", s_ldx, sem=DXA[slot], inc=16)

        def v_sum(v, slot=slot, col=col):
            v.reduce_sum(sums[:, col:col + 1], xsA[slot][:], axis=AX.X)
            return v.drain()
        cv_h = B.ev("v", v_sum, waits=[(DXA[slot], dx)], sem=CV, inc=1)

        def s_sq(s, slot=slot, col=col):
            s.activation(sqs[:], xsA[slot][:], AF.Square,
                         accum_out=sumsq[:, col:col + 1])
            return s.drain()
        cs_h = B.ev("sc", s_sq, waits=[(DXA[slot], dx)], sem=CS, inc=1)
        xrel[slot] = (cv_h, cs_h)

    def nr_rsqrt(v, r, vv, scratch, iters=6):
        v.memset(r, 1.0)
        v.drain()
        for _ in range(iters):
            v.tensor_mul(scratch, r, r)
            v.drain()
            v.tensor_mul(scratch, scratch, vv)
            v.drain()
            v.tensor_scalar(scratch, scratch, -0.5, 1.5, ALU.mult, ALU.add)
            v.drain()
            v.tensor_mul(r, r, scratch)
            v.drain()

    def v_stats(v):
        v.tensor_scalar_mul(mean_[:], sums[:], 1.0 / DM)
        v.tensor_scalar_mul(var_[:], sumsq[:], 1.0 / DM)
        v.drain()
        v.tensor_mul(nrs[:], mean_[:], mean_[:])
        v.drain()
        v.tensor_sub(var_[:], var_[:], nrs[:])
        v.drain()
        v.tensor_scalar_add(var_[:], var_[:], 1e-5)
        v.drain()
        nr_rsqrt(v, rstd[:], var_[:], nrs[:])
        v.tensor_mul(bias_[:], mean_[:], rstd[:])
        v.drain()
        v.tensor_scalar_mul(bias_[:], bias_[:], -1.0)
        v.tensor_scalar_add(var2[:], var_[:, 16:24], -1e-5)
        v.drain()
        v.tensor_mul(n8[:], rstd[:, 16:24], rstd[:, 16:24])
        v.drain()
        v.tensor_mul(var2[:], var2[:], n8[:])
        v.drain()
        v.tensor_scalar_add(var2[:], var2[:], 1e-8)
        v.drain()
        nr_rsqrt(v, rstd2[:], var2[:], n8[:])
        return v.drain()
    cv_stats = B.ev("v", v_stats,
                    waits=[(CV, B.val(CV)), (CS, B.val(CS))], sem=CV, inc=1)
    B.dbg("rstd", rstd)
    B.dbg("rstd2", rstd2)
    B.dbg("bias_", bias_)

    barrier("pc")

    relC_ct = [0, 0]
    relC_dtq = [0, 0]
    cv_rope_prev = 0
    rope_hist = []
    cs_gate_prev = 0
    xrelC = list(xrel)
    cs_xnbo = {}
    ct_last_q_half0 = 0
    for half in range(2):
        if half == 1:
            for b in range(8):
                wstream(d_wq[(8 + b) * 128:(9 + b) * 128, :], 128,
                        (lambda s, slot, b=b: s.dma_start(
                            out=wqTh[:, :, b, :], in_=wnat[slot][:],
                            transpose=True)),
                        extra_tp_wait_ct=ct_last_q_half0)
        dt_wq_half = (B.val(DTN[0]), B.val(DTN[1]))

        for j in range(NO):
            fslot = j % 2
            if True:
                cv_rope_m2 = rope_hist[-2] if len(rope_hist) >= 2 else 0
            if half == 0:
                slot = j % 2

                def s_ldxo(s, j=j, slot=slot, rel=xrelC[slot]):
                    if rel[0]:
                        s.wait_ge(CV, rel[0])
                    if rel[1]:
                        s.wait_ge(CS, rel[1])
                    return s.dma_start(out=xsA[slot][:],
                                       in_=d_xown[j * 128:(j + 1) * 128, :])
                dx = B.ev("s", s_ldxo, sem=DXA[slot], inc=16)

                def s_xn(s, j=j, slot=slot, dx=dx):
                    s.wait_ge(DXA[slot], dx)
                    return s.activation(xnbo[:, j, :], xsA[slot][:], AF.Identity,
                                        bias=bias_[:, 16 + j:17 + j],
                                        scale=rstd[:, 16 + j:17 + j])
                cs_xn = B.ev("sc", s_xn, waits=[(CV, cv_stats)], sem=CS, inc=1)
                cs_xnbo[j] = cs_xn
                xrelC[slot] = (0, cs_xn)

            def s_tpxc(s, j=j, fslot=fslot, csw=None, relct=relC_ct[fslot]):
                s.wait_ge(CS, cs_xnbo[j])
                if relct:
                    s.wait_ge(CT, relct)
                return s.dma_start(out=xfm[fslot][:, 0:16, :], in_=xnbo[:, j, :],
                                   transpose=True)
            dt_x = B.ev("s", s_tpxc, sem=DF[fslot], inc=16)

            def t_q(t, j=j, fslot=fslot, half=half, dt=dt_x,
                    cvprev=cv_rope_m2, csg_prev=cs_gate_prev, dtw=dt_wq_half):
                t.wait_ge(DF[fslot], dt)
                t.wait_ge(DTN[0], dtw[0])
                t.wait_ge(DTN[1], dtw[1])
                if cvprev:
                    t.wait_ge(CV, cvprev)
                if half == 0 and csg_prev:
                    t.wait_ge(CS, csg_prev)
                off = 0 if (j % 2 == 0) else 1024
                for k in range(16):
                    lhsT = xfm[fslot][:, k, :]
                    for nn in range(2):
                        t.matmul(psA[:, off + nn * 512:off + (nn + 1) * 512],
                                 lhsT, wqTh[:, k, nn * 4:(nn + 1) * 4, :],
                                 start=(k == 0), stop=(k == 15))
                    if half == 0:
                        t.matmul(psB[:, 1792:1824], lhsT, wgT[:, k, :],
                                 start=(k == 0), stop=(k == 15))
                return t.drain()
            ct_q = B.ev("t", t_q, sem=CT, inc=1)
            relC_ct[fslot] = ct_q

            if half == 0:
                def s_gate(s, j=j, ct=ct_q):
                    s.wait_ge(CT, ct)
                    return s.activation(gsig[:], psB[:, 1792:1824], AF.Tanh,
                                        scale=0.5)
                cs_gate = B.ev("sc", s_gate, sem=CS, inc=1)
                cs_gate_prev = cs_gate
            else:
                cs_gate = 0

            def v_rope_q(v, j=j, fslot=fslot, half=half, ct=ct_q, csg=cs_gate,
                         reldq=relC_dtq[fslot]):
                if reldq:
                    v.wait_ge(DQ[fslot], reldq)
                v.wait_ge(CT, ct)
                off = 0 if (j % 2 == 0) else 1024
                v.tensor_scalar_mul(crep[:, 0:64], costm[:, 2 * j, :], pm1b[:, 0:1])
                v.tensor_scalar_mul(srep[:, 0:64], sintm[:, 2 * j, :], pm1b[:, 0:1])
                v.drain()
                v.scalar_tensor_tensor(crep[:, 0:64], costm[:, 2 * j + 1, :],
                                       pmb[:, 0:1], crep[:, 0:64], ALU.mult, ALU.add)
                v.scalar_tensor_tensor(srep[:, 0:64], sintm[:, 2 * j + 1, :],
                                       pmb[:, 0:1], srep[:, 0:64], ALU.mult, ALU.add)
                v.drain()
                for w in (64, 128, 256):
                    v.tensor_copy(crep[:, w:2 * w], crep[:, 0:w])
                    v.tensor_copy(srep[:, w:2 * w], srep[:, 0:w])
                    v.drain()
                ps3 = psA[:, off:off + 1024].rearrange("p (h t) -> p h t", t=128)
                x1 = ps3[:, :, 0:64]
                x2 = ps3[:, :, 64:128]
                c3 = crep[:].rearrange("p (h t) -> p h t", t=64)
                s3 = srep[:].rearrange("p (h t) -> p h t", t=64)
                o3 = qtm[fslot][:].rearrange("p (h t) -> p h t", t=128)
                v.tensor_mul(zbW[:, 0:512], x1, c3)
                v.tensor_mul(zbW[:, 512:1024], x2, s3)
                v.tensor_mul(zbW[:, 1024:1536], x2, c3)
                v.tensor_mul(zbW[:, 1536:2048], x1, s3)
                v.drain()
                v.tensor_sub(o3[:, :, 0:64],
                             zbW[:, 0:512].rearrange("p (h t) -> p h t", t=64),
                             zbW[:, 512:1024].rearrange("p (h t) -> p h t", t=64))
                v.tensor_add(o3[:, :, 64:128],
                             zbW[:, 1024:1536].rearrange("p (h t) -> p h t", t=64),
                             zbW[:, 1536:2048].rearrange("p (h t) -> p h t", t=64))
                if half == 0:
                    v.wait_ge(CS, csg)
                    v.tensor_add(gsig[:, 0:16], gsig[:, 0:16], gsig[:, 16:32])
                    v.drain()
                    v.tensor_scalar(gtm[:, j, :], gsig[:, 0:16], 0.5, 1.0,
                                    ALU.mult, ALU.add)
                return v.drain()
            cv_rope_prev = B.ev("v", v_rope_q, sem=CV, inc=1)
            rope_hist.append(cv_rope_prev)

            def s_tpq(s, j=j, fslot=fslot, half=half, cv=cv_rope_prev):
                s.wait_ge(CV, cv)
                return s.dma_start(out=qfm[:, j, 8 * half:8 * half + 8, :],
                                   in_=qtm[fslot][:], transpose=True)
            relC_dtq[fslot] = B.ev("s", s_tpq, sem=DQ[fslot], inc=16)
        if half == 0:
            ct_last_q_half0 = B.val(CT)

    B.dbg("qfm", qfm)
    B.dbg("gtm", gtm)

    barrier("pb")

    xrelB = list(xrelC)
    cv_kv_par = [0, 0]
    relB_dtx = [0, 0]
    relB_ctx = list(relC_ct)
    relB_dtk = [0, 0]
    for t_i in range(NT):
        slot = t_i % 2
        fslot = t_i % 2

        def s_ldxs(s, t_i=t_i, slot=slot, rel=xrelB[slot]):
            if rel[0]:
                s.wait_ge(CV, rel[0])
            if rel[1]:
                s.wait_ge(CS, rel[1])
            return s.dma_start(out=xsA[slot][:],
                               in_=d_xseq[t_i * 128:(t_i + 1) * 128, :])
        dx = B.ev("s", s_ldxs, sem=DXA[slot], inc=16)

        def s_xn2(s, t_i=t_i, slot=slot, fslot=fslot, dx=dx, rel=relB_dtx[fslot]):
            s.wait_ge(DXA[slot], dx)
            if rel:
                s.wait_ge(DF[fslot], rel)
            return s.activation(xnb[fslot][:], xsA[slot][:], AF.Identity,
                                bias=bias_[:, t_i:t_i + 1],
                                scale=rstd[:, t_i:t_i + 1])
        cs_xn = B.ev("sc", s_xn2, waits=[(CV, cv_stats)], sem=CS, inc=1)
        xrelB[slot] = (0, cs_xn)

        def s_tpx2(s, fslot=fslot, cs=cs_xn, relct=relB_ctx[fslot]):
            s.wait_ge(CS, cs)
            if relct:
                s.wait_ge(CT, relct)
            return s.dma_start(out=xfm[fslot][:], in_=xnb[fslot][:], transpose=True)
        dt_x = B.ev("s", s_tpx2, sem=DF[fslot], inc=16)
        relB_dtx[fslot] = dt_x

        def t_kv(t, t_i=t_i, fslot=fslot, dt=dt_x, cvprev=cv_kv_par[t_i % 2]):
            t.wait_ge(DF[fslot], dt)
            t.wait_ge(DTN[0], DTN_attw[0])
            t.wait_ge(DTN[1], DTN_attw[1])
            if cvprev:
                t.wait_ge(CV, cvprev)
            off = 0 if (t_i % 2 == 0) else 1024
            for k in range(16):
                lhsT = xfm[fslot][:, k, :]
                t.matmul(psB[:, off:off + 512], lhsT, wkT[:, k, :, :],
                         start=(k == 0), stop=(k == 15))
                t.matmul(psB[:, off + 512:off + 1024], lhsT, wvT[:, k, :, :],
                         start=(k == 0), stop=(k == 15))
            return t.drain()
        ct_kv = B.ev("t", t_kv, sem=CT, inc=1)
        relB_ctx[fslot] = ct_kv

        def v_kv(v, t_i=t_i, fslot=fslot, ct=ct_kv, rel=relB_dtk[fslot]):
            v.wait_ge(CT, ct)
            if rel:
                v.wait_ge(DK[fslot], rel)
            off = 0 if (t_i % 2 == 0) else 1024
            v.tensor_copy(crep[:, 0:64], costm[:, t_i, :])
            v.tensor_copy(srep[:, 0:64], sintm[:, t_i, :])
            v.drain()
            for w in (64, 128):
                v.tensor_copy(crep[:, w:2 * w], crep[:, 0:w])
                v.tensor_copy(srep[:, w:2 * w], srep[:, 0:w])
                v.drain()
            ps3 = psB[:, off:off + 512].rearrange("p (h t) -> p h t", t=128)
            x1 = ps3[:, :, 0:64]
            x2 = ps3[:, :, 64:128]
            c3 = crep[:, 0:256].rearrange("p (h t) -> p h t", t=64)
            s3 = srep[:, 0:256].rearrange("p (h t) -> p h t", t=64)
            o3 = ktm[fslot][:].rearrange("p (h t) -> p h t", t=128)
            v.tensor_mul(zbW[:, 0:256], x1, c3)
            v.tensor_mul(zbW[:, 512:768], x2, s3)
            v.tensor_mul(zbW[:, 1024:1280], x2, c3)
            v.tensor_mul(zbW[:, 1536:1792], x1, s3)
            v.drain()
            v.tensor_sub(o3[:, :, 0:64],
                         zbW[:, 0:256].rearrange("p (h t) -> p h t", t=64),
                         zbW[:, 512:768].rearrange("p (h t) -> p h t", t=64))
            v.tensor_add(o3[:, :, 64:128],
                         zbW[:, 1024:1280].rearrange("p (h t) -> p h t", t=64),
                         zbW[:, 1536:1792].rearrange("p (h t) -> p h t", t=64))
            v.tensor_copy(vtm[:, t_i, :], psB[:, off + 512:off + 1024])
            return v.drain()
        cv_kv = B.ev("v", v_kv, sem=CV, inc=1)
        cv_kv_par[t_i % 2] = cv_kv

        def s_tpk(s, t_i=t_i, fslot=fslot, cv=cv_kv):
            s.wait_ge(CV, cv)
            return s.dma_start(out=kfm[:, :, t_i, :], in_=ktm[fslot][:],
                               transpose=True)
        relB_dtk[fslot] = B.ev("s", s_tpk, sem=DK[fslot], inc=16)

    B.dbg("kfm", kfm)
    B.dbg("vtm", vtm)
    DKQ_all = [(DK[0], B.val(DK[0])), (DK[1], B.val(DK[1])),
               (DQ[0], B.val(DQ[0])), (DQ[1], B.val(DQ[1]))]
    CV_kv_all = B.val(CV)

    barrier("pd")

    cs_exp_prev = 0
    cs_exp_byreg = [0, 0]
    cv_scale_prev = 0
    cv_sc_slot = [0, 0]
    dt_p_slot = [0, 0]
    pt_rel = [0, 0]
    dt_p = {}
    cv_pv_prev = 0
    for j in range(NO):
        nch = 2 * j + 2
        keys = 128 * nch
        for h in range(H):
            g = h // HKV
            hh = h % HKV
            pslot = h % 2
            tslot = (j * 4 + g) % 2
            reg = 1 if (h % 2 == 1 and j <= 5) else 0
            Sps = (psA[:, 0:keys] if reg == 0 else psB[:, 512:512 + keys])

            def t_S(t, j=j, h=h, g=g, nch=nch, Sps=Sps,
                    cse=cs_exp_byreg[reg]):
                if cse:
                    t.wait_ge(CS, cse)
                lhsT = qfm[:, j, h, :]
                for c0 in range(0, nch, 4):
                    cw = min(4, nch - c0)
                    t.matmul(Sps[:, c0 * 128:(c0 + cw) * 128], lhsT,
                             kfm[:, g, c0:c0 + cw, :], start=True, stop=True)
                return t.drain()
            ct_S = B.ev("t", t_S, waits=DKQ_all + [(CV, CV_kv_all)],
                        sem=CT, inc=1)

            def v_msk(v, j=j, nch=nch, keys=keys, Sps=Sps, ct=ct_S, pc=pslot):
                v.wait_ge(CT, ct)
                v.tensor_add(Sps[:, (nch - 2) * 128:(nch - 1) * 128],
                             Sps[:, (nch - 2) * 128:(nch - 1) * 128], mpre[:])
                v.tensor_add(Sps[:, (nch - 1) * 128:nch * 128],
                             Sps[:, (nch - 1) * 128:nch * 128], mlast[:])
                v.drain()
                v.reduce_max(mrow2[:, pc:pc + 1], Sps[:, 0:keys], axis=AX.X)
                v.drain()
                v.tensor_scalar_mul(mrow2[:, 2 + pc:3 + pc],
                                    mrow2[:, pc:pc + 1], -ATT_SCALE)
                return v.drain()
            cv_m = B.ev("v", v_msk, sem=CV, inc=1)

            def s_exp(s, keys=keys, pslot=pslot, cv=cv_m, Sps=Sps,
                      rel=cv_sc_slot[pslot]):
                s.wait_ge(CV, cv)
                if rel:
                    s.wait_ge(CV, rel)
                return s.activation(pexp[pslot][:, 0:keys], Sps[:, 0:keys],
                                    AF.Exp, bias=mrow2[:, 2 + pslot:3 + pslot],
                                    scale=ATT_SCALE,
                                    accum_out=lsum2[:, pslot:pslot + 1])
            cs_exp_prev = B.ev("sc", s_exp, sem=CS, inc=1)
            cs_exp_byreg[reg] = cs_exp_prev

            def v_scale(v, j=j, h=h, keys=keys, pslot=pslot, cs=cs_exp_prev,
                        reldt=dt_p_slot[pslot]):
                v.wait_ge(CS, cs)
                if reldt:
                    v.wait_ge(DP[pslot], reldt)
                v.reciprocal(mrow2[:, 4 + pslot:5 + pslot],
                             lsum2[:, pslot:pslot + 1])
                v.drain()
                v.tensor_mul(mrow2[:, 4 + pslot:5 + pslot],
                             mrow2[:, 4 + pslot:5 + pslot], gtm[:, j, h:h + 1])
                v.drain()
                v.tensor_scalar_mul(pbf[pslot][:, 0:keys], pexp[pslot][:, 0:keys],
                                    mrow2[:, 4 + pslot:5 + pslot])
                return v.drain()
            cv_sc = B.ev("v", v_scale, sem=CV, inc=1)
            cv_scale_prev = cv_sc
            cv_sc_slot[pslot] = cv_sc

            def s_tpp(s, nch=nch, keys=keys, hh=hh, pslot=pslot, tslot=tslot,
                      cv=cv_sc, relct=pt_rel[tslot]):
                s.wait_ge(CV, cv)
                if relct:
                    s.wait_ge(CT, relct)
                return s.dma_start(out=PT[tslot][:, 0:nch, hh, :],
                                   in_=pbf[pslot][:, 0:keys], transpose=True)
            dt_p[(j, h)] = B.ev("s", s_tpp, sem=DP[pslot], inc=16)
            dt_p_slot[pslot] = dt_p[(j, h)]

            if hh == HKV - 1:
                def t_PV(t, j=j, g=g, nch=nch, tslot=tslot,
                         dtp0=dt_p[(j, h - 1)], dtp1=dt_p[(j, h)],
                         cvp=cv_pv_prev):
                    t.wait_ge(DP[0], dtp0)
                    t.wait_ge(DP[1], dtp1)
                    if cvp:
                        t.wait_ge(CV, cvp)
                    for c in range(nch):
                        t.matmul(psB[:, 0:512], vtm[:, c, g * 128:(g + 1) * 128],
                                 PT[tslot][:, c, :, :],
                                 start=(c == 0), stop=(c == nch - 1))
                    return t.drain()
                ct_pv = B.ev("t", t_PV, waits=[(CV, CV_kv_all)], sem=CT, inc=1)
                pt_rel[tslot] = ct_pv

                def v_att(v, j=j, g=g, ct=ct_pv):
                    v.wait_ge(CT, ct)
                    v.tensor_copy(attn[:, 4 * g:4 * g + 4, j, :], psB[:, 0:512])
                    return v.drain()
                cv_pv_prev = B.ev("v", v_att, sem=CV, inc=1)

    B.dbg("attn", attn)
    CV_attn_done = B.val(CV)

    barrier("pe0")

    def psm(m):
        src = psA if m < 4 else psB
        return src[:, (m % 4) * 512:(m % 4 + 1) * 512]

    cv_evac_prev = 0
    ct_o = 0
    for n in range(4):
        dts = []
        for b in range(4):
            dts.append(wstream(
                d_wo[n * 512 + b * 128:n * 512 + (b + 1) * 128, :], 128,
                (lambda s, slot, b=b: s.dma_start(out=woT[:, :, b, :],
                                                  in_=wnat[slot][:],
                                                  transpose=True)),
                extra_tp_wait_ct=ct_o))

        for mg in range(2):
            def t_wo(t, n=n, mg=mg, dt=tuple(dts), cvp=cv_evac_prev):
                for (dsem, dval) in dt:
                    t.wait_ge(dsem, dval)
                if cvp:
                    t.wait_ge(CV, cvp)
                for m in range(4 * mg, 4 * mg + 4):
                    for k in range(16):
                        t.matmul(psm(m), attn[:, k, m, :], woT[:, k, :, :],
                                 start=(k == 0), stop=(k == 15))
                return t.drain()
            ct_o = B.ev("t", t_wo, waits=[(CV, CV_attn_done), (CS, B.val(CS))],
                        sem=CT, inc=1)

            def v_evo(v, n=n, mg=mg, ct=ct_o):
                v.wait_ge(CT, ct)
                for m in range(4 * mg, 4 * mg + 4):
                    v.tensor_copy(oacc[:, m, n * 512:(n + 1) * 512], psm(m))
                return v.drain()
            cv_evac_prev = B.ev("v", v_evo, sem=CV, inc=1)

    barrier("p45")

    xrelQ = [(cv_evac_prev, B.val(CS))] * 2
    relQ_dt = [B.val(DQ[0]), B.val(DQ[1])]
    cv_q_prev = 0
    for j in range(NO):
        slot = j % 2
        fslot = j % 2

        def s_ldxq(s, j=j, slot=slot, rel=xrelQ[slot]):
            if rel[0]:
                s.wait_ge(CV, rel[0])
            if rel[1]:
                s.wait_ge(CS, rel[1])
            return s.dma_start(out=xsB[slot][:], in_=d_xown[j * 128:(j + 1) * 128, :])
        dx = B.ev("s", s_ldxq, sem=DXA[slot], inc=16)

        def s_xn3(s, j=j, slot=slot, dx=dx, rel=cv_q_prev):
            s.wait_ge(DXA[slot], dx)
            if rel:
                s.wait_ge(CV, rel)
            return s.activation(xnf[:], xsB[slot][:], AF.Identity,
                                bias=bias_[:, 16 + j:17 + j],
                                scale=rstd[:, 16 + j:17 + j])
        cs_xn = B.ev("sc", s_xn3, sem=CS, inc=1)
        xrelQ[slot] = (0, cs_xn)

        def v_quant(v, j=j, fslot=fslot, cs=cs_xn, reldt=relQ_dt[fslot]):
            v.wait_ge(CS, cs)
            if reldt:
                v.wait_ge(DQ[fslot], reldt)
            xn = xnf[:]
            v.reduce_max(amax[:, j:j + 1], xn, axis=AX.X, apply_absolute_value=True)
            v.drain()
            v.tensor_mul(amax[:, j:j + 1], amax[:, j:j + 1], rstd2[:, j:j + 1])
            v.drain()
            v.reciprocal(n8[:, j:j + 1], amax[:, j:j + 1])
            v.drain()
            v.tensor_scalar_mul(n8[:, j:j + 1], n8[:, j:j + 1], 127.0)
            v.drain()
            v.reciprocal(ninvs[:, j:j + 1], n8[:, j:j + 1])
            v.tensor_mul(zsc[:, j:j + 1], rstd2[:, j:j + 1], n8[:, j:j + 1])
            v.drain()
            v.tensor_scalar_mul(ninvs[:, j:j + 1], ninvs[:, j:j + 1], -1.0)
            v.drain()
            v.tensor_scalar_mul(zb[:], xn, zsc[:, j:j + 1])
            v.drain()
            v.tensor_copy(zi[:], zb[:])
            v.drain()
            v.tensor_copy(zk[:], zi[:])
            return v.drain()
        cv_q1 = B.ev("v", v_quant, sem=CV, inc=1)

        def g_quant(g, cv=cv_q1):
            g.wait_ge(CV, cv)
            g.tensor_sub(zd[:], zb[:], zk[:])
            return g.drain()
        cg_q = B.ev("g", g_quant, sem=CG, inc=1)

        def v_quant2(v, j=j, fslot=fslot, cg=cg_q):
            v.wait_ge(CG, cg)
            v.scalar_tensor_tensor(zk[:], zd[:], 0.5, zk[:], ALU.is_ge, ALU.add)
            v.drain()
            v.scalar_tensor_tensor(xqtm[fslot][:], zd[:], -0.5, zk[:],
                                   ALU.is_lt, ALU.subtract)
            return v.drain()
        cv_q_prev = B.ev("v", v_quant2, sem=CV, inc=1)

        def s_tpxq(s, j=j, fslot=fslot, cv=cv_q_prev):
            s.wait_ge(CV, cv)
            return s.dma_start(out=xqfm[:, :, j, :], in_=xqtm[fslot][:],
                               transpose=True)
        relQ_dt[fslot] = B.ev("s", s_tpxq, sem=DQ[fslot], inc=16)
    B.dbg("xqfm", xqfm)
    B.dbg("ninvs", ninvs)
    B.dbg("zsc", zsc)
    cv_q_all = cv_q_prev

    def t_invT(t):
        for m in range(8):
            t.transpose(psB[0:1, m * 128:(m + 1) * 128], ninvs[:, m:m + 1],
                        ident[:])
        return t.drain()
    ct_it = B.ev("t", t_invT, waits=[(CV, max(cv_q_all, cv_evac_prev))],
                 sem=CT, inc=1)

    def v_row(v):
        v.wait_ge(CT, ct_it)
        v.tensor_copy(rowb[:], psB[0:1, 0:1024])
        return v.drain()
    cv_row = B.ev("v", v_row, sem=CV, inc=1)

    def t_bc(t):
        t.wait_ge(CV, cv_row)
        t.matmul(psA[:, 0:512], ones_row[:], rowb[0:1, 0:512], start=True, stop=True)
        t.matmul(psA[:, 512:1024], ones_row[:], rowb[0:1, 512:1024],
                 start=True, stop=True)
        return t.drain()
    ct_bc = B.ev("t", t_bc, sem=CT, inc=1)

    def v_bc(v):
        v.wait_ge(CT, ct_bc)
        v.tensor_copy(nb_full[:], psA[:, 0:1024])
        return v.drain()
    cv_bc = B.ev("v", v_bc, sem=CV, inc=1)
    B.dbg("nb_full", nb_full)

    barrier("pe")

    cs_tanh_prev = 0
    cv_up_evac = 0
    up_evac_par = [0, 0]
    cv_dn_evac = cv_bc
    wgT_rel = [B.val(CT)] * 2
    for qt in range(4):
        for m in range(16):
            mh = qt * 16 + m
            for (wi, dsrc) in ((0, d_wg), (1, d_wu)):
                tslot = (m * 2 + wi) % 2
                dt_w = wstream(
                    dsrc[mh * 128:(mh + 1) * 128, :], 128,
                    (lambda s, slot, tslot=tslot: s.dma_start(
                        out=wgnT[tslot][:], in_=wnat[slot][:], transpose=True)),
                    extra_tp_wait_ct=wgT_rel[tslot], sign=True)

                upps = psA if m % 2 == 0 else psB

                def t_up(t, wi=wi, tslot=tslot, dt=dt_w, upps=upps,
                         cvp=max(up_evac_par[m % 2], cv_dn_evac)):
                    t.wait_ge(dt[0], dt[1])
                    if cvp:
                        t.wait_ge(CV, cvp)
                    off = 0 if wi == 0 else 1024
                    for k in range(16):
                        for nn2 in range(2):
                            t.matmul(upps[:, off + nn2 * 512:off + (nn2 + 1) * 512],
                                     wgnT[tslot][:, k, :],
                                     xqfm[:, k, nn2 * 4:(nn2 + 1) * 4, :],
                                     start=(k == 0), stop=(k == 15))
                    return t.drain()
                wgT_rel[tslot] = B.ev("t", t_up, sem=CT, inc=1)
            ct_up = wgT_rel[(m * 2 + 1) % 2]

            def v_eps1(v, ct=ct_up, upps=upps, csrel=cs_tanh_prev):
                v.wait_ge(CT, ct)
                if csrel:
                    v.wait_ge(CS, csrel)
                v.tensor_mul(ffa[:], upps[:, 0:1024], nb_full[:])
                v.tensor_mul(ffu[:], upps[:, 1024:2048], nb_full[:])
                return v.drain()
            cv_e1 = B.ev("v", v_eps1, waits=[(CV, cv_bc)], sem=CV, inc=1)

            def s_tanh2(s, cv=cv_e1, cvrel=cv_up_evac):
                s.wait_ge(CV, cv)
                if cvrel:
                    s.wait_ge(CV, cvrel)
                return s.activation(fft[:], ffa[:], AF.Tanh, scale=0.5)
            cs_tanh_prev = B.ev("sc", s_tanh2, sem=CS, inc=1)

            def v_eps2(v, m=m, cs=cs_tanh_prev):
                v.wait_ge(CS, cs)
                v.tensor_mul(ffu[:], ffa[:], ffu[:])
                v.drain()
                v.tensor_scalar(fft[:], fft[:], 0.5, 0.5, ALU.mult, ALU.add)
                v.drain()
                v.tensor_mul(pq[:, m, :, :], fft[:], ffu[:])
                return v.drain()
            cv_up_evac = B.ev("v", v_eps2, sem=CV, inc=1)
            up_evac_par[m % 2] = cv_up_evac
        cv_p_done = cv_up_evac

        wdT_rel = [B.val(CT)] * 2
        for n in range(4):
            dslot = n % 2
            dts = []
            for b in range(4):
                dts.append(wstream(
                    d_wd[n * 512 + b * 128:n * 512 + (b + 1) * 128,
                         qt * 2048:(qt + 1) * 2048], 128,
                    (lambda s, slot, dslot=dslot, b=b: s.dma_start(
                        out=wdT[dslot][:, :, b, :], in_=wnat[slot][:],
                        transpose=True)),
                    extra_tp_wait_ct=wdT_rel[dslot], sign=True))

            for mg in range(2):
                def t_dn(t, qt=qt, n=n, dslot=dslot, mg=mg, dt=tuple(dts),
                         cvpd=cv_p_done, cvev=cv_dn_evac):
                    for (dsem, dval) in dt:
                        t.wait_ge(dsem, dval)
                    t.wait_ge(CV, max(cvpd, cvev))
                    for m in range(4 * mg, 4 * mg + 4):
                        for k in range(16):
                            t.matmul(psm(m), pq[:, k, m, :],
                                     wdT[dslot][:, k, :, :],
                                     start=(k == 0), stop=(k == 15))
                    return t.drain()
                ct_dn = B.ev("t", t_dn, sem=CT, inc=1)
                if mg == 1:
                    wdT_rel[dslot] = ct_dn

                def v_evd(v, n=n, mg=mg, ct=ct_dn):
                    v.wait_ge(CT, ct)
                    for m in range(4 * mg, 4 * mg + 4):
                        v.tensor_add(oacc[:, m, n * 512:(n + 1) * 512],
                                     oacc[:, m, n * 512:(n + 1) * 512], psm(m))
                    return v.drain()
                cv_dn_evac = B.ev("v", v_evd, sem=CV, inc=1)

    barrier("fin")

    cs_all_pre_final = B.val(CS)
    xrelF = [(cv_dn_evac, cs_all_pre_final)] * 2
    ost_rel = [0, 0]
    for m in range(NO):
        slot = m % 2
        oslot = m % 2

        def s_ldxf(s, m=m, slot=slot, rel=xrelF[slot]):
            if rel[0]:
                s.wait_ge(CV, rel[0])
            if rel[1]:
                s.wait_ge(CS, rel[1])
            return s.dma_start(out=xsC[slot][:], in_=d_xown[m * 128:(m + 1) * 128, :])
        dx = B.ev("s", s_ldxf, sem=DXA[slot], inc=16)

        def v_fin(v, m=m, slot=slot, oslot=oslot, dx=dx, rel=ost_rel[oslot]):
            v.wait_ge(DXA[slot], dx)
            if rel:
                v.wait_ge(DS[oslot], rel)
            v.tensor_add(ost[oslot][:], oacc[:, m, :], xsC[slot][:])
            return v.drain()
        cv_f = B.ev("v", v_fin, waits=[(CV, cv_dn_evac)], sem=CV, inc=1)
        xrelF[slot] = (cv_f, 0)

        def s_st(s, m=m, oslot=oslot, cv=cv_f):
            s.wait_ge(CV, cv)
            return s.dma_start(out=d_out[m * 128:(m + 1) * 128, :], in_=ost[oslot][:])
        ost_rel[oslot] = B.ev("s", s_st, sem=DS[oslot], inc=16)

    ds_fin = (B.val(DS[0]), B.val(DS[1]))
    if B.dbg_bufs:
        allw = [(CV, B.val(CV)), (CS, B.val(CS)), (CT, B.val(CT)),
                (CG, B.val(CG)), (DS[0], ds_fin[0]), (DS[1], ds_fin[1])]

        def s_dbg(s):
            for (name, buf, d) in B.dbg_bufs:
                s.dma_start(out=d, in_=buf[:]).then_inc(DDB, 16)
            return None
        B.ev("s", s_dbg, waits=allw)
        B.cnt[DDB] = B.cnt.get(DDB, 0) + 16 * len(B.dbg_bufs)

    def s_end(s, v0=ds_fin[0], v1=ds_fin[1], vd=B.val(DDB)):
        s.wait_ge(DS[0], v0)
        s.wait_ge(DS[1], v1)
        if vd:
            s.wait_ge(DDB, vd)
        return None
    B.ev("s", s_end)

    total, remain = B.pack()
    B.pack_info = (total, remain)
    B.emit()
    return B


ZMAP = [[2 * j for j in range(NO)], [2 * j + 1 for j in range(NO)]]


def kernel(L0_attn_hidden_states, L0_attn_Wq, L0_attn_Wk, L0_attn_Wv,
           L0_attn_Wo, L0_attn_Wkc, L0_attn_Wgslc, L0_attn_Wgswa,
           L0_ffn_W_g, L0_ffn_W_u, L0_ffn_W_d, _dbg_names=(), _cores=8):
    x = np.ascontiguousarray(np.asarray(L0_attn_hidden_states, dtype=np.float32))
    B = build(_dbg_names)
    in_maps = []
    for c in range(_cores):
        b, par = c // 2, c % 2
        rows = ZMAP[par]
        xo = np.ascontiguousarray(
            x[b].reshape(NT, 128, DM)[rows].reshape(NO * 128, DM))
        im = {
            "xseq": np.ascontiguousarray(x[b]),
            "xown": xo,
            "pm": np.full((128, 1), float(par), np.float32),
            "pm1": np.full((128, 1), float(1 - par), np.float32),
            "wq": np.ascontiguousarray(np.asarray(L0_attn_Wq, np.float32)),
            "wk": np.ascontiguousarray(np.asarray(L0_attn_Wk, np.float32)),
            "wv": np.ascontiguousarray(np.asarray(L0_attn_Wv, np.float32)),
            "wo": np.ascontiguousarray(np.asarray(L0_attn_Wo, np.float32)),
            "wgs": np.ascontiguousarray(np.asarray(L0_attn_Wgslc, np.float32)),
            "wgw": np.ascontiguousarray(np.asarray(L0_attn_Wgswa, np.float32)),
            "wg": np.ascontiguousarray(np.asarray(L0_ffn_W_g, np.float32)),
            "wu": np.ascontiguousarray(np.asarray(L0_ffn_W_u, np.float32)),
            "wd": np.ascontiguousarray(np.asarray(L0_ffn_W_d, np.float32)),
        }
        in_maps.append(im)
    res = run_bass_kernel_spmd(B.nc, in_maps, core_ids=list(range(_cores)))
    out = np.empty((4, SEQ, DM), res.results[0]["oown"].dtype)
    for c in range(_cores):
        b, par = c // 2, c % 2
        o = res.results[c]["oown"].reshape(NO, 128, DM)
        for j, g in enumerate(ZMAP[par]):
            out[b, g * 128:(g + 1) * 128, :] = o[j]
    if _dbg_names:
        return out, res
    return out


# version 05
# speedup vs baseline: 1.0183x; 1.0183x over previous
import numpy as np
import ml_dtypes
import concourse.bass as bass
import concourse.mybir as mybir
from concourse.bass_utils import run_bass_kernel_spmd

F32 = mybir.dt.float32
BF16 = mybir.dt.bfloat16
I32 = mybir.dt.int32
AF = mybir.ActivationFunctionType
ALU = mybir.AluOpType
AX = mybir.AxisListType

SEQ, DM, NT, NO = 2048, 2048, 16, 8
H, HKV, DH, FH = 16, 4, 128, 8192
NEG = -3.0e38
ATT_SCALE = float(np.float32(1.0) / np.sqrt(np.float32(DH)))

TWO_PI = 2.0 * np.pi
P2_1 = 6.28125
_r = TWO_PI - P2_1
P2_2 = float(np.float32(int(_r * 2 ** 22) / 2 ** 22))
P2_3 = float(np.float32(_r - P2_2))
INV_2PI = float(np.float32(1.0 / TWO_PI))
PIH = np.pi / 2.0
PIH_HI = float(np.float32(int(PIH * 2 ** 20) / 2 ** 20))
PIH_LO = float(np.float32(PIH - PIH_HI))
L64 = float(np.log(10000.0) / 64.0)
L64_HI = float(np.float32(int(L64 * 2 ** 18) / 2 ** 18))
L64_LO = float(np.float32(L64 - L64_HI))


class Builder:
    def __init__(self, dbg_names=()):
        self.nc = bass.Bass("TRN2", target_bir_lowering=False)
        self.events = []
        self.cnt = {}
        self.dbg_names = set(dbg_names)
        self.dbg_bufs = []
        self._ctxs = []
        self._ivbufs = []

    def sb(self, name, shape, dtype, phases=None, **kw):
        if phases is not None and name in self.dbg_names:
            phases = (phases[0], 7)
        if phases is None:
            cm = self.nc.sbuf_tensor(name, shape, dtype, **kw)
            t = cm.__enter__()
            self._ctxs.append(cm)
            return t
        t = self.nc.virtual_sbuf_tensor(name, list(shape), dtype)
        bpp = int(np.prod(shape[1:])) * mybir.dt.size(dtype)
        self._ivbufs.append((t, list(shape), dtype, bpp, phases[0], phases[1]))
        return t

    def pack(self):
        free0 = (self.nc.sbuf_base + 511) & ~511
        cap = self.nc.sbuf_bytes_remaining - (free0 - self.nc.sbuf_base)
        placed = []
        assign = {}
        for (t, shape, dtype, bpp, lo, hi) in sorted(
                self._ivbufs, key=lambda x: -x[3]):
            size = (bpp + 511) & ~511
            cands = sorted({0} | {p[0] + p[1] for p in placed})
            off = None
            for c in cands:
                if c + size > cap:
                    continue
                ok = True
                for (p0, p1, p2, p3) in placed:
                    if not (hi < p2 or lo > p3):
                        if not (c + size <= p0 or c >= p0 + p1):
                            ok = False
                            break
                if ok:
                    off = c
                    break
            assert off is not None, \
                f"pack failed for {t.name} size {size} (cap {cap})"
            placed.append((off, size, lo, hi))
            assign[t.name] = (off, shape, dtype, t)
        total = max(p[0] + p[1] for p in placed)
        for name, (off, shape, dtype, t) in assign.items():
            bk = self.nc.alloc_sbuf_tensor_at(
                name + "_bk", shape, dtype, offset=free0 + off)
            self.nc.back_virtual_tensor_with(t, bk)
        return total, cap

    def ps(self, name, shape):
        cm = self.nc.psum_tensor(name, shape, F32)
        t = cm.__enter__()
        self._ctxs.append(cm)
        return t

    def sem(self, name):
        cm = self.nc.semaphore(name)
        s = cm.__enter__()
        self._ctxs.append(cm)
        return s

    def init_chains(self):
        self.chsem = {}
        self.chcnt = {}

    def mkchain(self, tag):
        self.chsem[tag] = self.sem("CH_" + tag)
        self.chcnt[tag] = 0

    def ch(self, tag, eng, inst):
        self.chcnt[tag] += 1
        inst.then_inc(self.chsem[tag], 1)
        eng.wait_ge(self.chsem[tag], self.chcnt[tag])
        return inst

    def ev(self, tag, fn, waits=(), sem=None, inc=0):
        self.events.append((tag, fn, tuple((s, v) for (s, v) in waits if v > 0),
                            sem, inc))
        if sem is not None and inc:
            self.cnt[sem] = self.cnt.get(sem, 0) + inc
        return self.cnt.get(sem, 0)

    def val(self, sem):
        return self.cnt.get(sem, 0)

    def dbg(self, name, buf):
        if name in self.dbg_names:
            shape = [buf.shape[0], int(np.prod(buf.shape[1:]))]
            d = self.nc.dram_tensor("dbg_" + name, shape, buf.dtype,
                                    kind="ExternalOutput").ap()
            self.dbg_bufs.append((name, buf, d))

    def emit(self):
        P = self
        with self.nc.Block() as block:
            @block.sync
            def _(s):
                run_events(s, P, "s")

            @block.gpsimd
            def _(g):
                run_events(g, P, "g")

            @block.vector
            def _(v):
                run_events(v, P, "v")

            @block.scalar
            def _(sc):
                run_events(sc, P, "sc")

            @block.tensor
            def _(t):
                run_events(t, P, "t")
        for cm in reversed(self._ctxs):
            cm.__exit__(None, None, None)


def run_events(eng, P, tag):
    for (etag, fn, waits, sem, inc) in P.events:
        if etag != tag:
            continue
        for (s, v) in waits:
            eng.wait_ge(s, v)
        inst = fn(eng)
        if sem is not None and inc and inst is not None:
            inst.then_inc(sem, inc)


def build(dbg_names=()):
    B = Builder(dbg_names)
    nc = B.nc

    d_xseq = nc.dram_tensor("xseq", [SEQ, DM], F32, kind="ExternalInput").ap()
    d_xown = nc.dram_tensor("xown", [NO * 128, DM], F32, kind="ExternalInput").ap()
    d_pm = nc.dram_tensor("pm", [128, 1], F32, kind="ExternalInput").ap()
    d_pm1 = nc.dram_tensor("pm1", [128, 1], F32, kind="ExternalInput").ap()
    d_wq = nc.dram_tensor("wq", [DM, DM], F32, kind="ExternalInput").ap()
    d_wk = nc.dram_tensor("wk", [512, DM], F32, kind="ExternalInput").ap()
    d_wv = nc.dram_tensor("wv", [512, DM], F32, kind="ExternalInput").ap()
    d_wo = nc.dram_tensor("wo", [DM, DM], F32, kind="ExternalInput").ap()
    d_wgs = nc.dram_tensor("wgs", [H, DM], F32, kind="ExternalInput").ap()
    d_wgw = nc.dram_tensor("wgw", [H, DM], F32, kind="ExternalInput").ap()
    d_wg = nc.dram_tensor("wg", [FH, DM], F32, kind="ExternalInput").ap()
    d_wu = nc.dram_tensor("wu", [FH, DM], F32, kind="ExternalInput").ap()
    d_wd = nc.dram_tensor("wd", [DM, FH], F32, kind="ExternalInput").ap()
    d_out = nc.dram_tensor("oown", [NO * 128, DM], F32, kind="ExternalOutput").ap()

    DXA = [B.sem("DXA0"), B.sem("DXA1")]
    DW = [B.sem("DW0"), B.sem("DW1")]
    DTN = [B.sem("DTN0"), B.sem("DTN1")]
    DF = [B.sem("DF0"), B.sem("DF1")]
    DQ = [B.sem("DQ0"), B.sem("DQ1")]
    DK = [B.sem("DK0"), B.sem("DK1")]
    DP = [B.sem("DP0"), B.sem("DP1")]
    DS = [B.sem("DS0"), B.sem("DS1")]
    DDB = B.sem("DDB")
    B.init_chains()
    for _tag in ("v", "sc", "g"):
        B.mkchain(_tag)
    CV = B.sem("CV")
    CS = B.sem("CS")
    CT = B.sem("CT")
    CG = B.sem("CG")

    psA = B.ps("psA", [128, 2048])
    psB = B.ps("psB", [128, 2048])

    ALLSEMS = DXA + DW + DTN + DF + DQ + DK + DP + DS + [CV, CS, CT, CG]

    def barrier(name):
        snap = [(sm, B.val(sm)) for sm in ALLSEMS if B.val(sm) > 0]
        for tag in ("s", "g", "v", "sc", "t"):
            B.ev(tag, (lambda e: None), waits=snap)

    ones_row = B.sb("ones_row", [1, 128], F32)
    ident = B.sb("ident", [128, 128], F32)
    mdiag = B.sb("mdiag", [128, 128], F32)
    mpre = B.sb("mpre", [128, 128], F32)
    mlast = B.sb("mlast", [128, 128], F32)
    pmb = B.sb("pmb", [128, 1], F32)
    pm1b = B.sb("pm1b", [128, 1], F32)
    pm1neg = B.sb("pm1neg", [128, 1], F32)
    wrow = B.sb("wrow", [1, 64], F32)
    omega = B.sb("omega", [128, 64], F32)
    tgi = B.sb("tgi", [128, 64], F32)
    rr1 = B.sb("rr1", [128, 64], F32)
    rr2 = B.sb("rr2", [128, 64], F32)
    rri = B.sb("rri", [128, 64], I32)
    rrk = B.sb("rrk", [128, 64], F32)
    zbQ = B.sb("zbQ", [128, 256], F32)
    sums = B.sb("sums", [128, 24], F32)
    sumsq = B.sb("sumsq", [128, 24], F32)
    mean_ = B.sb("mean_", [128, 24], F32)
    var_ = B.sb("var_", [128, 24], F32)
    rstd = B.sb("rstd", [128, 24], F32)
    nrs = B.sb("nrs", [128, 24], F32)
    bias_ = B.sb("bias_", [128, 24], F32)
    var2 = B.sb("var2", [128, 8], F32)
    rstd2 = B.sb("rstd2", [128, 8], F32)
    n8 = B.sb("n8", [128, 8], F32)
    amax = B.sb("amax", [128, 8], F32)
    zsc = B.sb("zsc", [128, 8], F32)
    ninvs = B.sb("ninvs", [128, 8], F32)
    gtm = B.sb("gtm", [128, NO, H], F32)
    gsig = B.sb("gsig", [128, 32], F32)
    lsum = B.sb("lsum", [128, 1], F32)
    mrow = B.sb("mrow", [128, 2], F32)
    lsum2 = B.sb("lsum2", [128, 2], F32)
    mrow2 = B.sb("mrow2", [128, 6], F32)
    wnat = [B.sb(f"wnat{i}", [128, DM], BF16) for i in range(2)]

    costm = B.sb("costm", [128, NT, 64], F32, phases=(0, 2))
    sintm = B.sb("sintm", [128, NT, 64], F32, phases=(0, 2))
    wkT = B.sb("wkT", [128, 16, 4, 128], BF16, phases=(0, 2))
    wvT = B.sb("wvT", [128, 16, 4, 128], BF16, phases=(0, 2))
    wgT = B.sb("wgT", [128, 16, 32], BF16, phases=(0, 2))
    sqs = B.sb("sqs", [128, DM], F32, phases=(0, 0))
    xsA = [B.sb(f"xsA{i}", [128, DM], F32, phases=(0, 2)) for i in range(2)]
    wqTh = B.sb("wqTh", [128, 16, 8, 128], BF16, phases=(0, 1))
    xnbo = B.sb("xnbo", [128, NO, DM], BF16, phases=(1, 1))
    xfm = [B.sb(f"xfm{i}", [128, NT, 128], BF16, phases=(1, 2)) for i in range(2)]
    qtm = [B.sb(f"qtm{i}", [128, 1024], BF16, phases=(1, 1)) for i in range(2)]
    crep = B.sb("crep", [128, 512], F32, phases=(1, 2))
    srep = B.sb("srep", [128, 512], F32, phases=(1, 2))
    zbW = B.sb("zbW", [128, 2048], F32, phases=(1, 2))
    qfm = B.sb("qfm", [128, NO, H, 128], BF16, phases=(1, 3))
    xnb = [B.sb(f"xnb{i}", [128, DM], BF16, phases=(2, 2)) for i in range(2)]
    ktm = [B.sb(f"ktm{i}", [128, 512], BF16, phases=(2, 2)) for i in range(2)]
    kfm = B.sb("kfm", [128, HKV, NT, 128], BF16, phases=(2, 3))
    vtm = B.sb("vtm", [128, NT, 512], BF16, phases=(2, 3))
    pexp = [B.sb(f"pexp{i}", [128, 2048], F32, phases=(3, 3)) for i in range(2)]
    pbf = [B.sb(f"pbf{i}", [128, 2048], BF16, phases=(3, 3)) for i in range(2)]
    PT = [B.sb(f"PT{i}", [128, 16, 4, 128], BF16, phases=(3, 3)) for i in range(2)]
    attn = B.sb("attn", [128, 16, NO, 128], BF16, phases=(3, 4))
    oacc = B.sb("oacc", [128, NO, 2048], F32, phases=(4, 7))
    woT = B.sb("woT", [128, 16, 4, 128], BF16, phases=(4, 4))
    xsB = [B.sb(f"xsB{i}", [128, DM], F32, phases=(5, 5)) for i in range(2)]
    xnf = B.sb("xnf", [128, DM], F32, phases=(5, 5))
    zb = B.sb("zb", [128, DM], F32, phases=(5, 5))
    zi = B.sb("zi", [128, DM], I32, phases=(5, 5))
    zk = B.sb("zk", [128, DM], F32, phases=(5, 5))
    zd = B.sb("zd", [128, DM], F32, phases=(5, 5))
    xqtm = [B.sb(f"xqtm{i}", [128, DM], BF16, phases=(5, 5)) for i in range(2)]
    rowb = B.sb("rowb", [1, 1024], F32, phases=(5, 5))
    xqfm = B.sb("xqfm", [128, 16, NO, 128], BF16, phases=(5, 6))
    nb_full = B.sb("nb_full", [128, 1024], F32, phases=(5, 6))
    pq = B.sb("pq", [128, 16, NO, 128], BF16, phases=(6, 6))
    wgnT = [B.sb(f"wgnT{i}", [128, 16, 128], BF16, phases=(6, 6)) for i in range(2)]
    wdT = [B.sb(f"wdT{i}", [128, 16, 4, 128], BF16, phases=(6, 6)) for i in range(2)]
    ffa = B.sb("ffa", [128, 1024], F32, phases=(6, 6))
    fft = B.sb("fft", [128, 1024], F32, phases=(6, 6))
    ffu = B.sb("ffu", [128, 1024], F32, phases=(6, 6))
    xsC = [B.sb(f"xsC{i}", [128, DM], F32, phases=(7, 7)) for i in range(2)]
    ost = [B.sb(f"ost{i}", [128, DM], F32, phases=(7, 7)) for i in range(2)]

    def g_const(g):
        g.memset(ones_row[:], 1.0)
        g.memset(mdiag[:], 0.0)
        g.memset(ident[:], 1.0)
        g.drain()
        g.affine_select(mdiag[:], mdiag[:], [[-1, 128]], ALU.is_ge, NEG,
                        base=0, channel_multiplier=1)
        g.affine_select(ident[:], ident[:], [[-1, 128]], ALU.is_equal, 0.0,
                        base=0, channel_multiplier=1)
        g.iota(wrow[:], [[1, 64]], base=0, channel_multiplier=0,
               allow_small_or_imprecise_dtypes=True)
        return g.drain()
    cg_const = B.ev("g", g_const, sem=CG, inc=1)

    def s_pm(s):
        s.dma_start(out=pmb[:], in_=d_pm).then_inc(DXA[0], 16)
        s.dma_start(out=pm1b[:], in_=d_pm1).then_inc(DXA[1], 16)
        return None
    B.ev("s", s_pm)
    B.cnt[DXA[0]] = B.cnt.get(DXA[0], 0) + 16
    B.cnt[DXA[1]] = B.cnt.get(DXA[1], 0) + 16

    def v_masks(v):
        v.tensor_scalar_mul(pm1neg[:], pm1b[:], NEG)
        v.tensor_scalar_mul(mpre[:], mdiag[:], pm1b[:, 0:1])
        v.tensor_scalar_mul(mlast[:], mdiag[:], pmb[:, 0:1])
        v.drain()
        v.tensor_scalar_add(mlast[:], mlast[:], pm1neg[:, 0:1])
        v.tensor_scalar_mul(rr1[0:1, :], wrow[:], -L64_HI)
        v.drain()
        v.scalar_tensor_tensor(rr1[0:1, :], wrow[:], -L64_LO, rr1[0:1, :],
                               ALU.mult, ALU.add)
        return v.drain()
    cv_masks = B.ev("v", v_masks,
                    waits=[(CG, cg_const), (DXA[0], 16), (DXA[1], 16)],
                    sem=CV, inc=1)

    def s_omega(s):
        return s.activation(wrow[:], rr1[0:1, :], AF.Exp)
    cs_omega = B.ev("sc", s_omega, waits=[(CV, cv_masks)], sem=CS, inc=1)

    def t_omega(t):
        return t.matmul(psB[:, 0:64], ones_row[:], wrow[:], start=True, stop=True)
    ct_omega = B.ev("t", t_omega, waits=[(CS, cs_omega)], sem=CT, inc=1)

    def v_omega_ev(v):
        return v.tensor_copy(omega[:], psB[:, 0:64])
    cv_omega = B.ev("v", v_omega_ev, waits=[(CT, ct_omega)], sem=CV, inc=1)

    def rr_chain(v, src):
        for op in (
            lambda: v.tensor_scalar(rr2[:], src, INV_2PI, 0.5, ALU.mult, ALU.add),
            lambda: v.tensor_copy(rri[:], rr2[:]),
            lambda: v.tensor_copy(rrk[:], rri[:]),
            lambda: v.tensor_sub(rr2[:], rr2[:], rrk[:]),
            lambda: v.tensor_scalar(rr2[:], rr2[:], 0.0, None, ALU.is_lt),
            lambda: v.tensor_sub(rrk[:], rrk[:], rr2[:]),
            lambda: v.scalar_tensor_tensor(rr2[:], rrk[:], -P2_1, src,
                                           ALU.mult, ALU.add),
            lambda: v.scalar_tensor_tensor(rr2[:], rrk[:], -P2_2, rr2[:],
                                           ALU.mult, ALU.add),
            lambda: v.scalar_tensor_tensor(rr2[:], rrk[:], -P2_3, rr2[:],
                                           ALU.mult, ALU.add),
        ):
            B.ch("v", v, op())

    cs_cos_prev = 0
    for g in range(NT):
        def g_iota(gp, g=g):
            gp.iota(tgi[:], [[0, 64]], base=128 * g, channel_multiplier=1,
                    allow_small_or_imprecise_dtypes=True)
            return gp.drain()
        cg_i = B.ev("g", g_iota, waits=[(CV, B.val(CV) if g > 0 else 0)],
                    sem=CG, inc=1)

        def v_theta(v, g=g, csp=cs_cos_prev):
            if csp:
                v.wait_ge(CS, csp)
            v.tensor_mul(rr1[:], tgi[:], omega[:])
            v.drain()
            rr_chain(v, rr1[:])
            return v.drain()
        cv_sin = B.ev("v", v_theta, waits=[(CG, cg_i), (CV, cv_omega)],
                      sem=CV, inc=1)

        def s_sin(s, g=g):
            return s.activation(sintm[:, g, :], rr2[:], AF.Sin)
        cs_sin = B.ev("sc", s_sin, waits=[(CV, cv_sin)], sem=CS, inc=1)

        def v_theta2(v, g=g):
            v.tensor_scalar_add(rr1[:], rr1[:], PIH_HI)
            v.drain()
            v.tensor_scalar_add(rr1[:], rr1[:], PIH_LO)
            v.drain()
            rr_chain(v, rr1[:])
            return v.drain()
        cv_cos = B.ev("v", v_theta2, waits=[(CS, cs_sin)], sem=CV, inc=1)

        def s_cos(s, g=g):
            return s.activation(costm[:, g, :], rr2[:], AF.Sin)
        cs_cos_prev = B.ev("sc", s_cos, waits=[(CV, cv_cos)], sem=CS, inc=1)

    B.dbg("sintm", sintm)
    B.dbg("costm", costm)

    wnat_rel = [0, 0]

    class WS:
        idx = 0

    def wstream(src, rows, tpfn, extra_tp_wait_ct=0, sign=False):
        slot = WS.idx % 2
        WS.idx += 1

        def g_wload(g, src=src, rows=rows, slot=slot, rel=wnat_rel[slot]):
            if rel:
                g.wait_ge(DTN[slot], rel)
            return g.dma_start(out=wnat[slot][0:rows, :], in_=src)
        dw = B.ev("g", g_wload, sem=DW[slot], inc=16)

        cs_sg = 0
        if sign:
            def s_sign(s, slot=slot, rows=rows, dw=dw):
                s.wait_ge(DW[slot], dw)
                return s.activation(wnat[slot][0:rows, :], wnat[slot][0:rows, :],
                                    AF.Sign)
            cs_sg = B.ev("sc", s_sign, sem=CS, inc=1)

        def s_wtp(s, tpfn=tpfn, slot=slot, dw=dw, cs=cs_sg, xw=extra_tp_wait_ct):
            if cs:
                s.wait_ge(CS, cs)
            else:
                s.wait_ge(DW[slot], dw)
            if xw:
                s.wait_ge(CT, xw)
            return tpfn(s, slot)
        wnat_rel[slot] = B.ev("s", s_wtp, sem=DTN[slot], inc=16)
        return (DTN[slot], wnat_rel[slot])

    for b in range(4):
        wstream(d_wk[b * 128:(b + 1) * 128, :], 128,
                lambda s, slot, b=b: s.dma_start(
                    out=wkT[:, :, b, :], in_=wnat[slot][:], transpose=True))
    for b in range(4):
        wstream(d_wv[b * 128:(b + 1) * 128, :], 128,
                lambda s, slot, b=b: s.dma_start(
                    out=wvT[:, :, b, :], in_=wnat[slot][:], transpose=True))
    wstream(d_wgs, 16, lambda s, slot: s.dma_start(
        out=wgT[:, :, 0:16], in_=wnat[slot][0:16, :], transpose=True))
    wstream(d_wgw, 16, lambda s, slot: s.dma_start(
        out=wgT[:, :, 16:32], in_=wnat[slot][0:16, :], transpose=True))
    for b in range(8):
        wstream(d_wq[b * 128:(b + 1) * 128, :], 128,
                lambda s, slot, b=b: s.dma_start(
                    out=wqTh[:, :, b, :], in_=wnat[slot][:], transpose=True))
    DTN_attw = (B.val(DTN[0]), B.val(DTN[1]))
    B.dbg("wkT", wkT)

    xsrc = [(d_xseq, t, t) for t in range(NT)] + \
           [(d_xown, t, NT + t) for t in range(NO)]
    xrel = [(0, 0)] * 2
    cs_sq_prev = 0
    for idx, (src, t, col) in enumerate(xsrc):
        slot = idx % 2

        def s_ldx(s, src=src, t=t, slot=slot, rel=xrel[slot]):
            if rel[0]:
                s.wait_ge(CV, rel[0])
            if rel[1]:
                s.wait_ge(CS, rel[1])
            return s.dma_start(out=xsA[slot][:], in_=src[t * 128:(t + 1) * 128, :])
        dx = B.ev("s", s_ldx, sem=DXA[slot], inc=16)

        def v_sum(v, slot=slot, col=col):
            return v.reduce_sum(sums[:, col:col + 1], xsA[slot][:], axis=AX.X)
        cv_h = B.ev("v", v_sum, waits=[(DXA[slot], dx)], sem=CV, inc=1)

        def s_sq(s, slot=slot, col=col, relsq=cs_sq_prev):
            if relsq:
                s.wait_ge(CS, relsq)
            return s.activation(sqs[:], xsA[slot][:], AF.Square,
                                accum_out=sumsq[:, col:col + 1])
        cs_h = B.ev("sc", s_sq, waits=[(DXA[slot], dx)], sem=CS, inc=1)
        cs_sq_prev = cs_h
        xrel[slot] = (cv_h, cs_h)

    def nr_rsqrt(v, r, vv, scratch, iters=6):
        v.memset(r, 1.0)
        v.drain()
        for _ in range(iters):
            v.tensor_mul(scratch, r, r)
            v.drain()
            v.tensor_mul(scratch, scratch, vv)
            v.drain()
            v.tensor_scalar(scratch, scratch, -0.5, 1.5, ALU.mult, ALU.add)
            v.drain()
            v.tensor_mul(r, r, scratch)
            v.drain()

    def v_stats(v):
        v.tensor_scalar_mul(mean_[:], sums[:], 1.0 / DM)
        v.tensor_scalar_mul(var_[:], sumsq[:], 1.0 / DM)
        v.drain()
        v.tensor_mul(nrs[:], mean_[:], mean_[:])
        v.drain()
        v.tensor_sub(var_[:], var_[:], nrs[:])
        v.drain()
        v.tensor_scalar_add(var_[:], var_[:], 1e-5)
        v.drain()
        nr_rsqrt(v, rstd[:], var_[:], nrs[:])
        v.tensor_mul(bias_[:], mean_[:], rstd[:])
        v.drain()
        v.tensor_scalar_mul(bias_[:], bias_[:], -1.0)
        v.tensor_scalar_add(var2[:], var_[:, 16:24], -1e-5)
        v.drain()
        v.tensor_mul(n8[:], rstd[:, 16:24], rstd[:, 16:24])
        v.drain()
        v.tensor_mul(var2[:], var2[:], n8[:])
        v.drain()
        v.tensor_scalar_add(var2[:], var2[:], 1e-8)
        v.drain()
        nr_rsqrt(v, rstd2[:], var2[:], n8[:])
        return v.drain()
    cv_stats = B.ev("v", v_stats,
                    waits=[(CV, B.val(CV)), (CS, B.val(CS))], sem=CV, inc=1)
    B.dbg("rstd", rstd)
    B.dbg("rstd2", rstd2)
    B.dbg("bias_", bias_)

    barrier("pc")

    relC_ct = [0, 0]
    relC_dtq = [0, 0]
    cv_rope_prev = 0
    rope_hist = []
    cs_gate_prev = 0
    xrelC = list(xrel)
    cs_xnbo = {}
    ct_last_q_half0 = 0
    for half in range(2):
        if half == 1:
            for b in range(8):
                wstream(d_wq[(8 + b) * 128:(9 + b) * 128, :], 128,
                        (lambda s, slot, b=b: s.dma_start(
                            out=wqTh[:, :, b, :], in_=wnat[slot][:],
                            transpose=True)),
                        extra_tp_wait_ct=ct_last_q_half0)
        dt_wq_half = (B.val(DTN[0]), B.val(DTN[1]))

        for j in range(NO):
            fslot = j % 2
            if True:
                cv_rope_m2 = rope_hist[-2] if len(rope_hist) >= 2 else 0
            if half == 0:
                slot = j % 2

                def s_ldxo(s, j=j, slot=slot, rel=xrelC[slot]):
                    if rel[0]:
                        s.wait_ge(CV, rel[0])
                    if rel[1]:
                        s.wait_ge(CS, rel[1])
                    return s.dma_start(out=xsA[slot][:],
                                       in_=d_xown[j * 128:(j + 1) * 128, :])
                dx = B.ev("s", s_ldxo, sem=DXA[slot], inc=16)

                def s_xn(s, j=j, slot=slot, dx=dx):
                    s.wait_ge(DXA[slot], dx)
                    return s.activation(xnbo[:, j, :], xsA[slot][:], AF.Identity,
                                        bias=bias_[:, 16 + j:17 + j],
                                        scale=rstd[:, 16 + j:17 + j])
                cs_xn = B.ev("sc", s_xn, waits=[(CV, cv_stats)], sem=CS, inc=1)
                cs_xnbo[j] = cs_xn
                xrelC[slot] = (0, cs_xn)

            def s_tpxc(s, j=j, fslot=fslot, csw=None, relct=relC_ct[fslot]):
                s.wait_ge(CS, cs_xnbo[j])
                if relct:
                    s.wait_ge(CT, relct)
                return s.dma_start(out=xfm[fslot][:, 0:16, :], in_=xnbo[:, j, :],
                                   transpose=True)
            dt_x = B.ev("s", s_tpxc, sem=DF[fslot], inc=16)

            def t_q(t, j=j, fslot=fslot, half=half, dt=dt_x,
                    cvprev=cv_rope_m2, csg_prev=cs_gate_prev, dtw=dt_wq_half):
                t.wait_ge(DF[fslot], dt)
                t.wait_ge(DTN[0], dtw[0])
                t.wait_ge(DTN[1], dtw[1])
                if cvprev:
                    t.wait_ge(CV, cvprev)
                if half == 0 and csg_prev:
                    t.wait_ge(CS, csg_prev)
                off = 0 if (j % 2 == 0) else 1024
                mm = None
                for k in range(16):
                    lhsT = xfm[fslot][:, k, :]
                    for nn in range(2):
                        mm = t.matmul(psA[:, off + nn * 512:off + (nn + 1) * 512],
                                      lhsT, wqTh[:, k, nn * 4:(nn + 1) * 4, :],
                                      start=(k == 0), stop=(k == 15))
                    if half == 0:
                        mm = t.matmul(psB[:, 1792:1824], lhsT, wgT[:, k, :],
                                      start=(k == 0), stop=(k == 15))
                return mm
            ct_q = B.ev("t", t_q, sem=CT, inc=1)
            relC_ct[fslot] = ct_q

            if half == 0:
                def s_gate(s, j=j, ct=ct_q):
                    s.wait_ge(CT, ct)
                    return s.activation(gsig[:], psB[:, 1792:1824], AF.Tanh,
                                        scale=0.5)
                cs_gate = B.ev("sc", s_gate, sem=CS, inc=1)
                cs_gate_prev = cs_gate
            else:
                cs_gate = 0

            def v_rope_q(v, j=j, fslot=fslot, half=half, ct=ct_q, csg=cs_gate,
                         reldq=relC_dtq[fslot]):
                if reldq:
                    v.wait_ge(DQ[fslot], reldq)
                v.wait_ge(CT, ct)
                off = 0 if (j % 2 == 0) else 1024
                v.tensor_scalar_mul(crep[:, 0:64], costm[:, 2 * j, :], pm1b[:, 0:1])
                i1 = v.tensor_scalar_mul(srep[:, 0:64], sintm[:, 2 * j, :],
                                         pm1b[:, 0:1])
                B.ch("v", v, i1)
                v.scalar_tensor_tensor(crep[:, 0:64], costm[:, 2 * j + 1, :],
                                       pmb[:, 0:1], crep[:, 0:64], ALU.mult, ALU.add)
                i2 = v.scalar_tensor_tensor(srep[:, 0:64], sintm[:, 2 * j + 1, :],
                                            pmb[:, 0:1], srep[:, 0:64],
                                            ALU.mult, ALU.add)
                B.ch("v", v, i2)
                for w in (64, 128, 256):
                    v.tensor_copy(crep[:, w:2 * w], crep[:, 0:w])
                    i3 = v.tensor_copy(srep[:, w:2 * w], srep[:, 0:w])
                    B.ch("v", v, i3)
                ps3 = psA[:, off:off + 1024].rearrange("p (h t) -> p h t", t=128)
                x1 = ps3[:, :, 0:64]
                x2 = ps3[:, :, 64:128]
                c3 = crep[:].rearrange("p (h t) -> p h t", t=64)
                s3 = srep[:].rearrange("p (h t) -> p h t", t=64)
                o3 = qtm[fslot][:].rearrange("p (h t) -> p h t", t=128)
                v.tensor_mul(zbW[:, 0:512], x1, c3)
                v.tensor_mul(zbW[:, 512:1024], x2, s3)
                v.tensor_mul(zbW[:, 1024:1536], x2, c3)
                i4 = v.tensor_mul(zbW[:, 1536:2048], x1, s3)
                B.ch("v", v, i4)
                v.tensor_sub(o3[:, :, 0:64],
                             zbW[:, 0:512].rearrange("p (h t) -> p h t", t=64),
                             zbW[:, 512:1024].rearrange("p (h t) -> p h t", t=64))
                out = v.tensor_add(
                    o3[:, :, 64:128],
                    zbW[:, 1024:1536].rearrange("p (h t) -> p h t", t=64),
                    zbW[:, 1536:2048].rearrange("p (h t) -> p h t", t=64))
                if half == 0:
                    v.wait_ge(CS, csg)
                    i5 = v.tensor_add(gsig[:, 0:16], gsig[:, 0:16], gsig[:, 16:32])
                    B.ch("v", v, i5)
                    out = v.tensor_scalar(gtm[:, j, :], gsig[:, 0:16], 0.5, 1.0,
                                          ALU.mult, ALU.add)
                return out
            cv_rope_prev = B.ev("v", v_rope_q, sem=CV, inc=1)
            rope_hist.append(cv_rope_prev)

            def s_tpq(s, j=j, fslot=fslot, half=half, cv=cv_rope_prev):
                s.wait_ge(CV, cv)
                return s.dma_start(out=qfm[:, j, 8 * half:8 * half + 8, :],
                                   in_=qtm[fslot][:], transpose=True)
            relC_dtq[fslot] = B.ev("s", s_tpq, sem=DQ[fslot], inc=16)
        if half == 0:
            ct_last_q_half0 = B.val(CT)

    B.dbg("qfm", qfm)
    B.dbg("gtm", gtm)

    barrier("pb")

    xrelB = list(xrelC)
    cv_kv_par = [0, 0]
    relB_dtx = [0, 0]
    relB_ctx = list(relC_ct)
    relB_dtk = [0, 0]
    for t_i in range(NT):
        slot = t_i % 2
        fslot = t_i % 2

        def s_ldxs(s, t_i=t_i, slot=slot, rel=xrelB[slot]):
            if rel[0]:
                s.wait_ge(CV, rel[0])
            if rel[1]:
                s.wait_ge(CS, rel[1])
            return s.dma_start(out=xsA[slot][:],
                               in_=d_xseq[t_i * 128:(t_i + 1) * 128, :])
        dx = B.ev("s", s_ldxs, sem=DXA[slot], inc=16)

        def s_xn2(s, t_i=t_i, slot=slot, fslot=fslot, dx=dx, rel=relB_dtx[fslot]):
            s.wait_ge(DXA[slot], dx)
            if rel:
                s.wait_ge(DF[fslot], rel)
            return s.activation(xnb[fslot][:], xsA[slot][:], AF.Identity,
                                bias=bias_[:, t_i:t_i + 1],
                                scale=rstd[:, t_i:t_i + 1])
        cs_xn = B.ev("sc", s_xn2, waits=[(CV, cv_stats)], sem=CS, inc=1)
        xrelB[slot] = (0, cs_xn)

        def s_tpx2(s, fslot=fslot, cs=cs_xn, relct=relB_ctx[fslot]):
            s.wait_ge(CS, cs)
            if relct:
                s.wait_ge(CT, relct)
            return s.dma_start(out=xfm[fslot][:], in_=xnb[fslot][:], transpose=True)
        dt_x = B.ev("s", s_tpx2, sem=DF[fslot], inc=16)
        relB_dtx[fslot] = dt_x

        def t_kv(t, t_i=t_i, fslot=fslot, dt=dt_x, cvprev=cv_kv_par[t_i % 2]):
            t.wait_ge(DF[fslot], dt)
            t.wait_ge(DTN[0], DTN_attw[0])
            t.wait_ge(DTN[1], DTN_attw[1])
            if cvprev:
                t.wait_ge(CV, cvprev)
            off = 0 if (t_i % 2 == 0) else 1024
            mm = None
            for k in range(16):
                lhsT = xfm[fslot][:, k, :]
                t.matmul(psB[:, off:off + 512], lhsT, wkT[:, k, :, :],
                         start=(k == 0), stop=(k == 15))
                mm = t.matmul(psB[:, off + 512:off + 1024], lhsT, wvT[:, k, :, :],
                              start=(k == 0), stop=(k == 15))
            return mm
        ct_kv = B.ev("t", t_kv, sem=CT, inc=1)
        relB_ctx[fslot] = ct_kv

        def v_kv(v, t_i=t_i, fslot=fslot, ct=ct_kv, rel=relB_dtk[fslot]):
            v.wait_ge(CT, ct)
            if rel:
                v.wait_ge(DK[fslot], rel)
            off = 0 if (t_i % 2 == 0) else 1024
            v.tensor_copy(crep[:, 0:64], costm[:, t_i, :])
            i1 = v.tensor_copy(srep[:, 0:64], sintm[:, t_i, :])
            B.ch("v", v, i1)
            for w in (64, 128):
                v.tensor_copy(crep[:, w:2 * w], crep[:, 0:w])
                i2 = v.tensor_copy(srep[:, w:2 * w], srep[:, 0:w])
                B.ch("v", v, i2)
            ps3 = psB[:, off:off + 512].rearrange("p (h t) -> p h t", t=128)
            x1 = ps3[:, :, 0:64]
            x2 = ps3[:, :, 64:128]
            c3 = crep[:, 0:256].rearrange("p (h t) -> p h t", t=64)
            s3 = srep[:, 0:256].rearrange("p (h t) -> p h t", t=64)
            o3 = ktm[fslot][:].rearrange("p (h t) -> p h t", t=128)
            v.tensor_mul(zbW[:, 0:256], x1, c3)
            v.tensor_mul(zbW[:, 512:768], x2, s3)
            v.tensor_mul(zbW[:, 1024:1280], x2, c3)
            i3 = v.tensor_mul(zbW[:, 1536:1792], x1, s3)
            B.ch("v", v, i3)
            v.tensor_sub(o3[:, :, 0:64],
                         zbW[:, 0:256].rearrange("p (h t) -> p h t", t=64),
                         zbW[:, 512:768].rearrange("p (h t) -> p h t", t=64))
            v.tensor_add(o3[:, :, 64:128],
                         zbW[:, 1024:1280].rearrange("p (h t) -> p h t", t=64),
                         zbW[:, 1536:1792].rearrange("p (h t) -> p h t", t=64))
            return v.tensor_copy(vtm[:, t_i, :], psB[:, off + 512:off + 1024])
        cv_kv = B.ev("v", v_kv, sem=CV, inc=1)
        cv_kv_par[t_i % 2] = cv_kv

        def s_tpk(s, t_i=t_i, fslot=fslot, cv=cv_kv):
            s.wait_ge(CV, cv)
            return s.dma_start(out=kfm[:, :, t_i, :], in_=ktm[fslot][:],
                               transpose=True)
        relB_dtk[fslot] = B.ev("s", s_tpk, sem=DK[fslot], inc=16)

    B.dbg("kfm", kfm)
    B.dbg("vtm", vtm)
    DKQ_all = [(DK[0], B.val(DK[0])), (DK[1], B.val(DK[1])),
               (DQ[0], B.val(DQ[0])), (DQ[1], B.val(DQ[1]))]
    CV_kv_all = B.val(CV)

    barrier("pd")

    cs_exp_prev = 0
    cs_exp_byreg = [0, 0]
    cv_scale_prev = 0
    cv_sc_slot = [0, 0]
    dt_p_slot = [0, 0]
    pt_rel = [0, 0]
    dt_p = {}
    cv_pv_prev = 0
    for j in range(NO):
        nch = 2 * j + 2
        keys = 128 * nch
        for h in range(H):
            g = h // HKV
            hh = h % HKV
            pslot = h % 2
            tslot = (j * 4 + g) % 2
            reg = 1 if (h % 2 == 1 and j <= 5) else 0
            Sps = (psA[:, 0:keys] if reg == 0 else psB[:, 512:512 + keys])

            def t_S(t, j=j, h=h, g=g, nch=nch, Sps=Sps,
                    cse=cs_exp_byreg[reg]):
                if cse:
                    t.wait_ge(CS, cse)
                lhsT = qfm[:, j, h, :]
                mm = None
                for c0 in range(0, nch, 4):
                    cw = min(4, nch - c0)
                    mm = t.matmul(Sps[:, c0 * 128:(c0 + cw) * 128], lhsT,
                                  kfm[:, g, c0:c0 + cw, :], start=True, stop=True)
                return mm
            ct_S = B.ev("t", t_S, waits=DKQ_all + [(CV, CV_kv_all)],
                        sem=CT, inc=1)

            def v_msk(v, j=j, nch=nch, keys=keys, Sps=Sps, ct=ct_S, pc=pslot):
                v.wait_ge(CT, ct)
                v.tensor_add(Sps[:, (nch - 2) * 128:(nch - 1) * 128],
                             Sps[:, (nch - 2) * 128:(nch - 1) * 128], mpre[:])
                i1 = v.tensor_add(Sps[:, (nch - 1) * 128:nch * 128],
                                  Sps[:, (nch - 1) * 128:nch * 128], mlast[:])
                B.ch("v", v, i1)
                i2 = v.reduce_max(mrow2[:, pc:pc + 1], Sps[:, 0:keys], axis=AX.X)
                B.ch("v", v, i2)
                return v.tensor_scalar_mul(mrow2[:, 2 + pc:3 + pc],
                                           mrow2[:, pc:pc + 1], -ATT_SCALE)
            cv_m = B.ev("v", v_msk, sem=CV, inc=1)

            def s_exp(s, keys=keys, pslot=pslot, cv=cv_m, Sps=Sps,
                      rel=cv_sc_slot[pslot]):
                s.wait_ge(CV, cv)
                if rel:
                    s.wait_ge(CV, rel)
                return s.activation(pexp[pslot][:, 0:keys], Sps[:, 0:keys],
                                    AF.Exp, bias=mrow2[:, 2 + pslot:3 + pslot],
                                    scale=ATT_SCALE,
                                    accum_out=lsum2[:, pslot:pslot + 1])
            cs_exp_prev = B.ev("sc", s_exp, sem=CS, inc=1)
            cs_exp_byreg[reg] = cs_exp_prev

            def v_scale(v, j=j, h=h, keys=keys, pslot=pslot, cs=cs_exp_prev,
                        reldt=dt_p_slot[pslot]):
                v.wait_ge(CS, cs)
                if reldt:
                    v.wait_ge(DP[pslot], reldt)
                i1 = v.reciprocal(mrow2[:, 4 + pslot:5 + pslot],
                                  lsum2[:, pslot:pslot + 1])
                B.ch("v", v, i1)
                i2 = v.tensor_mul(mrow2[:, 4 + pslot:5 + pslot],
                                  mrow2[:, 4 + pslot:5 + pslot],
                                  gtm[:, j, h:h + 1])
                B.ch("v", v, i2)
                return v.tensor_scalar_mul(pbf[pslot][:, 0:keys],
                                           pexp[pslot][:, 0:keys],
                                           mrow2[:, 4 + pslot:5 + pslot])
            cv_sc = B.ev("v", v_scale, sem=CV, inc=1)
            cv_scale_prev = cv_sc
            cv_sc_slot[pslot] = cv_sc

            def s_tpp(s, nch=nch, keys=keys, hh=hh, pslot=pslot, tslot=tslot,
                      cv=cv_sc, relct=pt_rel[tslot]):
                s.wait_ge(CV, cv)
                if relct:
                    s.wait_ge(CT, relct)
                return s.dma_start(out=PT[tslot][:, 0:nch, hh, :],
                                   in_=pbf[pslot][:, 0:keys], transpose=True)
            dt_p[(j, h)] = B.ev("s", s_tpp, sem=DP[pslot], inc=16)
            dt_p_slot[pslot] = dt_p[(j, h)]

            if hh == HKV - 1:
                def t_PV(t, j=j, g=g, nch=nch, tslot=tslot,
                         dtp0=dt_p[(j, h - 1)], dtp1=dt_p[(j, h)],
                         cvp=cv_pv_prev):
                    t.wait_ge(DP[0], dtp0)
                    t.wait_ge(DP[1], dtp1)
                    if cvp:
                        t.wait_ge(CV, cvp)
                    mm = None
                    for c in range(nch):
                        mm = t.matmul(psB[:, 0:512],
                                      vtm[:, c, g * 128:(g + 1) * 128],
                                      PT[tslot][:, c, :, :],
                                      start=(c == 0), stop=(c == nch - 1))
                    return mm
                ct_pv = B.ev("t", t_PV, waits=[(CV, CV_kv_all)], sem=CT, inc=1)
                pt_rel[tslot] = ct_pv

                def v_att(v, j=j, g=g, ct=ct_pv):
                    v.wait_ge(CT, ct)
                    return v.tensor_copy(attn[:, 4 * g:4 * g + 4, j, :],
                                         psB[:, 0:512])
                cv_pv_prev = B.ev("v", v_att, sem=CV, inc=1)

    B.dbg("attn", attn)
    CV_attn_done = B.val(CV)

    barrier("pe0")

    def psm(m):
        src = psA if m < 4 else psB
        return src[:, (m % 4) * 512:(m % 4 + 1) * 512]

    cv_evac_prev = 0
    ct_o = 0
    for n in range(4):
        dts = []
        for b in range(4):
            dts.append(wstream(
                d_wo[n * 512 + b * 128:n * 512 + (b + 1) * 128, :], 128,
                (lambda s, slot, b=b: s.dma_start(out=woT[:, :, b, :],
                                                  in_=wnat[slot][:],
                                                  transpose=True)),
                extra_tp_wait_ct=ct_o))

        for mg in range(2):
            def t_wo(t, n=n, mg=mg, dt=tuple(dts), cvp=cv_evac_prev):
                for (dsem, dval) in dt:
                    t.wait_ge(dsem, dval)
                if cvp:
                    t.wait_ge(CV, cvp)
                mm = None
                for m in range(4 * mg, 4 * mg + 4):
                    for k in range(16):
                        mm = t.matmul(psm(m), attn[:, k, m, :], woT[:, k, :, :],
                                      start=(k == 0), stop=(k == 15))
                return mm
            ct_o = B.ev("t", t_wo, waits=[(CV, CV_attn_done), (CS, B.val(CS))],
                        sem=CT, inc=1)

            def v_evo(v, n=n, mg=mg, ct=ct_o):
                v.wait_ge(CT, ct)
                mm = None
                for m in range(4 * mg, 4 * mg + 4):
                    mm = v.tensor_copy(oacc[:, m, n * 512:(n + 1) * 512], psm(m))
                return mm
            cv_evac_prev = B.ev("v", v_evo, sem=CV, inc=1)

    barrier("p45")

    xrelQ = [(cv_evac_prev, B.val(CS))] * 2
    relQ_dt = [B.val(DQ[0]), B.val(DQ[1])]
    cv_q_prev = 0
    for j in range(NO):
        slot = j % 2
        fslot = j % 2

        def s_ldxq(s, j=j, slot=slot, rel=xrelQ[slot]):
            if rel[0]:
                s.wait_ge(CV, rel[0])
            if rel[1]:
                s.wait_ge(CS, rel[1])
            return s.dma_start(out=xsB[slot][:], in_=d_xown[j * 128:(j + 1) * 128, :])
        dx = B.ev("s", s_ldxq, sem=DXA[slot], inc=16)

        def s_xn3(s, j=j, slot=slot, dx=dx, rel=cv_q_prev):
            s.wait_ge(DXA[slot], dx)
            if rel:
                s.wait_ge(CV, rel)
            return s.activation(xnf[:], xsB[slot][:], AF.Identity,
                                bias=bias_[:, 16 + j:17 + j],
                                scale=rstd[:, 16 + j:17 + j])
        cs_xn = B.ev("sc", s_xn3, sem=CS, inc=1)
        xrelQ[slot] = (0, cs_xn)

        def v_quant(v, j=j, fslot=fslot, cs=cs_xn, reldt=relQ_dt[fslot]):
            v.wait_ge(CS, cs)
            if reldt:
                v.wait_ge(DQ[fslot], reldt)
            xn = xnf[:]
            i1 = v.reduce_max(amax[:, j:j + 1], xn, axis=AX.X,
                              apply_absolute_value=True)
            B.ch("v", v, i1)
            i2 = v.tensor_mul(amax[:, j:j + 1], amax[:, j:j + 1],
                              rstd2[:, j:j + 1])
            B.ch("v", v, i2)
            i3 = v.reciprocal(n8[:, j:j + 1], amax[:, j:j + 1])
            B.ch("v", v, i3)
            i4 = v.tensor_scalar_mul(n8[:, j:j + 1], n8[:, j:j + 1], 127.0)
            B.ch("v", v, i4)
            v.reciprocal(ninvs[:, j:j + 1], n8[:, j:j + 1])
            i5 = v.tensor_mul(zsc[:, j:j + 1], rstd2[:, j:j + 1], n8[:, j:j + 1])
            B.ch("v", v, i5)
            v.tensor_scalar_mul(ninvs[:, j:j + 1], ninvs[:, j:j + 1], -1.0)
            i6 = v.tensor_scalar_mul(zb[:], xn, zsc[:, j:j + 1])
            B.ch("v", v, i6)
            i7 = v.tensor_copy(zi[:], zb[:])
            B.ch("v", v, i7)
            return v.tensor_copy(zk[:], zi[:])
        cv_q1 = B.ev("v", v_quant, sem=CV, inc=1)

        def g_quant(g, cv=cv_q1):
            g.wait_ge(CV, cv)
            return g.tensor_sub(zd[:], zb[:], zk[:])
        cg_q = B.ev("g", g_quant, sem=CG, inc=1)

        def v_quant2(v, j=j, fslot=fslot, cg=cg_q):
            v.wait_ge(CG, cg)
            i1 = v.scalar_tensor_tensor(zk[:], zd[:], 0.5, zk[:],
                                        ALU.is_ge, ALU.add)
            B.ch("v", v, i1)
            return v.scalar_tensor_tensor(xqtm[fslot][:], zd[:], -0.5, zk[:],
                                          ALU.is_lt, ALU.subtract)
        cv_q_prev = B.ev("v", v_quant2, sem=CV, inc=1)

        def s_tpxq(s, j=j, fslot=fslot, cv=cv_q_prev):
            s.wait_ge(CV, cv)
            return s.dma_start(out=xqfm[:, :, j, :], in_=xqtm[fslot][:],
                               transpose=True)
        relQ_dt[fslot] = B.ev("s", s_tpxq, sem=DQ[fslot], inc=16)
    B.dbg("xqfm", xqfm)
    B.dbg("ninvs", ninvs)
    B.dbg("zsc", zsc)
    cv_q_all = cv_q_prev

    def t_invT(t):
        mm = None
        for m in range(8):
            mm = t.transpose(psB[0:1, m * 128:(m + 1) * 128], ninvs[:, m:m + 1],
                             ident[:])
        return mm
    ct_it = B.ev("t", t_invT, waits=[(CV, max(cv_q_all, cv_evac_prev))],
                 sem=CT, inc=1)

    def v_row(v):
        v.wait_ge(CT, ct_it)
        return v.tensor_copy(rowb[:], psB[0:1, 0:1024])
    cv_row = B.ev("v", v_row, sem=CV, inc=1)

    def t_bc(t):
        t.wait_ge(CV, cv_row)
        t.matmul(psA[:, 0:512], ones_row[:], rowb[0:1, 0:512], start=True, stop=True)
        return t.matmul(psA[:, 512:1024], ones_row[:], rowb[0:1, 512:1024],
                        start=True, stop=True)
    ct_bc = B.ev("t", t_bc, sem=CT, inc=1)

    def v_bc(v):
        v.wait_ge(CT, ct_bc)
        return v.tensor_copy(nb_full[:], psA[:, 0:1024])
    cv_bc = B.ev("v", v_bc, sem=CV, inc=1)
    B.dbg("nb_full", nb_full)

    barrier("pe")

    cs_tanh_prev = 0
    cv_up_evac = 0
    up_evac_par = [0, 0]
    cv_dn_evac = cv_bc
    wgT_rel = [B.val(CT)] * 2
    for qt in range(4):
        for m in range(16):
            mh = qt * 16 + m
            for (wi, dsrc) in ((0, d_wg), (1, d_wu)):
                tslot = (m * 2 + wi) % 2
                dt_w = wstream(
                    dsrc[mh * 128:(mh + 1) * 128, :], 128,
                    (lambda s, slot, tslot=tslot: s.dma_start(
                        out=wgnT[tslot][:], in_=wnat[slot][:], transpose=True)),
                    extra_tp_wait_ct=wgT_rel[tslot], sign=True)

                upps = psA if m % 2 == 0 else psB

                def t_up(t, wi=wi, tslot=tslot, dt=dt_w, upps=upps,
                         cvp=max(up_evac_par[m % 2], cv_dn_evac)):
                    t.wait_ge(dt[0], dt[1])
                    if cvp:
                        t.wait_ge(CV, cvp)
                    off = 0 if wi == 0 else 1024
                    mm = None
                    for k in range(16):
                        for nn2 in range(2):
                            mm = t.matmul(
                                upps[:, off + nn2 * 512:off + (nn2 + 1) * 512],
                                wgnT[tslot][:, k, :],
                                xqfm[:, k, nn2 * 4:(nn2 + 1) * 4, :],
                                start=(k == 0), stop=(k == 15))
                    return mm
                wgT_rel[tslot] = B.ev("t", t_up, sem=CT, inc=1)
            ct_up = wgT_rel[(m * 2 + 1) % 2]

            def v_eps1(v, ct=ct_up, upps=upps, csrel=cs_tanh_prev,
                       cvrel=cv_up_evac):
                v.wait_ge(CT, ct)
                if csrel:
                    v.wait_ge(CS, csrel)
                if cvrel:
                    v.wait_ge(CV, cvrel)
                v.tensor_mul(ffa[:], upps[:, 0:1024], nb_full[:])
                return v.tensor_mul(ffu[:], upps[:, 1024:2048], nb_full[:])
            cv_e1 = B.ev("v", v_eps1, waits=[(CV, cv_bc)], sem=CV, inc=1)

            def s_tanh2(s, cv=cv_e1, cvrel=cv_up_evac):
                s.wait_ge(CV, cv)
                if cvrel:
                    s.wait_ge(CV, cvrel)
                return s.activation(fft[:], ffa[:], AF.Tanh, scale=0.5)
            cs_tanh_prev = B.ev("sc", s_tanh2, sem=CS, inc=1)

            def v_eps2(v, m=m, cs=cs_tanh_prev):
                v.wait_ge(CS, cs)
                v.tensor_mul(ffu[:], ffa[:], ffu[:])
                i1 = v.tensor_scalar(fft[:], fft[:], 0.5, 0.5, ALU.mult, ALU.add)
                B.ch("v", v, i1)
                return v.tensor_mul(pq[:, m, :, :], fft[:], ffu[:])
            cv_up_evac = B.ev("v", v_eps2, sem=CV, inc=1)
            up_evac_par[m % 2] = cv_up_evac
        cv_p_done = cv_up_evac

        wdT_rel = [B.val(CT)] * 2
        for n in range(4):
            dslot = n % 2
            dts = []
            for b in range(4):
                dts.append(wstream(
                    d_wd[n * 512 + b * 128:n * 512 + (b + 1) * 128,
                         qt * 2048:(qt + 1) * 2048], 128,
                    (lambda s, slot, dslot=dslot, b=b: s.dma_start(
                        out=wdT[dslot][:, :, b, :], in_=wnat[slot][:],
                        transpose=True)),
                    extra_tp_wait_ct=wdT_rel[dslot], sign=True))

            for mg in range(2):
                def t_dn(t, qt=qt, n=n, dslot=dslot, mg=mg, dt=tuple(dts),
                         cvpd=cv_p_done, cvev=cv_dn_evac):
                    for (dsem, dval) in dt:
                        t.wait_ge(dsem, dval)
                    t.wait_ge(CV, max(cvpd, cvev))
                    mm = None
                    for m in range(4 * mg, 4 * mg + 4):
                        for k in range(16):
                            mm = t.matmul(psm(m), pq[:, k, m, :],
                                          wdT[dslot][:, k, :, :],
                                          start=(k == 0), stop=(k == 15))
                    return mm
                ct_dn = B.ev("t", t_dn, sem=CT, inc=1)
                if mg == 1:
                    wdT_rel[dslot] = ct_dn

                def v_evd(v, n=n, mg=mg, ct=ct_dn):
                    v.wait_ge(CT, ct)
                    mm = None
                    for m in range(4 * mg, 4 * mg + 4):
                        mm = v.tensor_add(oacc[:, m, n * 512:(n + 1) * 512],
                                          oacc[:, m, n * 512:(n + 1) * 512],
                                          psm(m))
                    return mm
                cv_dn_evac = B.ev("v", v_evd, sem=CV, inc=1)

    barrier("fin")

    cs_all_pre_final = B.val(CS)
    xrelF = [(cv_dn_evac, cs_all_pre_final)] * 2
    ost_rel = [0, 0]
    for m in range(NO):
        slot = m % 2
        oslot = m % 2

        def s_ldxf(s, m=m, slot=slot, rel=xrelF[slot]):
            if rel[0]:
                s.wait_ge(CV, rel[0])
            if rel[1]:
                s.wait_ge(CS, rel[1])
            return s.dma_start(out=xsC[slot][:], in_=d_xown[m * 128:(m + 1) * 128, :])
        dx = B.ev("s", s_ldxf, sem=DXA[slot], inc=16)

        def v_fin(v, m=m, slot=slot, oslot=oslot, dx=dx, rel=ost_rel[oslot]):
            v.wait_ge(DXA[slot], dx)
            if rel:
                v.wait_ge(DS[oslot], rel)
            return v.tensor_add(ost[oslot][:], oacc[:, m, :], xsC[slot][:])
        cv_f = B.ev("v", v_fin, waits=[(CV, cv_dn_evac)], sem=CV, inc=1)
        xrelF[slot] = (cv_f, 0)

        def s_st(s, m=m, oslot=oslot, cv=cv_f):
            s.wait_ge(CV, cv)
            return s.dma_start(out=d_out[m * 128:(m + 1) * 128, :], in_=ost[oslot][:])
        ost_rel[oslot] = B.ev("s", s_st, sem=DS[oslot], inc=16)

    ds_fin = (B.val(DS[0]), B.val(DS[1]))
    if B.dbg_bufs:
        allw = [(CV, B.val(CV)), (CS, B.val(CS)), (CT, B.val(CT)),
                (CG, B.val(CG)), (DS[0], ds_fin[0]), (DS[1], ds_fin[1])]

        def s_dbg(s):
            for (name, buf, d) in B.dbg_bufs:
                s.dma_start(out=d, in_=buf[:]).then_inc(DDB, 16)
            return None
        B.ev("s", s_dbg, waits=allw)
        B.cnt[DDB] = B.cnt.get(DDB, 0) + 16 * len(B.dbg_bufs)

    def s_end(s, v0=ds_fin[0], v1=ds_fin[1], vd=B.val(DDB)):
        s.wait_ge(DS[0], v0)
        s.wait_ge(DS[1], v1)
        if vd:
            s.wait_ge(DDB, vd)
        return None
    B.ev("s", s_end)

    total, remain = B.pack()
    B.pack_info = (total, remain)
    B.emit()
    return B


ZMAP = [[2 * j for j in range(NO)], [2 * j + 1 for j in range(NO)]]


def kernel(L0_attn_hidden_states, L0_attn_Wq, L0_attn_Wk, L0_attn_Wv,
           L0_attn_Wo, L0_attn_Wkc, L0_attn_Wgslc, L0_attn_Wgswa,
           L0_ffn_W_g, L0_ffn_W_u, L0_ffn_W_d, _dbg_names=(), _cores=8):
    x = np.ascontiguousarray(np.asarray(L0_attn_hidden_states, dtype=np.float32))
    B = build(_dbg_names)
    in_maps = []
    for c in range(_cores):
        b, par = c // 2, c % 2
        rows = ZMAP[par]
        xo = np.ascontiguousarray(
            x[b].reshape(NT, 128, DM)[rows].reshape(NO * 128, DM))
        im = {
            "xseq": np.ascontiguousarray(x[b]),
            "xown": xo,
            "pm": np.full((128, 1), float(par), np.float32),
            "pm1": np.full((128, 1), float(1 - par), np.float32),
            "wq": np.ascontiguousarray(np.asarray(L0_attn_Wq, np.float32)),
            "wk": np.ascontiguousarray(np.asarray(L0_attn_Wk, np.float32)),
            "wv": np.ascontiguousarray(np.asarray(L0_attn_Wv, np.float32)),
            "wo": np.ascontiguousarray(np.asarray(L0_attn_Wo, np.float32)),
            "wgs": np.ascontiguousarray(np.asarray(L0_attn_Wgslc, np.float32)),
            "wgw": np.ascontiguousarray(np.asarray(L0_attn_Wgswa, np.float32)),
            "wg": np.ascontiguousarray(np.asarray(L0_ffn_W_g, np.float32)),
            "wu": np.ascontiguousarray(np.asarray(L0_ffn_W_u, np.float32)),
            "wd": np.ascontiguousarray(np.asarray(L0_ffn_W_d, np.float32)),
        }
        in_maps.append(im)
    res = run_bass_kernel_spmd(B.nc, in_maps, core_ids=list(range(_cores)))
    out = np.empty((4, SEQ, DM), res.results[0]["oown"].dtype)
    for c in range(_cores):
        b, par = c // 2, c % 2
        o = res.results[c]["oown"].reshape(NO, 128, DM)
        for j, g in enumerate(ZMAP[par]):
            out[b, g * 128:(g + 1) * 128, :] = o[j]
    if _dbg_names:
        return out, res
    return out


# version 06
# speedup vs baseline: 1.0654x; 1.0462x over previous
import numpy as np
import ml_dtypes
import concourse.bass as bass
import concourse.mybir as mybir
from concourse.bass_utils import run_bass_kernel_spmd

F32 = mybir.dt.float32
BF16 = mybir.dt.bfloat16
I32 = mybir.dt.int32
AF = mybir.ActivationFunctionType
ALU = mybir.AluOpType
AX = mybir.AxisListType

SEQ, DM, NT, NO = 2048, 2048, 16, 8
H, HKV, DH, FH = 16, 4, 128, 8192
NEG = -3.0e38
ATT_SCALE = float(np.float32(1.0) / np.sqrt(np.float32(DH)))

TWO_PI = 2.0 * np.pi
P2_1 = 6.28125
_r = TWO_PI - P2_1
P2_2 = float(np.float32(int(_r * 2 ** 22) / 2 ** 22))
P2_3 = float(np.float32(_r - P2_2))
INV_2PI = float(np.float32(1.0 / TWO_PI))
PIH = np.pi / 2.0
PIH_HI = float(np.float32(int(PIH * 2 ** 20) / 2 ** 20))
PIH_LO = float(np.float32(PIH - PIH_HI))
L64 = float(np.log(10000.0) / 64.0)
L64_HI = float(np.float32(int(L64 * 2 ** 18) / 2 ** 18))
L64_LO = float(np.float32(L64 - L64_HI))


class Builder:
    def __init__(self, dbg_names=()):
        self.nc = bass.Bass("TRN2", target_bir_lowering=False)
        self.events = []
        self.cnt = {}
        self.dbg_names = set(dbg_names)
        self.dbg_bufs = []
        self._ctxs = []
        self._ivbufs = []

    def sb(self, name, shape, dtype, phases=None, **kw):
        if phases is not None and name in self.dbg_names:
            phases = (phases[0], 7)
        if phases is None:
            cm = self.nc.sbuf_tensor(name, shape, dtype, **kw)
            t = cm.__enter__()
            self._ctxs.append(cm)
            return t
        t = self.nc.virtual_sbuf_tensor(name, list(shape), dtype)
        bpp = int(np.prod(shape[1:])) * mybir.dt.size(dtype)
        self._ivbufs.append((t, list(shape), dtype, bpp, phases[0], phases[1]))
        return t

    def pack(self):
        free0 = (self.nc.sbuf_base + 511) & ~511
        cap = self.nc.sbuf_bytes_remaining - (free0 - self.nc.sbuf_base)
        placed = []
        assign = {}
        for (t, shape, dtype, bpp, lo, hi) in sorted(
                self._ivbufs, key=lambda x: -x[3]):
            size = (bpp + 511) & ~511
            cands = sorted({0} | {p[0] + p[1] for p in placed})
            off = None
            for c in cands:
                if c + size > cap:
                    continue
                ok = True
                for (p0, p1, p2, p3) in placed:
                    if not (hi < p2 or lo > p3):
                        if not (c + size <= p0 or c >= p0 + p1):
                            ok = False
                            break
                if ok:
                    off = c
                    break
            assert off is not None, \
                f"pack failed for {t.name} size {size} (cap {cap})"
            placed.append((off, size, lo, hi))
            assign[t.name] = (off, shape, dtype, t)
        total = max(p[0] + p[1] for p in placed)
        for name, (off, shape, dtype, t) in assign.items():
            bk = self.nc.alloc_sbuf_tensor_at(
                name + "_bk", shape, dtype, offset=free0 + off)
            self.nc.back_virtual_tensor_with(t, bk)
        return total, cap

    def ps(self, name, shape):
        cm = self.nc.psum_tensor(name, shape, F32)
        t = cm.__enter__()
        self._ctxs.append(cm)
        return t

    def sem(self, name):
        cm = self.nc.semaphore(name)
        s = cm.__enter__()
        self._ctxs.append(cm)
        return s

    def init_chains(self):
        self.chsem = {}
        self.chcnt = {}

    def mkchain(self, tag):
        self.chsem[tag] = self.sem("CH_" + tag)
        self.chcnt[tag] = 0

    def ch(self, tag, eng, inst):
        self.chcnt[tag] += 1
        inst.then_inc(self.chsem[tag], 1)
        eng.wait_ge(self.chsem[tag], self.chcnt[tag])
        return inst

    def ev(self, tag, fn, waits=(), sem=None, inc=0):
        self.events.append((tag, fn, tuple((s, v) for (s, v) in waits if v > 0),
                            sem, inc))
        if sem is not None and inc:
            self.cnt[sem] = self.cnt.get(sem, 0) + inc
        return self.cnt.get(sem, 0)

    def val(self, sem):
        return self.cnt.get(sem, 0)

    def dbg(self, name, buf):
        if name in self.dbg_names:
            shape = [buf.shape[0], int(np.prod(buf.shape[1:]))]
            d = self.nc.dram_tensor("dbg_" + name, shape, buf.dtype,
                                    kind="ExternalOutput").ap()
            self.dbg_bufs.append((name, buf, d))

    def emit(self):
        P = self
        with self.nc.Block() as block:
            @block.sync
            def _(s):
                run_events(s, P, "s")

            @block.gpsimd
            def _(g):
                run_events(g, P, "g")

            @block.vector
            def _(v):
                run_events(v, P, "v")

            @block.scalar
            def _(sc):
                run_events(sc, P, "sc")

            @block.tensor
            def _(t):
                run_events(t, P, "t")
        for cm in reversed(self._ctxs):
            cm.__exit__(None, None, None)


def run_events(eng, P, tag):
    for (etag, fn, waits, sem, inc) in P.events:
        if etag != tag:
            continue
        for (s, v) in waits:
            eng.wait_ge(s, v)
        inst = fn(eng)
        if sem is not None and inc and inst is not None:
            inst.then_inc(sem, inc)


def build(dbg_names=()):
    B = Builder(dbg_names)
    nc = B.nc

    d_xseq = nc.dram_tensor("xseq", [SEQ, DM], F32, kind="ExternalInput").ap()
    d_xown = nc.dram_tensor("xown", [NO * 128, DM], F32, kind="ExternalInput").ap()
    d_pm = nc.dram_tensor("pm", [128, 1], F32, kind="ExternalInput").ap()
    d_pm1 = nc.dram_tensor("pm1", [128, 1], F32, kind="ExternalInput").ap()
    d_wq = nc.dram_tensor("wq", [DM, DM], F32, kind="ExternalInput").ap()
    d_wk = nc.dram_tensor("wk", [512, DM], F32, kind="ExternalInput").ap()
    d_wv = nc.dram_tensor("wv", [512, DM], F32, kind="ExternalInput").ap()
    d_wo = nc.dram_tensor("wo", [DM, DM], F32, kind="ExternalInput").ap()
    d_wgs = nc.dram_tensor("wgs", [H, DM], F32, kind="ExternalInput").ap()
    d_wgw = nc.dram_tensor("wgw", [H, DM], F32, kind="ExternalInput").ap()
    d_wg = nc.dram_tensor("wg", [FH, DM], F32, kind="ExternalInput").ap()
    d_wu = nc.dram_tensor("wu", [FH, DM], F32, kind="ExternalInput").ap()
    d_wd = nc.dram_tensor("wd", [DM, FH], F32, kind="ExternalInput").ap()
    d_out = nc.dram_tensor("oown", [NO * 128, DM], F32, kind="ExternalOutput").ap()

    DXA = [B.sem("DXA0"), B.sem("DXA1")]
    DW = [B.sem("DW0"), B.sem("DW1"), B.sem("DW2"), B.sem("DW3")]
    DTN = [B.sem("DTN0"), B.sem("DTN1"), B.sem("DTN2"), B.sem("DTN3")]
    DF = [B.sem("DF0"), B.sem("DF1")]
    DQ = [B.sem("DQ0"), B.sem("DQ1")]
    DK = [B.sem("DK0"), B.sem("DK1")]
    DP = [B.sem("DP0"), B.sem("DP1")]
    DS = [B.sem("DS0"), B.sem("DS1")]
    DDB = B.sem("DDB")
    B.init_chains()
    for _tag in ("v", "sc", "g"):
        B.mkchain(_tag)
    CV = B.sem("CV")
    CS = B.sem("CS")
    CT = B.sem("CT")
    CG = B.sem("CG")

    psA = B.ps("psA", [128, 2048])
    psB = B.ps("psB", [128, 2048])

    ALLSEMS = DXA + DW + DTN + DF + DQ + DK + DP + DS + [CV, CS, CT, CG]

    def barrier(name):
        snap = [(sm, B.val(sm)) for sm in ALLSEMS if B.val(sm) > 0]
        for tag in ("s", "g", "v", "sc", "t"):
            B.ev(tag, (lambda e: None), waits=snap)

    ones_row = B.sb("ones_row", [1, 128], F32)
    ident = B.sb("ident", [128, 128], F32)
    mdiag = B.sb("mdiag", [128, 128], F32)
    mpre = B.sb("mpre", [128, 128], F32)
    mlast = B.sb("mlast", [128, 128], F32)
    pmb = B.sb("pmb", [128, 1], F32)
    pm1b = B.sb("pm1b", [128, 1], F32)
    pm1neg = B.sb("pm1neg", [128, 1], F32)
    wrow = B.sb("wrow", [1, 64], F32)
    omega = B.sb("omega", [128, 64], F32)
    tgi = B.sb("tgi", [128, 64], F32)
    rr1 = B.sb("rr1", [128, 64], F32)
    rr2 = B.sb("rr2", [128, 64], F32)
    rri = B.sb("rri", [128, 64], I32)
    rrk = B.sb("rrk", [128, 64], F32)
    zbQ = B.sb("zbQ", [128, 256], F32)
    sums = B.sb("sums", [128, 24], F32)
    sumsq = B.sb("sumsq", [128, 24], F32)
    mean_ = B.sb("mean_", [128, 24], F32)
    var_ = B.sb("var_", [128, 24], F32)
    rstd = B.sb("rstd", [128, 24], F32)
    nrs = B.sb("nrs", [128, 24], F32)
    bias_ = B.sb("bias_", [128, 24], F32)
    var2 = B.sb("var2", [128, 8], F32)
    rstd2 = B.sb("rstd2", [128, 8], F32)
    n8 = B.sb("n8", [128, 8], F32)
    amax = B.sb("amax", [128, 8], F32)
    zsc = B.sb("zsc", [128, 8], F32)
    ninvs = B.sb("ninvs", [128, 8], F32)
    gtm = B.sb("gtm", [128, NO, H], F32)
    gsig = B.sb("gsig", [128, 32], F32)
    lsum = B.sb("lsum", [128, 1], F32)
    mrow = B.sb("mrow", [128, 2], F32)
    lsum2 = B.sb("lsum2", [128, 2], F32)
    mrow2 = B.sb("mrow2", [128, 6], F32)
    wnat = [B.sb(f"wnat{i}", [128, DM], BF16) for i in range(4)]

    costm = B.sb("costm", [128, NT, 64], F32, phases=(0, 2))
    sintm = B.sb("sintm", [128, NT, 64], F32, phases=(0, 2))
    wkT = B.sb("wkT", [128, 16, 4, 128], BF16, phases=(2, 3))
    wvT = B.sb("wvT", [128, 16, 4, 128], BF16, phases=(2, 3))
    wgT = B.sb("wgT", [128, 16, 32], BF16, phases=(0, 2))
    sqs = B.sb("sqs", [128, DM], F32, phases=(0, 0))
    xsA = [B.sb(f"xsA{i}", [128, DM], F32, phases=(0, 2)) for i in range(2)]
    wqTh = B.sb("wqTh", [128, 16, 16, 128], BF16, phases=(0, 1))
    xnbo = [B.sb(f"xnbo{i}", [128, DM], BF16, phases=(1, 1)) for i in range(2)]
    xfm = [B.sb(f"xfm{i}", [128, NT, 128], BF16, phases=(1, 2)) for i in range(2)]
    qtm = [B.sb(f"qtm{i}", [128, 2048], BF16, phases=(1, 1)) for i in range(2)]
    crep = B.sb("crep", [128, 1024], F32, phases=(0, 2))
    srep = B.sb("srep", [128, 1024], F32, phases=(0, 2))
    zbW = B.sb("zbW", [128, 2048], F32, phases=(0, 2))
    zbW2 = B.sb("zbW2", [128, 1024], F32, phases=(0, 0))
    rriW = B.sb("rriW", [128, 1024], I32, phases=(0, 0))
    qfm = B.sb("qfm", [128, NO, H, 128], BF16, phases=(1, 3))
    xnb = [B.sb(f"xnb{i}", [128, DM], BF16, phases=(2, 2)) for i in range(2)]
    ktm = [B.sb(f"ktm{i}", [128, 512], BF16, phases=(2, 2)) for i in range(2)]
    kfm = B.sb("kfm", [128, HKV, NT, 128], BF16, phases=(2, 3))
    vtm = B.sb("vtm", [128, NT, 512], BF16, phases=(2, 3))
    pexp = [B.sb(f"pexp{i}", [128, 2048], F32, phases=(3, 3)) for i in range(2)]
    pbf = [B.sb(f"pbf{i}", [128, 2048], BF16, phases=(3, 3)) for i in range(2)]
    PT = [B.sb(f"PT{i}", [128, 16, 4, 128], BF16, phases=(3, 3)) for i in range(2)]
    attn = B.sb("attn", [128, 16, NO, 128], BF16, phases=(3, 4))
    oacc = B.sb("oacc", [128, NO, 2048], F32, phases=(4, 7))
    woT = B.sb("woT", [128, 16, 4, 128], BF16, phases=(4, 4))
    xsB = [B.sb(f"xsB{i}", [128, DM], F32, phases=(5, 5)) for i in range(2)]
    xnf = [B.sb(f"xnf{i}", [128, DM], F32, phases=(5, 5)) for i in range(2)]
    zb = B.sb("zb", [128, DM], F32, phases=(5, 5))
    zi = B.sb("zi", [128, DM], I32, phases=(5, 5))
    zk = B.sb("zk", [128, DM], F32, phases=(5, 5))
    zd = B.sb("zd", [128, DM], F32, phases=(5, 5))
    xqtm = [B.sb(f"xqtm{i}", [128, DM], BF16, phases=(5, 5)) for i in range(2)]
    rowb = B.sb("rowb", [1, 1024], F32, phases=(5, 5))
    xqfm = B.sb("xqfm", [128, 16, NO, 128], BF16, phases=(5, 6))
    nb_full = B.sb("nb_full", [128, 1024], F32, phases=(5, 6))
    pq = B.sb("pq", [128, 16, NO, 128], BF16, phases=(6, 6))
    wgnT = [B.sb(f"wgnT{i}", [128, 16, 128], BF16, phases=(6, 6)) for i in range(2)]
    wdT = [B.sb(f"wdT{i}", [128, 16, 4, 128], BF16, phases=(6, 6)) for i in range(2)]
    ffa = B.sb("ffa", [128, 1024], F32, phases=(6, 6))
    fft = B.sb("fft", [128, 1024], F32, phases=(6, 6))
    ffu = B.sb("ffu", [128, 1024], F32, phases=(6, 6))
    xsC = [B.sb(f"xsC{i}", [128, DM], F32, phases=(7, 7)) for i in range(2)]
    ost = [B.sb(f"ost{i}", [128, DM], F32, phases=(7, 7)) for i in range(2)]

    def g_const(g):
        g.memset(ones_row[:], 1.0)
        g.memset(mdiag[:], 0.0)
        g.memset(ident[:], 1.0)
        g.drain()
        g.affine_select(mdiag[:], mdiag[:], [[-1, 128]], ALU.is_ge, NEG,
                        base=0, channel_multiplier=1)
        g.affine_select(ident[:], ident[:], [[-1, 128]], ALU.is_equal, 0.0,
                        base=0, channel_multiplier=1)
        g.iota(wrow[:], [[1, 64]], base=0, channel_multiplier=0,
               allow_small_or_imprecise_dtypes=True)
        g.iota(srep[:, 0:1024].rearrange("p (a b) -> p a b", b=64),
               [[128, 16], [0, 64]], base=0, channel_multiplier=1,
               allow_small_or_imprecise_dtypes=True)
        return g.drain()
    cg_const = B.ev("g", g_const, sem=CG, inc=1)

    def s_pm(s):
        s.dma_start(out=pmb[:], in_=d_pm).then_inc(DXA[0], 16)
        s.dma_start(out=pm1b[:], in_=d_pm1).then_inc(DXA[1], 16)
        return None
    B.ev("s", s_pm)
    B.cnt[DXA[0]] = B.cnt.get(DXA[0], 0) + 16
    B.cnt[DXA[1]] = B.cnt.get(DXA[1], 0) + 16

    def v_masks(v):
        v.tensor_scalar_mul(pm1neg[:], pm1b[:], NEG)
        v.tensor_scalar_mul(mpre[:], mdiag[:], pm1b[:, 0:1])
        v.tensor_scalar_mul(mlast[:], mdiag[:], pmb[:, 0:1])
        v.drain()
        v.tensor_scalar_add(mlast[:], mlast[:], pm1neg[:, 0:1])
        v.tensor_scalar_mul(rr1[0:1, :], wrow[:], -L64_HI)
        v.drain()
        v.scalar_tensor_tensor(rr1[0:1, :], wrow[:], -L64_LO, rr1[0:1, :],
                               ALU.mult, ALU.add)
        return v.drain()
    cv_masks = B.ev("v", v_masks,
                    waits=[(CG, cg_const), (DXA[0], 16), (DXA[1], 16)],
                    sem=CV, inc=1)

    def s_omega(s):
        return s.activation(wrow[:], rr1[0:1, :], AF.Exp)
    cs_omega = B.ev("sc", s_omega, waits=[(CV, cv_masks)], sem=CS, inc=1)

    def t_omega(t):
        return t.matmul(psB[:, 0:64], ones_row[:], wrow[:], start=True, stop=True)
    ct_omega = B.ev("t", t_omega, waits=[(CS, cs_omega)], sem=CT, inc=1)

    def v_omega_ev(v):
        return v.tensor_copy(omega[:], psB[:, 0:64])
    cv_omega = B.ev("v", v_omega_ev, waits=[(CT, ct_omega)], sem=CV, inc=1)

    def rr_chain(v, src):
        for op in (
            lambda: v.tensor_scalar(rr2[:], src, INV_2PI, 0.5, ALU.mult, ALU.add),
            lambda: v.tensor_copy(rri[:], rr2[:]),
            lambda: v.tensor_copy(rrk[:], rri[:]),
            lambda: v.tensor_sub(rr2[:], rr2[:], rrk[:]),
            lambda: v.tensor_scalar(rr2[:], rr2[:], 0.0, None, ALU.is_lt),
            lambda: v.tensor_sub(rrk[:], rrk[:], rr2[:]),
            lambda: v.scalar_tensor_tensor(rr2[:], rrk[:], -P2_1, src,
                                           ALU.mult, ALU.add),
            lambda: v.scalar_tensor_tensor(rr2[:], rrk[:], -P2_2, rr2[:],
                                           ALU.mult, ALU.add),
            lambda: v.scalar_tensor_tensor(rr2[:], rrk[:], -P2_3, rr2[:],
                                           ALU.mult, ALU.add),
        ):
            B.ch("v", v, op())

    def v_tables(v):
        i0 = v.tensor_copy(crep[:, 0:64], omega[:])
        B.ch("v", v, i0)
        for w in (64, 128, 256, 512):
            B.ch("v", v, v.tensor_copy(crep[:, w:2 * w], crep[:, 0:w]))
        B.ch("v", v, v.tensor_mul(zbW[:, 0:1024], srep[:, 0:1024], crep[:, 0:1024]))
        return v.engine_nop()
    cv_tab0 = B.ev("v", v_tables, waits=[(CV, cv_omega), (CG, B.val(CG))],
                   sem=CV, inc=1)

    def rr_chain_w(v, src, uW, riW, rkW):
        for op in (
            lambda: v.tensor_scalar(uW, src, INV_2PI, 0.5, ALU.mult, ALU.add),
            lambda: v.tensor_copy(riW, uW),
            lambda: v.tensor_copy(rkW, riW),
            lambda: v.tensor_sub(uW, uW, rkW),
            lambda: v.tensor_scalar(uW, uW, 0.0, None, ALU.is_lt),
            lambda: v.tensor_sub(rkW, rkW, uW),
            lambda: v.scalar_tensor_tensor(uW, rkW, -P2_1, src, ALU.mult, ALU.add),
            lambda: v.scalar_tensor_tensor(uW, rkW, -P2_2, uW, ALU.mult, ALU.add),
            lambda: v.scalar_tensor_tensor(uW, rkW, -P2_3, uW, ALU.mult, ALU.add),
        ):
            B.ch("v", v, op())

    def v_tab_sin(v):
        rr_chain_w(v, zbW[:, 0:1024], zbW[:, 1024:2048], rriW[:], zbW2[:])
        return v.engine_nop()
    cv_tsin = B.ev("v", v_tab_sin, sem=CV, inc=1)

    def s_tab_sin(s):
        return s.activation(sintm[:].rearrange("p a b -> p (a b)"),
                            zbW[:, 1024:2048], AF.Sin)
    cs_tsin = B.ev("sc", s_tab_sin, waits=[(CV, cv_tsin)], sem=CS, inc=1)

    def v_tab_cos(v):
        B.ch("v", v, v.tensor_scalar_add(zbW[:, 0:1024], zbW[:, 0:1024], PIH_HI))
        B.ch("v", v, v.tensor_scalar_add(zbW[:, 0:1024], zbW[:, 0:1024], PIH_LO))
        rr_chain_w(v, zbW[:, 0:1024], zbW[:, 1024:2048], rriW[:], zbW2[:])
        return v.engine_nop()
    cv_tcos = B.ev("v", v_tab_cos, waits=[(CS, cs_tsin)], sem=CV, inc=1)

    def s_tab_cos(s):
        return s.activation(costm[:].rearrange("p a b -> p (a b)"),
                            zbW[:, 1024:2048], AF.Sin)
    B.ev("sc", s_tab_cos, waits=[(CV, cv_tcos)], sem=CS, inc=1)

    B.dbg("sintm", sintm)
    B.dbg("costm", costm)

    wnat_rel = [0, 0, 0, 0]

    class WS:
        idx = 0

    def wstream(src, rows, tpfn, extra_tp_wait_ct=0, sign=False):
        slot = WS.idx % 4
        WS.idx += 1

        def g_wload(g, src=src, rows=rows, slot=slot, rel=wnat_rel[slot]):
            if rel:
                g.wait_ge(DTN[slot], rel)
            return g.dma_start(out=wnat[slot][0:rows, :], in_=src)
        dw = B.ev("g", g_wload, sem=DW[slot], inc=16)

        cs_sg = 0
        if sign:
            def s_sign(s, slot=slot, rows=rows, dw=dw):
                s.wait_ge(DW[slot], dw)
                return s.activation(wnat[slot][0:rows, :], wnat[slot][0:rows, :],
                                    AF.Sign)
            cs_sg = B.ev("sc", s_sign, sem=CS, inc=1)

        def s_wtp(s, tpfn=tpfn, slot=slot, dw=dw, cs=cs_sg, xw=extra_tp_wait_ct):
            if cs:
                s.wait_ge(CS, cs)
            else:
                s.wait_ge(DW[slot], dw)
            if xw:
                s.wait_ge(CT, xw)
            return tpfn(s, slot)
        wnat_rel[slot] = B.ev("s", s_wtp, sem=DTN[slot], inc=16)
        return (DTN[slot], wnat_rel[slot])

    wstream(d_wgs, 16, lambda s, slot: s.dma_start(
        out=wgT[:, :, 0:16], in_=wnat[slot][0:16, :], transpose=True))
    wstream(d_wgw, 16, lambda s, slot: s.dma_start(
        out=wgT[:, :, 16:32], in_=wnat[slot][0:16, :], transpose=True))
    for b in range(16):
        wstream(d_wq[b * 128:(b + 1) * 128, :], 128,
                lambda s, slot, b=b: s.dma_start(
                    out=wqTh[:, :, b, :], in_=wnat[slot][:], transpose=True))
    B.dbg("wkT", wkT)

    xsrc = [(d_xseq, t, t) for t in range(NT)] + \
           [(d_xown, t, NT + t) for t in range(NO)]
    xrel = [(0, 0)] * 2
    cs_sq_prev = 0
    for idx, (src, t, col) in enumerate(xsrc):
        slot = idx % 2

        def s_ldx(s, src=src, t=t, slot=slot, rel=xrel[slot],
                  dxp=B.val(DXA[slot])):
            s.wait_ge(DXA[slot], dxp)
            if rel[0]:
                s.wait_ge(CV, rel[0])
            if rel[1]:
                s.wait_ge(CS, rel[1])
            return s.dma_start(out=xsA[slot][:], in_=src[t * 128:(t + 1) * 128, :])
        dx = B.ev("s", s_ldx, sem=DXA[slot], inc=16)

        def v_sum(v, slot=slot, col=col):
            return v.reduce_sum(sums[:, col:col + 1], xsA[slot][:], axis=AX.X)
        cv_h = B.ev("v", v_sum, waits=[(DXA[slot], dx)], sem=CV, inc=1)

        def s_sq(s, slot=slot, col=col, relsq=cs_sq_prev):
            if relsq:
                s.wait_ge(CS, relsq)
            return s.activation(sqs[:], xsA[slot][:], AF.Square,
                                accum_out=sumsq[:, col:col + 1])
        cs_h = B.ev("sc", s_sq, waits=[(DXA[slot], dx)], sem=CS, inc=1)
        cs_sq_prev = cs_h
        xrel[slot] = (cv_h, cs_h)

    def nr_rsqrt(v, r, vv, scratch, iters=6):
        v.memset(r, 1.0)
        v.drain()
        for _ in range(iters):
            v.tensor_mul(scratch, r, r)
            v.drain()
            v.tensor_mul(scratch, scratch, vv)
            v.drain()
            v.tensor_scalar(scratch, scratch, -0.5, 1.5, ALU.mult, ALU.add)
            v.drain()
            v.tensor_mul(r, r, scratch)
            v.drain()

    def v_stats(v):
        v.tensor_scalar_mul(mean_[:], sums[:], 1.0 / DM)
        v.tensor_scalar_mul(var_[:], sumsq[:], 1.0 / DM)
        v.drain()
        v.tensor_mul(nrs[:], mean_[:], mean_[:])
        v.drain()
        v.tensor_sub(var_[:], var_[:], nrs[:])
        v.drain()
        v.tensor_scalar_add(var_[:], var_[:], 1e-5)
        v.drain()
        nr_rsqrt(v, rstd[:], var_[:], nrs[:])
        v.tensor_mul(bias_[:], mean_[:], rstd[:])
        v.drain()
        v.tensor_scalar_mul(bias_[:], bias_[:], -1.0)
        v.tensor_scalar_add(var2[:], var_[:, 16:24], -1e-5)
        v.drain()
        v.tensor_mul(n8[:], rstd[:, 16:24], rstd[:, 16:24])
        v.drain()
        v.tensor_mul(var2[:], var2[:], n8[:])
        v.drain()
        v.tensor_scalar_add(var2[:], var2[:], 1e-8)
        v.drain()
        nr_rsqrt(v, rstd2[:], var2[:], n8[:])
        return v.drain()
    cv_stats = B.ev("v", v_stats,
                    waits=[(CV, B.val(CV)), (CS, B.val(CS))], sem=CV, inc=1)
    B.dbg("rstd", rstd)
    B.dbg("rstd2", rstd2)
    B.dbg("bias_", bias_)

    barrier("pc")
    relC_ct = [0, 0]
    relC_dtq = [0, 0]
    rope_hist = []
    cs_gate_prev = 0
    xrelC = list(xrel)
    cs_xnbo = {}
    xnbo_rel = [0, 0]
    dt_wq_all = tuple(B.val(x) for x in DTN)
    for j in range(NO):
        fslot = j % 2
        slot = j % 2
        cv_rope_m2 = rope_hist[-2] if len(rope_hist) >= 2 else 0

        def s_ldxo(s, j=j, slot=slot, rel=xrelC[slot], dxp=B.val(DXA[slot])):
            s.wait_ge(DXA[slot], dxp)
            if rel[0]:
                s.wait_ge(CV, rel[0])
            if rel[1]:
                s.wait_ge(CS, rel[1])
            return s.dma_start(out=xsA[slot][:],
                               in_=d_xown[j * 128:(j + 1) * 128, :])
        dx = B.ev("s", s_ldxo, sem=DXA[slot], inc=16)

        def s_xn(s, j=j, slot=slot, fslot=fslot, dx=dx,
                 relb=xnbo_rel[fslot]):
            s.wait_ge(DXA[slot], dx)
            if relb:
                s.wait_ge(DF[fslot], relb)
            return s.activation(xnbo[fslot][:], xsA[slot][:], AF.Identity,
                                bias=bias_[:, 16 + j:17 + j],
                                scale=rstd[:, 16 + j:17 + j])
        cs_xn = B.ev("sc", s_xn, waits=[(CV, cv_stats)], sem=CS, inc=1)
        cs_xnbo[j] = cs_xn
        xrelC[slot] = (0, cs_xn)

        def s_tpxc(s, j=j, fslot=fslot, relct=relC_ct[fslot]):
            s.wait_ge(CS, cs_xnbo[j])
            if relct:
                s.wait_ge(CT, relct)
            return s.dma_start(out=xfm[fslot][:, 0:16, :], in_=xnbo[fslot][:],
                               transpose=True)
        dt_x = B.ev("s", s_tpxc, sem=DF[fslot], inc=16)
        xnbo_rel[fslot] = dt_x

        def t_q(t, j=j, fslot=fslot, dt=dt_x, cvprev=cv_rope_m2,
                csg_prev=cs_gate_prev, dtw=dt_wq_all):
            t.wait_ge(DF[fslot], dt)
            for _i in range(4):
                t.wait_ge(DTN[_i], dtw[_i])
            if cvprev:
                t.wait_ge(CV, cvprev)
            if csg_prev:
                t.wait_ge(CS, csg_prev)
            mm = None
            for k in range(16):
                lhsT = xfm[fslot][:, k, :]
                for nn in range(4):
                    mm = t.matmul(psA[:, nn * 512:(nn + 1) * 512],
                                  lhsT, wqTh[:, k, nn * 4:(nn + 1) * 4, :],
                                  start=(k == 0), stop=(k == 15))
                mm = t.matmul(psB[:, 1792:1824], lhsT, wgT[:, k, :],
                              start=(k == 0), stop=(k == 15))
            return mm
        ct_q = B.ev("t", t_q, sem=CT, inc=1)
        relC_ct[fslot] = ct_q

        def s_gate(s, j=j, ct=ct_q):
            s.wait_ge(CT, ct)
            return s.activation(gsig[:], psB[:, 1792:1824], AF.Tanh, scale=0.5)
        cs_gate = B.ev("sc", s_gate, sem=CS, inc=1)
        cs_gate_prev = cs_gate

        def v_rope_q(v, j=j, fslot=fslot, ct=ct_q, csg=cs_gate,
                     reldq=relC_dtq[fslot]):
            if reldq:
                v.wait_ge(DQ[fslot], reldq)
            v.wait_ge(CT, ct)
            v.tensor_scalar_mul(crep[:, 0:64], costm[:, 2 * j, :], pm1b[:, 0:1])
            i1 = v.tensor_scalar_mul(srep[:, 0:64], sintm[:, 2 * j, :],
                                     pm1b[:, 0:1])
            B.ch("v", v, i1)
            v.scalar_tensor_tensor(crep[:, 0:64], costm[:, 2 * j + 1, :],
                                   pmb[:, 0:1], crep[:, 0:64], ALU.mult, ALU.add)
            i2 = v.scalar_tensor_tensor(srep[:, 0:64], sintm[:, 2 * j + 1, :],
                                        pmb[:, 0:1], srep[:, 0:64],
                                        ALU.mult, ALU.add)
            B.ch("v", v, i2)
            for w in (64, 128, 256, 512):
                v.tensor_copy(crep[:, w:2 * w], crep[:, 0:w])
                i3 = v.tensor_copy(srep[:, w:2 * w], srep[:, 0:w])
                B.ch("v", v, i3)
            ps3 = psA[:].rearrange("p (h t) -> p h t", t=128)
            x1 = ps3[:, :, 0:64]
            x2 = ps3[:, :, 64:128]
            c3 = crep[:].rearrange("p (h t) -> p h t", t=64)
            s3 = srep[:].rearrange("p (h t) -> p h t", t=64)
            o3 = qtm[fslot][:].rearrange("p (h t) -> p h t", t=128)
            v.tensor_mul(zbW[:, 0:1024], x1, c3)
            i4 = v.tensor_mul(zbW[:, 1024:2048], x2, s3)
            B.ch("v", v, i4)
            i4b = v.tensor_sub(
                o3[:, :, 0:64],
                zbW[:, 0:1024].rearrange("p (h t) -> p h t", t=64),
                zbW[:, 1024:2048].rearrange("p (h t) -> p h t", t=64))
            B.ch("v", v, i4b)
            v.tensor_mul(zbW[:, 0:1024], x2, c3)
            i5 = v.tensor_mul(zbW[:, 1024:2048], x1, s3)
            B.ch("v", v, i5)
            out = v.tensor_add(
                o3[:, :, 64:128],
                zbW[:, 0:1024].rearrange("p (h t) -> p h t", t=64),
                zbW[:, 1024:2048].rearrange("p (h t) -> p h t", t=64))
            v.wait_ge(CS, csg)
            i6 = v.tensor_add(gsig[:, 0:16], gsig[:, 0:16], gsig[:, 16:32])
            B.ch("v", v, i6)
            out = v.tensor_scalar(gtm[:, j, :], gsig[:, 0:16], 0.5, 1.0,
                                  ALU.mult, ALU.add)
            return out
        cv_rope = B.ev("v", v_rope_q, sem=CV, inc=1)
        rope_hist.append(cv_rope)

        def s_tpq(s, j=j, fslot=fslot, cv=cv_rope):
            s.wait_ge(CV, cv)
            return s.dma_start(out=qfm[:, j, :, :], in_=qtm[fslot][:],
                               transpose=True)
        relC_dtq[fslot] = B.ev("s", s_tpq, sem=DQ[fslot], inc=16)

    B.dbg("qfm", qfm)
    B.dbg("gtm", gtm)

    barrier("pb")

    for b in range(4):
        wstream(d_wk[b * 128:(b + 1) * 128, :], 128,
                lambda s, slot, b=b: s.dma_start(
                    out=wkT[:, :, b, :], in_=wnat[slot][:], transpose=True))
    for b in range(4):
        wstream(d_wv[b * 128:(b + 1) * 128, :], 128,
                lambda s, slot, b=b: s.dma_start(
                    out=wvT[:, :, b, :], in_=wnat[slot][:], transpose=True))
    DTN_attw = tuple(B.val(x) for x in DTN)
    xrelB = list(xrelC)
    cv_kv_par = [0, 0]
    relB_dtx = [0, 0]
    relB_ctx = list(relC_ct)
    relB_dtk = [0, 0]
    for t_i in range(NT):
        slot = t_i % 2
        fslot = t_i % 2

        def s_ldxs(s, t_i=t_i, slot=slot, rel=xrelB[slot], dxp=B.val(DXA[slot])):
            s.wait_ge(DXA[slot], dxp)
            if rel[0]:
                s.wait_ge(CV, rel[0])
            if rel[1]:
                s.wait_ge(CS, rel[1])
            return s.dma_start(out=xsA[slot][:],
                               in_=d_xseq[t_i * 128:(t_i + 1) * 128, :])
        dx = B.ev("s", s_ldxs, sem=DXA[slot], inc=16)

        def s_xn2(s, t_i=t_i, slot=slot, fslot=fslot, dx=dx, rel=relB_dtx[fslot]):
            s.wait_ge(DXA[slot], dx)
            if rel:
                s.wait_ge(DF[fslot], rel)
            return s.activation(xnb[fslot][:], xsA[slot][:], AF.Identity,
                                bias=bias_[:, t_i:t_i + 1],
                                scale=rstd[:, t_i:t_i + 1])
        cs_xn = B.ev("sc", s_xn2, waits=[(CV, cv_stats)], sem=CS, inc=1)
        xrelB[slot] = (0, cs_xn)

        def s_tpx2(s, fslot=fslot, cs=cs_xn, relct=relB_ctx[fslot]):
            s.wait_ge(CS, cs)
            if relct:
                s.wait_ge(CT, relct)
            return s.dma_start(out=xfm[fslot][:], in_=xnb[fslot][:], transpose=True)
        dt_x = B.ev("s", s_tpx2, sem=DF[fslot], inc=16)
        relB_dtx[fslot] = dt_x

        def t_kv(t, t_i=t_i, fslot=fslot, dt=dt_x, cvprev=cv_kv_par[t_i % 2]):
            t.wait_ge(DF[fslot], dt)
            for _i in range(4):
                t.wait_ge(DTN[_i], DTN_attw[_i])
            if cvprev:
                t.wait_ge(CV, cvprev)
            off = 0 if (t_i % 2 == 0) else 1024
            mm = None
            for k in range(16):
                lhsT = xfm[fslot][:, k, :]
                t.matmul(psB[:, off:off + 512], lhsT, wkT[:, k, :, :],
                         start=(k == 0), stop=(k == 15))
                mm = t.matmul(psB[:, off + 512:off + 1024], lhsT, wvT[:, k, :, :],
                              start=(k == 0), stop=(k == 15))
            return mm
        ct_kv = B.ev("t", t_kv, sem=CT, inc=1)
        relB_ctx[fslot] = ct_kv

        def v_kv(v, t_i=t_i, fslot=fslot, ct=ct_kv, rel=relB_dtk[fslot]):
            v.wait_ge(CT, ct)
            if rel:
                v.wait_ge(DK[fslot], rel)
            off = 0 if (t_i % 2 == 0) else 1024
            v.tensor_copy(crep[:, 0:64], costm[:, t_i, :])
            i1 = v.tensor_copy(srep[:, 0:64], sintm[:, t_i, :])
            B.ch("v", v, i1)
            for w in (64, 128):
                v.tensor_copy(crep[:, w:2 * w], crep[:, 0:w])
                i2 = v.tensor_copy(srep[:, w:2 * w], srep[:, 0:w])
                B.ch("v", v, i2)
            ps3 = psB[:, off:off + 512].rearrange("p (h t) -> p h t", t=128)
            x1 = ps3[:, :, 0:64]
            x2 = ps3[:, :, 64:128]
            c3 = crep[:, 0:256].rearrange("p (h t) -> p h t", t=64)
            s3 = srep[:, 0:256].rearrange("p (h t) -> p h t", t=64)
            o3 = ktm[fslot][:].rearrange("p (h t) -> p h t", t=128)
            v.tensor_mul(zbW[:, 0:256], x1, c3)
            v.tensor_mul(zbW[:, 512:768], x2, s3)
            v.tensor_mul(zbW[:, 1024:1280], x2, c3)
            i3 = v.tensor_mul(zbW[:, 1536:1792], x1, s3)
            B.ch("v", v, i3)
            v.tensor_sub(o3[:, :, 0:64],
                         zbW[:, 0:256].rearrange("p (h t) -> p h t", t=64),
                         zbW[:, 512:768].rearrange("p (h t) -> p h t", t=64))
            v.tensor_add(o3[:, :, 64:128],
                         zbW[:, 1024:1280].rearrange("p (h t) -> p h t", t=64),
                         zbW[:, 1536:1792].rearrange("p (h t) -> p h t", t=64))
            return v.tensor_copy(vtm[:, t_i, :], psB[:, off + 512:off + 1024])
        cv_kv = B.ev("v", v_kv, sem=CV, inc=1)
        cv_kv_par[t_i % 2] = cv_kv

        def s_tpk(s, t_i=t_i, fslot=fslot, cv=cv_kv):
            s.wait_ge(CV, cv)
            return s.dma_start(out=kfm[:, :, t_i, :], in_=ktm[fslot][:],
                               transpose=True)
        relB_dtk[fslot] = B.ev("s", s_tpk, sem=DK[fslot], inc=16)

    B.dbg("kfm", kfm)
    B.dbg("vtm", vtm)
    DKQ_all = [(DK[0], B.val(DK[0])), (DK[1], B.val(DK[1])),
               (DQ[0], B.val(DQ[0])), (DQ[1], B.val(DQ[1]))]
    CV_kv_all = B.val(CV)

    barrier("pd")

    cs_exp_prev = 0
    cs_exp_byreg = [0, 0]
    cv_scale_prev = 0
    cv_sc_slot = [0, 0]
    dt_p_slot = [0, 0]
    pt_rel = [0, 0]
    dt_p = {}
    cv_pv_prev = 0
    for j in range(NO):
        nch = 2 * j + 2
        keys = 128 * nch
        for h in range(H):
            g = h // HKV
            hh = h % HKV
            pslot = h % 2
            tslot = (j * 4 + g) % 2
            reg = 1 if (h % 2 == 1 and j <= 5) else 0
            Sps = (psA[:, 0:keys] if reg == 0 else psB[:, 512:512 + keys])

            def t_S(t, j=j, h=h, g=g, nch=nch, Sps=Sps,
                    cse=cs_exp_byreg[reg]):
                if cse:
                    t.wait_ge(CS, cse)
                lhsT = qfm[:, j, h, :]
                mm = None
                for c0 in range(0, nch, 4):
                    cw = min(4, nch - c0)
                    mm = t.matmul(Sps[:, c0 * 128:(c0 + cw) * 128], lhsT,
                                  kfm[:, g, c0:c0 + cw, :], start=True, stop=True)
                return mm
            ct_S = B.ev("t", t_S, waits=DKQ_all + [(CV, CV_kv_all)],
                        sem=CT, inc=1)

            def v_msk(v, j=j, nch=nch, keys=keys, Sps=Sps, ct=ct_S, pc=pslot):
                v.wait_ge(CT, ct)
                v.tensor_add(Sps[:, (nch - 2) * 128:(nch - 1) * 128],
                             Sps[:, (nch - 2) * 128:(nch - 1) * 128], mpre[:])
                i1 = v.tensor_add(Sps[:, (nch - 1) * 128:nch * 128],
                                  Sps[:, (nch - 1) * 128:nch * 128], mlast[:])
                B.ch("v", v, i1)
                i2 = v.reduce_max(mrow2[:, pc:pc + 1], Sps[:, 0:keys], axis=AX.X)
                B.ch("v", v, i2)
                return v.tensor_scalar_mul(mrow2[:, 2 + pc:3 + pc],
                                           mrow2[:, pc:pc + 1], -ATT_SCALE)
            cv_m = B.ev("v", v_msk, sem=CV, inc=1)

            def s_exp(s, keys=keys, pslot=pslot, cv=cv_m, Sps=Sps,
                      rel=cv_sc_slot[pslot]):
                s.wait_ge(CV, cv)
                if rel:
                    s.wait_ge(CV, rel)
                return s.activation(pexp[pslot][:, 0:keys], Sps[:, 0:keys],
                                    AF.Exp, bias=mrow2[:, 2 + pslot:3 + pslot],
                                    scale=ATT_SCALE,
                                    accum_out=lsum2[:, pslot:pslot + 1])
            cs_exp_prev = B.ev("sc", s_exp, sem=CS, inc=1)
            cs_exp_byreg[reg] = cs_exp_prev

            def v_scale(v, j=j, h=h, keys=keys, pslot=pslot, cs=cs_exp_prev,
                        reldt=dt_p_slot[pslot]):
                v.wait_ge(CS, cs)
                if reldt:
                    v.wait_ge(DP[pslot], reldt)
                i1 = v.reciprocal(mrow2[:, 4 + pslot:5 + pslot],
                                  lsum2[:, pslot:pslot + 1])
                B.ch("v", v, i1)
                i2 = v.tensor_mul(mrow2[:, 4 + pslot:5 + pslot],
                                  mrow2[:, 4 + pslot:5 + pslot],
                                  gtm[:, j, h:h + 1])
                B.ch("v", v, i2)
                return v.tensor_scalar_mul(pbf[pslot][:, 0:keys],
                                           pexp[pslot][:, 0:keys],
                                           mrow2[:, 4 + pslot:5 + pslot])
            cv_sc = B.ev("v", v_scale, sem=CV, inc=1)
            cv_scale_prev = cv_sc
            cv_sc_slot[pslot] = cv_sc

            def s_tpp(s, nch=nch, keys=keys, hh=hh, pslot=pslot, tslot=tslot,
                      cv=cv_sc, relct=pt_rel[tslot]):
                s.wait_ge(CV, cv)
                if relct:
                    s.wait_ge(CT, relct)
                return s.dma_start(out=PT[tslot][:, 0:nch, hh, :],
                                   in_=pbf[pslot][:, 0:keys], transpose=True)
            dt_p[(j, h)] = B.ev("s", s_tpp, sem=DP[pslot], inc=16)
            dt_p_slot[pslot] = dt_p[(j, h)]

            if hh == HKV - 1:
                def t_PV(t, j=j, g=g, nch=nch, tslot=tslot,
                         dtp0=dt_p[(j, h - 1)], dtp1=dt_p[(j, h)],
                         cvp=cv_pv_prev):
                    t.wait_ge(DP[0], dtp0)
                    t.wait_ge(DP[1], dtp1)
                    if cvp:
                        t.wait_ge(CV, cvp)
                    mm = None
                    for c in range(nch):
                        mm = t.matmul(psB[:, 0:512],
                                      vtm[:, c, g * 128:(g + 1) * 128],
                                      PT[tslot][:, c, :, :],
                                      start=(c == 0), stop=(c == nch - 1))
                    return mm
                ct_pv = B.ev("t", t_PV, waits=[(CV, CV_kv_all)], sem=CT, inc=1)
                pt_rel[tslot] = ct_pv

                def v_att(v, j=j, g=g, ct=ct_pv):
                    v.wait_ge(CT, ct)
                    return v.tensor_copy(attn[:, 4 * g:4 * g + 4, j, :],
                                         psB[:, 0:512])
                cv_pv_prev = B.ev("v", v_att, sem=CV, inc=1)

    B.dbg("attn", attn)
    CV_attn_done = B.val(CV)

    barrier("pe0")

    def psm(m):
        src = psA if m < 4 else psB
        return src[:, (m % 4) * 512:(m % 4 + 1) * 512]

    cv_evac_prev = 0
    ct_o = 0
    for n in range(4):
        dts = []
        for b in range(4):
            dts.append(wstream(
                d_wo[n * 512 + b * 128:n * 512 + (b + 1) * 128, :], 128,
                (lambda s, slot, b=b: s.dma_start(out=woT[:, :, b, :],
                                                  in_=wnat[slot][:],
                                                  transpose=True)),
                extra_tp_wait_ct=ct_o))

        for mg in range(2):
            def t_wo(t, n=n, mg=mg, dt=tuple(dts), cvp=cv_evac_prev):
                for (dsem, dval) in dt:
                    t.wait_ge(dsem, dval)
                if cvp:
                    t.wait_ge(CV, cvp)
                mm = None
                for m in range(4 * mg, 4 * mg + 4):
                    for k in range(16):
                        mm = t.matmul(psm(m), attn[:, k, m, :], woT[:, k, :, :],
                                      start=(k == 0), stop=(k == 15))
                return mm
            ct_o = B.ev("t", t_wo, waits=[(CV, CV_attn_done), (CS, B.val(CS))],
                        sem=CT, inc=1)

            def v_evo(v, n=n, mg=mg, ct=ct_o):
                v.wait_ge(CT, ct)
                mm = None
                for m in range(4 * mg, 4 * mg + 4):
                    mm = v.tensor_copy(oacc[:, m, n * 512:(n + 1) * 512], psm(m))
                return mm
            cv_evac_prev = B.ev("v", v_evo, sem=CV, inc=1)

    barrier("p45")

    xrelQ = [(cv_evac_prev, B.val(CS))] * 2
    relQ_dt = [B.val(DQ[0]), B.val(DQ[1])]
    xnf_rel = [0, 0]
    cv_q_prev = 0
    for j in range(NO):
        slot = j % 2
        fslot = j % 2

        def s_ldxq(s, j=j, slot=slot, rel=xrelQ[slot], dxp=B.val(DXA[slot])):
            s.wait_ge(DXA[slot], dxp)
            if rel[0]:
                s.wait_ge(CV, rel[0])
            if rel[1]:
                s.wait_ge(CS, rel[1])
            return s.dma_start(out=xsB[slot][:], in_=d_xown[j * 128:(j + 1) * 128, :])
        dx = B.ev("s", s_ldxq, sem=DXA[slot], inc=16)

        def s_xn3(s, j=j, slot=slot, fslot=fslot, dx=dx, rel=xnf_rel[fslot]):
            s.wait_ge(DXA[slot], dx)
            if rel:
                s.wait_ge(CV, rel)
            return s.activation(xnf[fslot][:], xsB[slot][:], AF.Identity,
                                bias=bias_[:, 16 + j:17 + j],
                                scale=rstd[:, 16 + j:17 + j])
        cs_xn = B.ev("sc", s_xn3, sem=CS, inc=1)
        xrelQ[slot] = (0, cs_xn)

        def v_quant(v, j=j, fslot=fslot, cs=cs_xn, reldt=relQ_dt[fslot]):
            v.wait_ge(CS, cs)
            if reldt:
                v.wait_ge(DQ[fslot], reldt)
            xn = xnf[fslot][:]
            i1 = v.reduce_max(amax[:, j:j + 1], xn, axis=AX.X,
                              apply_absolute_value=True)
            B.ch("v", v, i1)
            i2 = v.tensor_mul(amax[:, j:j + 1], amax[:, j:j + 1],
                              rstd2[:, j:j + 1])
            B.ch("v", v, i2)
            i3 = v.reciprocal(n8[:, j:j + 1], amax[:, j:j + 1])
            B.ch("v", v, i3)
            i4 = v.tensor_scalar_mul(n8[:, j:j + 1], n8[:, j:j + 1], 127.0)
            B.ch("v", v, i4)
            v.reciprocal(ninvs[:, j:j + 1], n8[:, j:j + 1])
            i5 = v.tensor_mul(zsc[:, j:j + 1], rstd2[:, j:j + 1], n8[:, j:j + 1])
            B.ch("v", v, i5)
            v.tensor_scalar_mul(ninvs[:, j:j + 1], ninvs[:, j:j + 1], -1.0)
            i6 = v.tensor_scalar_mul(zb[:], xn, zsc[:, j:j + 1])
            B.ch("v", v, i6)
            i7 = v.tensor_copy(zi[:], zb[:])
            B.ch("v", v, i7)
            return v.tensor_copy(zk[:], zi[:])
        cv_q1 = B.ev("v", v_quant, sem=CV, inc=1)
        xnf_rel[fslot] = cv_q1

        def g_quant(g, cv=cv_q1):
            g.wait_ge(CV, cv)
            return g.tensor_sub(zd[:], zb[:], zk[:])
        cg_q = B.ev("g", g_quant, sem=CG, inc=1)

        def v_quant2(v, j=j, fslot=fslot, cg=cg_q):
            v.wait_ge(CG, cg)
            i1 = v.scalar_tensor_tensor(zk[:], zd[:], 0.5, zk[:],
                                        ALU.is_ge, ALU.add)
            B.ch("v", v, i1)
            return v.scalar_tensor_tensor(xqtm[fslot][:], zd[:], -0.5, zk[:],
                                          ALU.is_lt, ALU.subtract)
        cv_q_prev = B.ev("v", v_quant2, sem=CV, inc=1)

        def s_tpxq(s, j=j, fslot=fslot, cv=cv_q_prev):
            s.wait_ge(CV, cv)
            return s.dma_start(out=xqfm[:, :, j, :], in_=xqtm[fslot][:],
                               transpose=True)
        relQ_dt[fslot] = B.ev("s", s_tpxq, sem=DQ[fslot], inc=16)
    B.dbg("xqfm", xqfm)
    B.dbg("ninvs", ninvs)
    B.dbg("zsc", zsc)
    cv_q_all = cv_q_prev

    def t_invT(t):
        mm = None
        for m in range(8):
            mm = t.transpose(psB[0:1, m * 128:(m + 1) * 128], ninvs[:, m:m + 1],
                             ident[:])
        return mm
    ct_it = B.ev("t", t_invT, waits=[(CV, max(cv_q_all, cv_evac_prev))],
                 sem=CT, inc=1)

    def v_row(v):
        v.wait_ge(CT, ct_it)
        return v.tensor_copy(rowb[:], psB[0:1, 0:1024])
    cv_row = B.ev("v", v_row, sem=CV, inc=1)

    def t_bc(t):
        t.wait_ge(CV, cv_row)
        t.matmul(psA[:, 0:512], ones_row[:], rowb[0:1, 0:512], start=True, stop=True)
        return t.matmul(psA[:, 512:1024], ones_row[:], rowb[0:1, 512:1024],
                        start=True, stop=True)
    ct_bc = B.ev("t", t_bc, sem=CT, inc=1)

    def v_bc(v):
        v.wait_ge(CT, ct_bc)
        return v.tensor_copy(nb_full[:], psA[:, 0:1024])
    cv_bc = B.ev("v", v_bc, sem=CV, inc=1)
    B.dbg("nb_full", nb_full)

    barrier("pe")

    cs_tanh_prev = 0
    cv_up_evac = 0
    up_evac_par = [0, 0]
    cv_dn_evac = cv_bc
    wgT_rel = [B.val(CT)] * 2
    for qt in range(4):
        for m in range(16):
            mh = qt * 16 + m
            for (wi, dsrc) in ((0, d_wg), (1, d_wu)):
                tslot = (m * 2 + wi) % 2
                dt_w = wstream(
                    dsrc[mh * 128:(mh + 1) * 128, :], 128,
                    (lambda s, slot, tslot=tslot: s.dma_start(
                        out=wgnT[tslot][:], in_=wnat[slot][:], transpose=True)),
                    extra_tp_wait_ct=wgT_rel[tslot], sign=True)

                upps = psA if m % 2 == 0 else psB

                def t_up(t, wi=wi, tslot=tslot, dt=dt_w, upps=upps,
                         cvp=max(up_evac_par[m % 2], cv_dn_evac)):
                    t.wait_ge(dt[0], dt[1])
                    if cvp:
                        t.wait_ge(CV, cvp)
                    off = 0 if wi == 0 else 1024
                    mm = None
                    for k in range(16):
                        for nn2 in range(2):
                            mm = t.matmul(
                                upps[:, off + nn2 * 512:off + (nn2 + 1) * 512],
                                wgnT[tslot][:, k, :],
                                xqfm[:, k, nn2 * 4:(nn2 + 1) * 4, :],
                                start=(k == 0), stop=(k == 15))
                    return mm
                wgT_rel[tslot] = B.ev("t", t_up, sem=CT, inc=1)
            ct_up = wgT_rel[(m * 2 + 1) % 2]

            def v_eps1(v, ct=ct_up, upps=upps, csrel=cs_tanh_prev,
                       cvrel=cv_up_evac):
                v.wait_ge(CT, ct)
                if csrel:
                    v.wait_ge(CS, csrel)
                if cvrel:
                    v.wait_ge(CV, cvrel)
                v.tensor_mul(ffa[:], upps[:, 0:1024], nb_full[:])
                return v.tensor_mul(ffu[:], upps[:, 1024:2048], nb_full[:])
            cv_e1 = B.ev("v", v_eps1, waits=[(CV, cv_bc)], sem=CV, inc=1)

            def s_tanh2(s, cv=cv_e1, cvrel=cv_up_evac):
                s.wait_ge(CV, cv)
                if cvrel:
                    s.wait_ge(CV, cvrel)
                return s.activation(fft[:], ffa[:], AF.Tanh, scale=0.5)
            cs_tanh_prev = B.ev("sc", s_tanh2, sem=CS, inc=1)

            def v_eps2(v, m=m, cs=cs_tanh_prev):
                v.wait_ge(CS, cs)
                v.tensor_mul(ffu[:], ffa[:], ffu[:])
                i1 = v.tensor_scalar(fft[:], fft[:], 0.5, 0.5, ALU.mult, ALU.add)
                B.ch("v", v, i1)
                return v.tensor_mul(pq[:, m, :, :], fft[:], ffu[:])
            cv_up_evac = B.ev("v", v_eps2, sem=CV, inc=1)
            up_evac_par[m % 2] = cv_up_evac
        cv_p_done = cv_up_evac

        wdT_rel = [B.val(CT)] * 2
        for n in range(4):
            dslot = n % 2
            dts = []
            for b in range(4):
                dts.append(wstream(
                    d_wd[n * 512 + b * 128:n * 512 + (b + 1) * 128,
                         qt * 2048:(qt + 1) * 2048], 128,
                    (lambda s, slot, dslot=dslot, b=b: s.dma_start(
                        out=wdT[dslot][:, :, b, :], in_=wnat[slot][:],
                        transpose=True)),
                    extra_tp_wait_ct=wdT_rel[dslot], sign=True))

            for mg in range(2):
                def t_dn(t, qt=qt, n=n, dslot=dslot, mg=mg, dt=tuple(dts),
                         cvpd=cv_p_done, cvev=cv_dn_evac):
                    for (dsem, dval) in dt:
                        t.wait_ge(dsem, dval)
                    t.wait_ge(CV, max(cvpd, cvev))
                    mm = None
                    for m in range(4 * mg, 4 * mg + 4):
                        for k in range(16):
                            mm = t.matmul(psm(m), pq[:, k, m, :],
                                          wdT[dslot][:, k, :, :],
                                          start=(k == 0), stop=(k == 15))
                    return mm
                ct_dn = B.ev("t", t_dn, sem=CT, inc=1)
                if mg == 1:
                    wdT_rel[dslot] = ct_dn

                def v_evd(v, n=n, mg=mg, ct=ct_dn):
                    v.wait_ge(CT, ct)
                    mm = None
                    for m in range(4 * mg, 4 * mg + 4):
                        mm = v.tensor_add(oacc[:, m, n * 512:(n + 1) * 512],
                                          oacc[:, m, n * 512:(n + 1) * 512],
                                          psm(m))
                    return mm
                cv_dn_evac = B.ev("v", v_evd, sem=CV, inc=1)

    barrier("fin")

    cs_all_pre_final = B.val(CS)
    xrelF = [(cv_dn_evac, cs_all_pre_final)] * 2
    ost_rel = [0, 0]
    for m in range(NO):
        slot = m % 2
        oslot = m % 2

        def s_ldxf(s, m=m, slot=slot, rel=xrelF[slot], dxp=B.val(DXA[slot])):
            s.wait_ge(DXA[slot], dxp)
            if rel[0]:
                s.wait_ge(CV, rel[0])
            if rel[1]:
                s.wait_ge(CS, rel[1])
            return s.dma_start(out=xsC[slot][:], in_=d_xown[m * 128:(m + 1) * 128, :])
        dx = B.ev("s", s_ldxf, sem=DXA[slot], inc=16)

        def v_fin(v, m=m, slot=slot, oslot=oslot, dx=dx, rel=ost_rel[oslot]):
            v.wait_ge(DXA[slot], dx)
            if rel:
                v.wait_ge(DS[oslot], rel)
            return v.tensor_add(ost[oslot][:], oacc[:, m, :], xsC[slot][:])
        cv_f = B.ev("v", v_fin, waits=[(CV, cv_dn_evac)], sem=CV, inc=1)
        xrelF[slot] = (cv_f, 0)

        def s_st(s, m=m, oslot=oslot, cv=cv_f):
            s.wait_ge(CV, cv)
            return s.dma_start(out=d_out[m * 128:(m + 1) * 128, :], in_=ost[oslot][:])
        ost_rel[oslot] = B.ev("s", s_st, sem=DS[oslot], inc=16)

    ds_fin = (B.val(DS[0]), B.val(DS[1]))
    if B.dbg_bufs:
        allw = [(CV, B.val(CV)), (CS, B.val(CS)), (CT, B.val(CT)),
                (CG, B.val(CG)), (DS[0], ds_fin[0]), (DS[1], ds_fin[1])]

        def s_dbg(s):
            for (name, buf, d) in B.dbg_bufs:
                s.dma_start(out=d, in_=buf[:]).then_inc(DDB, 16)
            return None
        B.ev("s", s_dbg, waits=allw)
        B.cnt[DDB] = B.cnt.get(DDB, 0) + 16 * len(B.dbg_bufs)

    def s_end(s, v0=ds_fin[0], v1=ds_fin[1], vd=B.val(DDB)):
        s.wait_ge(DS[0], v0)
        s.wait_ge(DS[1], v1)
        if vd:
            s.wait_ge(DDB, vd)
        return None
    B.ev("s", s_end)

    total, remain = B.pack()
    B.pack_info = (total, remain)
    B.emit()
    return B


ZMAP = [[2 * j for j in range(NO)], [2 * j + 1 for j in range(NO)]]


def kernel(L0_attn_hidden_states, L0_attn_Wq, L0_attn_Wk, L0_attn_Wv,
           L0_attn_Wo, L0_attn_Wkc, L0_attn_Wgslc, L0_attn_Wgswa,
           L0_ffn_W_g, L0_ffn_W_u, L0_ffn_W_d, _dbg_names=(), _cores=8):
    x = np.ascontiguousarray(np.asarray(L0_attn_hidden_states, dtype=np.float32))
    B = build(_dbg_names)
    in_maps = []
    for c in range(_cores):
        b, par = c // 2, c % 2
        rows = ZMAP[par]
        xo = np.ascontiguousarray(
            x[b].reshape(NT, 128, DM)[rows].reshape(NO * 128, DM))
        im = {
            "xseq": np.ascontiguousarray(x[b]),
            "xown": xo,
            "pm": np.full((128, 1), float(par), np.float32),
            "pm1": np.full((128, 1), float(1 - par), np.float32),
            "wq": np.ascontiguousarray(np.asarray(L0_attn_Wq, np.float32)),
            "wk": np.ascontiguousarray(np.asarray(L0_attn_Wk, np.float32)),
            "wv": np.ascontiguousarray(np.asarray(L0_attn_Wv, np.float32)),
            "wo": np.ascontiguousarray(np.asarray(L0_attn_Wo, np.float32)),
            "wgs": np.ascontiguousarray(np.asarray(L0_attn_Wgslc, np.float32)),
            "wgw": np.ascontiguousarray(np.asarray(L0_attn_Wgswa, np.float32)),
            "wg": np.ascontiguousarray(np.asarray(L0_ffn_W_g, np.float32)),
            "wu": np.ascontiguousarray(np.asarray(L0_ffn_W_u, np.float32)),
            "wd": np.ascontiguousarray(np.asarray(L0_ffn_W_d, np.float32)),
        }
        in_maps.append(im)
    res = run_bass_kernel_spmd(B.nc, in_maps, core_ids=list(range(_cores)))
    out = np.empty((4, SEQ, DM), res.results[0]["oown"].dtype)
    for c in range(_cores):
        b, par = c // 2, c % 2
        o = res.results[c]["oown"].reshape(NO, 128, DM)
        for j, g in enumerate(ZMAP[par]):
            out[b, g * 128:(g + 1) * 128, :] = o[j]
    if _dbg_names:
        return out, res
    return out


# version 07
# speedup vs baseline: 1.1840x; 1.1114x over previous
import numpy as np
import ml_dtypes
import concourse.bass as bass
import concourse.mybir as mybir
from concourse.bass_utils import run_bass_kernel_spmd

F32 = mybir.dt.float32
BF16 = mybir.dt.bfloat16
I32 = mybir.dt.int32
AF = mybir.ActivationFunctionType
ALU = mybir.AluOpType
AX = mybir.AxisListType

SEQ, DM, NT, NO = 2048, 2048, 16, 8
H, HKV, DH, FH = 16, 4, 128, 8192
NEG = -3.0e38
ATT_SCALE = float(np.float32(1.0) / np.sqrt(np.float32(DH)))

TWO_PI = 2.0 * np.pi
P2_1 = 6.28125
_r = TWO_PI - P2_1
P2_2 = float(np.float32(int(_r * 2 ** 22) / 2 ** 22))
P2_3 = float(np.float32(_r - P2_2))
INV_2PI = float(np.float32(1.0 / TWO_PI))
PIH = np.pi / 2.0
PIH_HI = float(np.float32(int(PIH * 2 ** 20) / 2 ** 20))
PIH_LO = float(np.float32(PIH - PIH_HI))
L64 = float(np.log(10000.0) / 64.0)
L64_HI = float(np.float32(int(L64 * 2 ** 18) / 2 ** 18))
L64_LO = float(np.float32(L64 - L64_HI))


class Builder:
    def __init__(self, dbg_names=()):
        self.nc = bass.Bass("TRN2", target_bir_lowering=False)
        self.events = []
        self.cnt = {}
        self.dbg_names = set(dbg_names)
        self.dbg_bufs = []
        self._ctxs = []
        self._ivbufs = []

    def sb(self, name, shape, dtype, phases=None, **kw):
        if phases is not None and name in self.dbg_names:
            phases = (phases[0], 7)
        if phases is None:
            cm = self.nc.sbuf_tensor(name, shape, dtype, **kw)
            t = cm.__enter__()
            self._ctxs.append(cm)
            return t
        t = self.nc.virtual_sbuf_tensor(name, list(shape), dtype)
        bpp = int(np.prod(shape[1:])) * mybir.dt.size(dtype)
        self._ivbufs.append((t, list(shape), dtype, bpp, phases[0], phases[1]))
        return t

    def pack(self):
        free0 = (self.nc.sbuf_base + 511) & ~511
        cap = self.nc.sbuf_bytes_remaining - (free0 - self.nc.sbuf_base)
        placed = []
        assign = {}
        for (t, shape, dtype, bpp, lo, hi) in sorted(
                self._ivbufs, key=lambda x: -x[3]):
            size = (bpp + 511) & ~511
            cands = sorted({0} | {p[0] + p[1] for p in placed})
            off = None
            for c in cands:
                if c + size > cap:
                    continue
                ok = True
                for (p0, p1, p2, p3) in placed:
                    if not (hi < p2 or lo > p3):
                        if not (c + size <= p0 or c >= p0 + p1):
                            ok = False
                            break
                if ok:
                    off = c
                    break
            assert off is not None, \
                f"pack failed for {t.name} size {size} (cap {cap})"
            placed.append((off, size, lo, hi))
            assign[t.name] = (off, shape, dtype, t)
        total = max(p[0] + p[1] for p in placed)
        for name, (off, shape, dtype, t) in assign.items():
            bk = self.nc.alloc_sbuf_tensor_at(
                name + "_bk", shape, dtype, offset=free0 + off)
            self.nc.back_virtual_tensor_with(t, bk)
        return total, cap

    def ps(self, name, shape):
        cm = self.nc.psum_tensor(name, shape, F32)
        t = cm.__enter__()
        self._ctxs.append(cm)
        return t

    def sem(self, name):
        cm = self.nc.semaphore(name)
        s = cm.__enter__()
        self._ctxs.append(cm)
        return s

    def init_chains(self):
        self.chsem = {}
        self.chcnt = {}

    def mkchain(self, tag):
        self.chsem[tag] = self.sem("CH_" + tag)
        self.chcnt[tag] = 0

    def ch(self, tag, eng, inst):
        self.chcnt[tag] += 1
        inst.then_inc(self.chsem[tag], 1)
        eng.wait_ge(self.chsem[tag], self.chcnt[tag])
        return inst

    def ev(self, tag, fn, waits=(), sem=None, inc=0):
        self.events.append((tag, fn, tuple((s, v) for (s, v) in waits if v > 0),
                            sem, inc))
        if sem is not None and inc:
            self.cnt[sem] = self.cnt.get(sem, 0) + inc
        return self.cnt.get(sem, 0)

    def val(self, sem):
        return self.cnt.get(sem, 0)

    def dbg(self, name, buf):
        if name in self.dbg_names:
            shape = [buf.shape[0], int(np.prod(buf.shape[1:]))]
            d = self.nc.dram_tensor("dbg_" + name, shape, buf.dtype,
                                    kind="ExternalOutput").ap()
            self.dbg_bufs.append((name, buf, d))

    def emit(self):
        P = self
        with self.nc.Block() as block:
            @block.sync
            def _(s):
                run_events(s, P, "s")

            @block.gpsimd
            def _(g):
                run_events(g, P, "g")

            @block.vector
            def _(v):
                run_events(v, P, "v")

            @block.scalar
            def _(sc):
                run_events(sc, P, "sc")

            @block.tensor
            def _(t):
                run_events(t, P, "t")
        for cm in reversed(self._ctxs):
            cm.__exit__(None, None, None)


def run_events(eng, P, tag):
    for (etag, fn, waits, sem, inc) in P.events:
        if etag != tag:
            continue
        for (s, v) in waits:
            eng.wait_ge(s, v)
        inst = fn(eng)
        if sem is not None and inc and inst is not None:
            inst.then_inc(sem, inc)


def build(dbg_names=()):
    B = Builder(dbg_names)
    nc = B.nc

    d_xseq = nc.dram_tensor("xseq", [SEQ, DM], F32, kind="ExternalInput").ap()
    d_xown = nc.dram_tensor("xown", [NO * 128, DM], F32, kind="ExternalInput").ap()
    d_pm = nc.dram_tensor("pm", [128, 1], F32, kind="ExternalInput").ap()
    d_pm1 = nc.dram_tensor("pm1", [128, 1], F32, kind="ExternalInput").ap()
    d_wq = nc.dram_tensor("wq", [DM, DM], F32, kind="ExternalInput").ap()
    d_wk = nc.dram_tensor("wk", [128, 8192], F32, kind="ExternalInput").ap()
    d_wv = nc.dram_tensor("wv", [128, 8192], F32, kind="ExternalInput").ap()
    d_wo = nc.dram_tensor("wo", [128, 4 * 8192], F32, kind="ExternalInput").ap()
    d_wgs = nc.dram_tensor("wgs", [128, 512], F32, kind="ExternalInput").ap()
    d_wg = nc.dram_tensor("wg", [FH, DM], F32, kind="ExternalInput").ap()
    d_wu = nc.dram_tensor("wu", [FH, DM], F32, kind="ExternalInput").ap()
    d_wd = nc.dram_tensor("wd", [2048, FH], F32, kind="ExternalInput").ap()
    d_out = nc.dram_tensor("oown", [NO * 128, DM], F32, kind="ExternalOutput").ap()

    DXA = [B.sem("DXA0"), B.sem("DXA1")]
    DW = [B.sem("DW0"), B.sem("DW1"), B.sem("DW2"), B.sem("DW3")]
    DF = [B.sem("DF0"), B.sem("DF1")]
    DQ = [B.sem("DQ0"), B.sem("DQ1")]
    DK = [B.sem("DK0"), B.sem("DK1")]
    DP = [B.sem("DP0"), B.sem("DP1")]
    DS = [B.sem("DS0"), B.sem("DS1")]
    DDB = B.sem("DDB")
    B.init_chains()
    for _tag in ("v", "sc", "g"):
        B.mkchain(_tag)
    CV = B.sem("CV")
    CS = B.sem("CS")
    CT = B.sem("CT")
    CG = B.sem("CG")

    psA = B.ps("psA", [128, 2048])
    psB = B.ps("psB", [128, 2048])

    ALLSEMS = DXA + DW + DF + DQ + DK + DP + DS + [CV, CS, CT, CG]

    def barrier(name):
        snap = [(sm, B.val(sm)) for sm in ALLSEMS if B.val(sm) > 0]
        for tag in ("s", "g", "v", "sc", "t"):
            B.ev(tag, (lambda e: None), waits=snap)

    ones_row = B.sb("ones_row", [1, 128], F32)
    ident = B.sb("ident", [128, 128], F32)
    mdiag = B.sb("mdiag", [128, 128], F32)
    mpre = B.sb("mpre", [128, 128], F32)
    mlast = B.sb("mlast", [128, 128], F32)
    pmb = B.sb("pmb", [128, 1], F32)
    pm1b = B.sb("pm1b", [128, 1], F32)
    pm1neg = B.sb("pm1neg", [128, 1], F32)
    wrow = B.sb("wrow", [1, 64], F32)
    omega = B.sb("omega", [128, 64], F32)
    tgi = B.sb("tgi", [128, 64], F32)
    rr1 = B.sb("rr1", [128, 64], F32)
    rr2 = B.sb("rr2", [128, 64], F32)
    rri = B.sb("rri", [128, 64], I32)
    rrk = B.sb("rrk", [128, 64], F32)
    zbQ = B.sb("zbQ", [128, 256], F32)
    sums = B.sb("sums", [128, 24], F32)
    sumsq = B.sb("sumsq", [128, 24], F32)
    mean_ = B.sb("mean_", [128, 24], F32)
    var_ = B.sb("var_", [128, 24], F32)
    rstd = B.sb("rstd", [128, 24], F32)
    nrs = B.sb("nrs", [128, 24], F32)
    bias_ = B.sb("bias_", [128, 24], F32)
    var2 = B.sb("var2", [128, 8], F32)
    rstd2 = B.sb("rstd2", [128, 8], F32)
    n8 = B.sb("n8", [128, 8], F32)
    amax = B.sb("amax", [128, 8], F32)
    zsc = B.sb("zsc", [128, 8], F32)
    ninvs = B.sb("ninvs", [128, 8], F32)
    gtm = B.sb("gtm", [128, NO, H], F32)
    gsig = B.sb("gsig", [128, 32], F32)
    lsum = B.sb("lsum", [128, 1], F32)
    mrow = B.sb("mrow", [128, 2], F32)
    lsum2 = B.sb("lsum2", [128, 2], F32)
    mrow2 = B.sb("mrow2", [128, 6], F32)

    costm = B.sb("costm", [128, NT, 64], F32, phases=(0, 2))
    sintm = B.sb("sintm", [128, NT, 64], F32, phases=(0, 2))
    wkT = B.sb("wkT", [128, 16, 4, 128], BF16, phases=(2, 3))
    wvT = B.sb("wvT", [128, 16, 4, 128], BF16, phases=(2, 3))
    wgT = B.sb("wgT", [128, 16, 32], BF16, phases=(0, 2))
    sqs = B.sb("sqs", [128, DM], F32, phases=(0, 0))
    xsA = [B.sb(f"xsA{i}", [128, DM], F32, phases=(0, 2)) for i in range(2)]
    wqTh = B.sb("wqTh", [128, 16, 16, 128], BF16, phases=(0, 1))
    xnbo = [B.sb(f"xnbo{i}", [128, DM], BF16, phases=(1, 1)) for i in range(2)]
    xfm = [B.sb(f"xfm{i}", [128, NT, 128], BF16, phases=(1, 2)) for i in range(2)]
    qtm = [B.sb(f"qtm{i}", [128, 2048], BF16, phases=(1, 1)) for i in range(2)]
    crep = B.sb("crep", [128, 1024], F32, phases=(0, 2))
    srep = B.sb("srep", [128, 1024], F32, phases=(0, 2))
    zbW = B.sb("zbW", [128, 2048], F32, phases=(0, 2))
    zbW2 = B.sb("zbW2", [128, 1024], F32, phases=(0, 0))
    rriW = B.sb("rriW", [128, 1024], I32, phases=(0, 0))
    qfm = B.sb("qfm", [128, NO, H, 128], BF16, phases=(1, 3))
    xnb = [B.sb(f"xnb{i}", [128, DM], BF16, phases=(2, 2)) for i in range(2)]
    ktm = [B.sb(f"ktm{i}", [128, 512], BF16, phases=(2, 2)) for i in range(2)]
    kfm = B.sb("kfm", [128, HKV, NT, 128], BF16, phases=(2, 3))
    vtm = B.sb("vtm", [128, NT, 512], BF16, phases=(2, 3))
    pexp = [B.sb(f"pexp{i}", [128, 2048], F32, phases=(3, 3)) for i in range(2)]
    pbf = [B.sb(f"pbf{i}", [128, 2048], BF16, phases=(3, 3)) for i in range(2)]
    PT = [B.sb(f"PT{i}", [128, 16, 4, 128], BF16, phases=(3, 3)) for i in range(2)]
    attn = B.sb("attn", [128, 16, NO, 128], BF16, phases=(3, 4))
    oacc = B.sb("oacc", [128, NO, 2048], F32, phases=(4, 7))
    woT = [B.sb(f"woT{i}", [128, 16, 4, 128], BF16, phases=(4, 4))
           for i in range(2)]
    xsB = [B.sb(f"xsB{i}", [128, DM], F32, phases=(5, 5)) for i in range(2)]
    xnf = [B.sb(f"xnf{i}", [128, DM], F32, phases=(5, 5)) for i in range(2)]
    zb = B.sb("zb", [128, DM], F32, phases=(5, 5))
    zi = B.sb("zi", [128, DM], I32, phases=(5, 5))
    zk = B.sb("zk", [128, DM], F32, phases=(5, 5))
    zd = B.sb("zd", [128, DM], F32, phases=(5, 5))
    xqtm = [B.sb(f"xqtm{i}", [128, DM], BF16, phases=(5, 5)) for i in range(2)]
    rowb = B.sb("rowb", [1, 1024], F32, phases=(5, 5))
    xqfm = B.sb("xqfm", [128, 16, NO, 128], BF16, phases=(5, 6))
    nb_full = B.sb("nb_full", [128, 1024], F32, phases=(5, 6))
    pq = B.sb("pq", [128, 16, NO, 128], BF16, phases=(6, 6))
    wgn = [B.sb(f"wgn{i}", [128, 16, 128], BF16, phases=(6, 6)) for i in range(4)]
    wdT = [B.sb(f"wdT{i}", [128, 16, 4, 128], BF16, phases=(6, 6)) for i in range(2)]
    ffa = B.sb("ffa", [128, 1024], F32, phases=(6, 6))
    fft = B.sb("fft", [128, 1024], F32, phases=(6, 6))
    ffu = B.sb("ffu", [128, 1024], F32, phases=(6, 6))
    xsC = [B.sb(f"xsC{i}", [128, DM], F32, phases=(7, 7)) for i in range(2)]
    ost = [B.sb(f"ost{i}", [128, DM], F32, phases=(7, 7)) for i in range(2)]

    def g_const(g):
        g.memset(ones_row[:], 1.0)
        g.memset(mdiag[:], 0.0)
        g.memset(ident[:], 1.0)
        g.drain()
        g.affine_select(mdiag[:], mdiag[:], [[-1, 128]], ALU.is_ge, NEG,
                        base=0, channel_multiplier=1)
        g.affine_select(ident[:], ident[:], [[-1, 128]], ALU.is_equal, 0.0,
                        base=0, channel_multiplier=1)
        g.iota(wrow[:], [[1, 64]], base=0, channel_multiplier=0,
               allow_small_or_imprecise_dtypes=True)
        g.iota(srep[:, 0:1024].rearrange("p (a b) -> p a b", b=64),
               [[128, 16], [0, 64]], base=0, channel_multiplier=1,
               allow_small_or_imprecise_dtypes=True)
        return g.drain()
    cg_const = B.ev("g", g_const, sem=CG, inc=1)

    def s_pm(s):
        s.dma_start(out=pmb[:], in_=d_pm).then_inc(DXA[0], 16)
        s.dma_start(out=pm1b[:], in_=d_pm1).then_inc(DXA[1], 16)
        return None
    B.ev("s", s_pm)
    B.cnt[DXA[0]] = B.cnt.get(DXA[0], 0) + 16
    B.cnt[DXA[1]] = B.cnt.get(DXA[1], 0) + 16

    def v_masks(v):
        v.tensor_scalar_mul(pm1neg[:], pm1b[:], NEG)
        v.tensor_scalar_mul(mpre[:], mdiag[:], pm1b[:, 0:1])
        v.tensor_scalar_mul(mlast[:], mdiag[:], pmb[:, 0:1])
        v.drain()
        v.tensor_scalar_add(mlast[:], mlast[:], pm1neg[:, 0:1])
        v.tensor_scalar_mul(rr1[0:1, :], wrow[:], -L64_HI)
        v.drain()
        v.scalar_tensor_tensor(rr1[0:1, :], wrow[:], -L64_LO, rr1[0:1, :],
                               ALU.mult, ALU.add)
        return v.drain()
    cv_masks = B.ev("v", v_masks,
                    waits=[(CG, cg_const), (DXA[0], 16), (DXA[1], 16)],
                    sem=CV, inc=1)

    def s_omega(s):
        return s.activation(wrow[:], rr1[0:1, :], AF.Exp)
    cs_omega = B.ev("sc", s_omega, waits=[(CV, cv_masks)], sem=CS, inc=1)

    def t_omega(t):
        return t.matmul(psB[:, 0:64], ones_row[:], wrow[:], start=True, stop=True)
    ct_omega = B.ev("t", t_omega, waits=[(CS, cs_omega)], sem=CT, inc=1)

    def v_omega_ev(v):
        return v.tensor_copy(omega[:], psB[:, 0:64])
    cv_omega = B.ev("v", v_omega_ev, waits=[(CT, ct_omega)], sem=CV, inc=1)

    def rr_chain(v, src):
        for op in (
            lambda: v.tensor_scalar(rr2[:], src, INV_2PI, 0.5, ALU.mult, ALU.add),
            lambda: v.tensor_copy(rri[:], rr2[:]),
            lambda: v.tensor_copy(rrk[:], rri[:]),
            lambda: v.tensor_sub(rr2[:], rr2[:], rrk[:]),
            lambda: v.tensor_scalar(rr2[:], rr2[:], 0.0, None, ALU.is_lt),
            lambda: v.tensor_sub(rrk[:], rrk[:], rr2[:]),
            lambda: v.scalar_tensor_tensor(rr2[:], rrk[:], -P2_1, src,
                                           ALU.mult, ALU.add),
            lambda: v.scalar_tensor_tensor(rr2[:], rrk[:], -P2_2, rr2[:],
                                           ALU.mult, ALU.add),
            lambda: v.scalar_tensor_tensor(rr2[:], rrk[:], -P2_3, rr2[:],
                                           ALU.mult, ALU.add),
        ):
            B.ch("v", v, op())

    def v_tables(v):
        i0 = v.tensor_copy(crep[:, 0:64], omega[:])
        B.ch("v", v, i0)
        for w in (64, 128, 256, 512):
            B.ch("v", v, v.tensor_copy(crep[:, w:2 * w], crep[:, 0:w]))
        B.ch("v", v, v.tensor_mul(zbW[:, 0:1024], srep[:, 0:1024], crep[:, 0:1024]))
        return v.engine_nop()
    cv_tab0 = B.ev("v", v_tables, waits=[(CV, cv_omega), (CG, B.val(CG))],
                   sem=CV, inc=1)

    def rr_chain_w(v, src, uW, riW, rkW):
        for op in (
            lambda: v.tensor_scalar(uW, src, INV_2PI, 0.5, ALU.mult, ALU.add),
            lambda: v.tensor_copy(riW, uW),
            lambda: v.tensor_copy(rkW, riW),
            lambda: v.tensor_sub(uW, uW, rkW),
            lambda: v.tensor_scalar(uW, uW, 0.0, None, ALU.is_lt),
            lambda: v.tensor_sub(rkW, rkW, uW),
            lambda: v.scalar_tensor_tensor(uW, rkW, -P2_1, src, ALU.mult, ALU.add),
            lambda: v.scalar_tensor_tensor(uW, rkW, -P2_2, uW, ALU.mult, ALU.add),
            lambda: v.scalar_tensor_tensor(uW, rkW, -P2_3, uW, ALU.mult, ALU.add),
        ):
            B.ch("v", v, op())

    def v_tab_sin(v):
        rr_chain_w(v, zbW[:, 0:1024], zbW[:, 1024:2048], rriW[:], zbW2[:])
        return v.engine_nop()
    cv_tsin = B.ev("v", v_tab_sin, sem=CV, inc=1)

    def s_tab_sin(s):
        return s.activation(sintm[:].rearrange("p a b -> p (a b)"),
                            zbW[:, 1024:2048], AF.Sin)
    cs_tsin = B.ev("sc", s_tab_sin, waits=[(CV, cv_tsin)], sem=CS, inc=1)

    def v_tab_cos(v):
        B.ch("v", v, v.tensor_scalar_add(zbW[:, 0:1024], zbW[:, 0:1024], PIH_HI))
        B.ch("v", v, v.tensor_scalar_add(zbW[:, 0:1024], zbW[:, 0:1024], PIH_LO))
        rr_chain_w(v, zbW[:, 0:1024], zbW[:, 1024:2048], rriW[:], zbW2[:])
        return v.engine_nop()
    cv_tcos = B.ev("v", v_tab_cos, waits=[(CS, cs_tsin)], sem=CV, inc=1)

    def s_tab_cos(s):
        return s.activation(costm[:].rearrange("p a b -> p (a b)"),
                            zbW[:, 1024:2048], AF.Sin)
    B.ev("sc", s_tab_cos, waits=[(CV, cv_tcos)], sem=CS, inc=1)

    B.dbg("sintm", sintm)
    B.dbg("costm", costm)

    wl_prev = [0, 0, 0, 0]

    class WS:
        idx = 0

    def wload(src, dst, sdst=None, waits=(), sign=False):
        slot = WS.idx % 4
        WS.idx += 1

        def g_w(g, src=src, dst=dst, slot=slot, prev=wl_prev[slot],
                waits=tuple(waits)):
            g.wait_ge(DW[slot], prev)
            for (sm, vv) in waits:
                if vv:
                    g.wait_ge(sm, vv)
            return g.dma_start(out=dst, in_=src)
        dw = B.ev("g", g_w, sem=DW[slot], inc=16)
        wl_prev[slot] = dw
        if sign:
            def s_sg(s, d2=(sdst if sdst is not None else dst), slot=slot,
                     dw=dw):
                s.wait_ge(DW[slot], dw)
                return s.activation(d2, d2, AF.Sign)
            return (CS, B.ev("sc", s_sg, sem=CS, inc=1))
        return (DW[slot], dw)

    wload(d_wgs, wgT[:].rearrange("p a b -> p (a b)"))
    for k in range(16):
        wload(d_wq[k * 128:(k + 1) * 128, :],
              wqTh[:, k, :, :].rearrange("p a b -> p (a b)"))
    dw_wq_all = tuple(B.val(x) for x in DW)
    B.dbg("wkT", wkT)

    xsrc = [(d_xseq, t, t) for t in range(NT)] + \
           [(d_xown, t, NT + t) for t in range(NO)]
    xrel = [(0, 0)] * 2
    cs_sq_prev = 0
    for idx, (src, t, col) in enumerate(xsrc):
        slot = idx % 2

        def s_ldx(s, src=src, t=t, slot=slot, rel=xrel[slot],
                  dxp=B.val(DXA[slot])):
            s.wait_ge(DXA[slot], dxp)
            if rel[0]:
                s.wait_ge(CV, rel[0])
            if rel[1]:
                s.wait_ge(CS, rel[1])
            return s.dma_start(out=xsA[slot][:], in_=src[t * 128:(t + 1) * 128, :])
        dx = B.ev("s", s_ldx, sem=DXA[slot], inc=16)

        def v_sum(v, slot=slot, col=col):
            return v.reduce_sum(sums[:, col:col + 1], xsA[slot][:], axis=AX.X)
        cv_h = B.ev("v", v_sum, waits=[(DXA[slot], dx)], sem=CV, inc=1)

        def s_sq(s, slot=slot, col=col, relsq=cs_sq_prev):
            if relsq:
                s.wait_ge(CS, relsq)
            return s.activation(sqs[:], xsA[slot][:], AF.Square,
                                accum_out=sumsq[:, col:col + 1])
        cs_h = B.ev("sc", s_sq, waits=[(DXA[slot], dx)], sem=CS, inc=1)
        cs_sq_prev = cs_h
        xrel[slot] = (cv_h, cs_h)

    def nr_rsqrt(v, r, vv, scratch, iters=6):
        v.memset(r, 1.0)
        v.drain()
        for _ in range(iters):
            v.tensor_mul(scratch, r, r)
            v.drain()
            v.tensor_mul(scratch, scratch, vv)
            v.drain()
            v.tensor_scalar(scratch, scratch, -0.5, 1.5, ALU.mult, ALU.add)
            v.drain()
            v.tensor_mul(r, r, scratch)
            v.drain()

    def v_stats(v):
        v.tensor_scalar_mul(mean_[:], sums[:], 1.0 / DM)
        v.tensor_scalar_mul(var_[:], sumsq[:], 1.0 / DM)
        v.drain()
        v.tensor_mul(nrs[:], mean_[:], mean_[:])
        v.drain()
        v.tensor_sub(var_[:], var_[:], nrs[:])
        v.drain()
        v.tensor_scalar_add(var_[:], var_[:], 1e-5)
        v.drain()
        nr_rsqrt(v, rstd[:], var_[:], nrs[:])
        v.tensor_mul(bias_[:], mean_[:], rstd[:])
        v.drain()
        v.tensor_scalar_mul(bias_[:], bias_[:], -1.0)
        v.tensor_scalar_add(var2[:], var_[:, 16:24], -1e-5)
        v.drain()
        v.tensor_mul(n8[:], rstd[:, 16:24], rstd[:, 16:24])
        v.drain()
        v.tensor_mul(var2[:], var2[:], n8[:])
        v.drain()
        v.tensor_scalar_add(var2[:], var2[:], 1e-8)
        v.drain()
        nr_rsqrt(v, rstd2[:], var2[:], n8[:])
        return v.drain()
    cv_stats = B.ev("v", v_stats,
                    waits=[(CV, B.val(CV)), (CS, B.val(CS))], sem=CV, inc=1)
    B.dbg("rstd", rstd)
    B.dbg("rstd2", rstd2)
    B.dbg("bias_", bias_)

    barrier("pc")
    relC_ct = [0, 0]
    relC_dtq = [0, 0]
    rope_hist = []
    cs_gate_prev = 0
    xrelC = list(xrel)
    cs_xnbo = {}
    xnbo_rel = [0, 0]
    for j in range(NO):
        fslot = j % 2
        slot = j % 2
        cv_rope_m2 = rope_hist[-2] if len(rope_hist) >= 2 else 0

        def s_ldxo(s, j=j, slot=slot, rel=xrelC[slot], dxp=B.val(DXA[slot])):
            s.wait_ge(DXA[slot], dxp)
            if rel[0]:
                s.wait_ge(CV, rel[0])
            if rel[1]:
                s.wait_ge(CS, rel[1])
            return s.dma_start(out=xsA[slot][:],
                               in_=d_xown[j * 128:(j + 1) * 128, :])
        dx = B.ev("s", s_ldxo, sem=DXA[slot], inc=16)

        def s_xn(s, j=j, slot=slot, fslot=fslot, dx=dx,
                 relb=xnbo_rel[fslot]):
            s.wait_ge(DXA[slot], dx)
            if relb:
                s.wait_ge(DF[fslot], relb)
            return s.activation(xnbo[fslot][:], xsA[slot][:], AF.Identity,
                                bias=bias_[:, 16 + j:17 + j],
                                scale=rstd[:, 16 + j:17 + j])
        cs_xn = B.ev("sc", s_xn, waits=[(CV, cv_stats)], sem=CS, inc=1)
        cs_xnbo[j] = cs_xn
        xrelC[slot] = (0, cs_xn)

        def s_tpxc(s, j=j, fslot=fslot, relct=relC_ct[fslot]):
            s.wait_ge(CS, cs_xnbo[j])
            if relct:
                s.wait_ge(CT, relct)
            return s.dma_start(out=xfm[fslot][:, 0:16, :], in_=xnbo[fslot][:],
                               transpose=True)
        dt_x = B.ev("s", s_tpxc, sem=DF[fslot], inc=16)
        xnbo_rel[fslot] = dt_x

        def t_q(t, j=j, fslot=fslot, dt=dt_x, cvprev=cv_rope_m2,
                csg_prev=cs_gate_prev, dtw=dw_wq_all):
            t.wait_ge(DF[fslot], dt)
            for _i in range(4):
                t.wait_ge(DW[_i], dtw[_i])
            if cvprev:
                t.wait_ge(CV, cvprev)
            if csg_prev:
                t.wait_ge(CS, csg_prev)
            mm = None
            for k in range(16):
                lhsT = xfm[fslot][:, k, :]
                for nn in range(4):
                    mm = t.matmul(psA[:, nn * 512:(nn + 1) * 512],
                                  lhsT, wqTh[:, k, nn * 4:(nn + 1) * 4, :],
                                  start=(k == 0), stop=(k == 15))
                mm = t.matmul(psB[:, 1792:1824], lhsT, wgT[:, k, :],
                              start=(k == 0), stop=(k == 15))
            return mm
        ct_q = B.ev("t", t_q, sem=CT, inc=1)
        relC_ct[fslot] = ct_q

        def s_gate(s, j=j, ct=ct_q):
            s.wait_ge(CT, ct)
            return s.activation(gsig[:], psB[:, 1792:1824], AF.Tanh, scale=0.5)
        cs_gate = B.ev("sc", s_gate, sem=CS, inc=1)
        cs_gate_prev = cs_gate

        def v_rope_q(v, j=j, fslot=fslot, ct=ct_q, csg=cs_gate,
                     reldq=relC_dtq[fslot]):
            if reldq:
                v.wait_ge(DQ[fslot], reldq)
            v.wait_ge(CT, ct)
            v.tensor_scalar_mul(crep[:, 0:64], costm[:, 2 * j, :], pm1b[:, 0:1])
            i1 = v.tensor_scalar_mul(srep[:, 0:64], sintm[:, 2 * j, :],
                                     pm1b[:, 0:1])
            B.ch("v", v, i1)
            v.scalar_tensor_tensor(crep[:, 0:64], costm[:, 2 * j + 1, :],
                                   pmb[:, 0:1], crep[:, 0:64], ALU.mult, ALU.add)
            i2 = v.scalar_tensor_tensor(srep[:, 0:64], sintm[:, 2 * j + 1, :],
                                        pmb[:, 0:1], srep[:, 0:64],
                                        ALU.mult, ALU.add)
            B.ch("v", v, i2)
            for w in (64, 128, 256, 512):
                v.tensor_copy(crep[:, w:2 * w], crep[:, 0:w])
                i3 = v.tensor_copy(srep[:, w:2 * w], srep[:, 0:w])
                B.ch("v", v, i3)
            ps3 = psA[:].rearrange("p (h t) -> p h t", t=128)
            x1 = ps3[:, :, 0:64]
            x2 = ps3[:, :, 64:128]
            c3 = crep[:].rearrange("p (h t) -> p h t", t=64)
            s3 = srep[:].rearrange("p (h t) -> p h t", t=64)
            o3 = qtm[fslot][:].rearrange("p (h t) -> p h t", t=128)
            v.tensor_mul(zbW[:, 0:1024], x1, c3)
            i4 = v.tensor_mul(zbW[:, 1024:2048], x2, s3)
            B.ch("v", v, i4)
            i4b = v.tensor_sub(
                o3[:, :, 0:64],
                zbW[:, 0:1024].rearrange("p (h t) -> p h t", t=64),
                zbW[:, 1024:2048].rearrange("p (h t) -> p h t", t=64))
            B.ch("v", v, i4b)
            v.tensor_mul(zbW[:, 0:1024], x2, c3)
            i5 = v.tensor_mul(zbW[:, 1024:2048], x1, s3)
            B.ch("v", v, i5)
            out = v.tensor_add(
                o3[:, :, 64:128],
                zbW[:, 0:1024].rearrange("p (h t) -> p h t", t=64),
                zbW[:, 1024:2048].rearrange("p (h t) -> p h t", t=64))
            v.wait_ge(CS, csg)
            i6 = v.tensor_add(gsig[:, 0:16], gsig[:, 0:16], gsig[:, 16:32])
            B.ch("v", v, i6)
            out = v.tensor_scalar(gtm[:, j, :], gsig[:, 0:16], 0.5, 1.0,
                                  ALU.mult, ALU.add)
            return out
        cv_rope = B.ev("v", v_rope_q, sem=CV, inc=1)
        rope_hist.append(cv_rope)

        def s_tpq(s, j=j, fslot=fslot, cv=cv_rope):
            s.wait_ge(CV, cv)
            return s.dma_start(out=qfm[:, j, :, :], in_=qtm[fslot][:],
                               transpose=True)
        relC_dtq[fslot] = B.ev("s", s_tpq, sem=DQ[fslot], inc=16)

    B.dbg("qfm", qfm)
    B.dbg("gtm", gtm)

    barrier("pb")

    for h in range(2):
        wload(d_wk[:, h * 4096:(h + 1) * 4096],
              wkT[:, 8 * h:8 * h + 8, :, :].rearrange("p a b c -> p (a b c)"))
        wload(d_wv[:, h * 4096:(h + 1) * 4096],
              wvT[:, 8 * h:8 * h + 8, :, :].rearrange("p a b c -> p (a b c)"))
    DW_attw = tuple(B.val(x) for x in DW)
    xrelB = list(xrelC)
    cv_kv_par = [0, 0]
    relB_dtx = [0, 0]
    relB_ctx = list(relC_ct)
    relB_dtk = [0, 0]
    for t_i in range(NT):
        slot = t_i % 2
        fslot = t_i % 2

        def s_ldxs(s, t_i=t_i, slot=slot, rel=xrelB[slot], dxp=B.val(DXA[slot])):
            s.wait_ge(DXA[slot], dxp)
            if rel[0]:
                s.wait_ge(CV, rel[0])
            if rel[1]:
                s.wait_ge(CS, rel[1])
            return s.dma_start(out=xsA[slot][:],
                               in_=d_xseq[t_i * 128:(t_i + 1) * 128, :])
        dx = B.ev("s", s_ldxs, sem=DXA[slot], inc=16)

        def s_xn2(s, t_i=t_i, slot=slot, fslot=fslot, dx=dx, rel=relB_dtx[fslot]):
            s.wait_ge(DXA[slot], dx)
            if rel:
                s.wait_ge(DF[fslot], rel)
            return s.activation(xnb[fslot][:], xsA[slot][:], AF.Identity,
                                bias=bias_[:, t_i:t_i + 1],
                                scale=rstd[:, t_i:t_i + 1])
        cs_xn = B.ev("sc", s_xn2, waits=[(CV, cv_stats)], sem=CS, inc=1)
        xrelB[slot] = (0, cs_xn)

        def s_tpx2(s, fslot=fslot, cs=cs_xn, relct=relB_ctx[fslot]):
            s.wait_ge(CS, cs)
            if relct:
                s.wait_ge(CT, relct)
            return s.dma_start(out=xfm[fslot][:], in_=xnb[fslot][:], transpose=True)
        dt_x = B.ev("s", s_tpx2, sem=DF[fslot], inc=16)
        relB_dtx[fslot] = dt_x

        def t_kv(t, t_i=t_i, fslot=fslot, dt=dt_x, cvprev=cv_kv_par[t_i % 2]):
            t.wait_ge(DF[fslot], dt)
            for _i in range(4):
                t.wait_ge(DW[_i], DW_attw[_i])
            if cvprev:
                t.wait_ge(CV, cvprev)
            off = 0 if (t_i % 2 == 0) else 1024
            mm = None
            for k in range(16):
                lhsT = xfm[fslot][:, k, :]
                t.matmul(psB[:, off:off + 512], lhsT, wkT[:, k, :, :],
                         start=(k == 0), stop=(k == 15))
                mm = t.matmul(psB[:, off + 512:off + 1024], lhsT, wvT[:, k, :, :],
                              start=(k == 0), stop=(k == 15))
            return mm
        ct_kv = B.ev("t", t_kv, sem=CT, inc=1)
        relB_ctx[fslot] = ct_kv

        def v_kv(v, t_i=t_i, fslot=fslot, ct=ct_kv, rel=relB_dtk[fslot]):
            v.wait_ge(CT, ct)
            if rel:
                v.wait_ge(DK[fslot], rel)
            off = 0 if (t_i % 2 == 0) else 1024
            v.tensor_copy(crep[:, 0:64], costm[:, t_i, :])
            i1 = v.tensor_copy(srep[:, 0:64], sintm[:, t_i, :])
            B.ch("v", v, i1)
            for w in (64, 128):
                v.tensor_copy(crep[:, w:2 * w], crep[:, 0:w])
                i2 = v.tensor_copy(srep[:, w:2 * w], srep[:, 0:w])
                B.ch("v", v, i2)
            ps3 = psB[:, off:off + 512].rearrange("p (h t) -> p h t", t=128)
            x1 = ps3[:, :, 0:64]
            x2 = ps3[:, :, 64:128]
            c3 = crep[:, 0:256].rearrange("p (h t) -> p h t", t=64)
            s3 = srep[:, 0:256].rearrange("p (h t) -> p h t", t=64)
            o3 = ktm[fslot][:].rearrange("p (h t) -> p h t", t=128)
            v.tensor_mul(zbW[:, 0:256], x1, c3)
            v.tensor_mul(zbW[:, 512:768], x2, s3)
            v.tensor_mul(zbW[:, 1024:1280], x2, c3)
            i3 = v.tensor_mul(zbW[:, 1536:1792], x1, s3)
            B.ch("v", v, i3)
            v.tensor_sub(o3[:, :, 0:64],
                         zbW[:, 0:256].rearrange("p (h t) -> p h t", t=64),
                         zbW[:, 512:768].rearrange("p (h t) -> p h t", t=64))
            v.tensor_add(o3[:, :, 64:128],
                         zbW[:, 1024:1280].rearrange("p (h t) -> p h t", t=64),
                         zbW[:, 1536:1792].rearrange("p (h t) -> p h t", t=64))
            return v.tensor_copy(vtm[:, t_i, :], psB[:, off + 512:off + 1024])
        cv_kv = B.ev("v", v_kv, sem=CV, inc=1)
        cv_kv_par[t_i % 2] = cv_kv

        def s_tpk(s, t_i=t_i, fslot=fslot, cv=cv_kv):
            s.wait_ge(CV, cv)
            return s.dma_start(out=kfm[:, :, t_i, :], in_=ktm[fslot][:],
                               transpose=True)
        relB_dtk[fslot] = B.ev("s", s_tpk, sem=DK[fslot], inc=16)

    B.dbg("kfm", kfm)
    B.dbg("vtm", vtm)
    DKQ_all = [(DK[0], B.val(DK[0])), (DK[1], B.val(DK[1])),
               (DQ[0], B.val(DQ[0])), (DQ[1], B.val(DQ[1]))]
    CV_kv_all = B.val(CV)

    barrier("pd")

    cs_exp_prev = 0
    cs_exp_byreg = [0, 0]
    cv_scale_prev = 0
    cv_sc_slot = [0, 0]
    dt_p_slot = [0, 0]
    pt_rel = [0, 0]
    dt_p = {}
    cv_pv_prev = 0
    for j in range(NO):
        nch = 2 * j + 2
        keys = 128 * nch
        for h in range(H):
            g = h // HKV
            hh = h % HKV
            pslot = h % 2
            tslot = (j * 4 + g) % 2
            reg = 1 if (h % 2 == 1 and j <= 5) else 0
            Sps = (psA[:, 0:keys] if reg == 0 else psB[:, 512:512 + keys])

            def t_S(t, j=j, h=h, g=g, nch=nch, Sps=Sps,
                    cse=cs_exp_byreg[reg]):
                if cse:
                    t.wait_ge(CS, cse)
                lhsT = qfm[:, j, h, :]
                mm = None
                for c0 in range(0, nch, 4):
                    cw = min(4, nch - c0)
                    mm = t.matmul(Sps[:, c0 * 128:(c0 + cw) * 128], lhsT,
                                  kfm[:, g, c0:c0 + cw, :], start=True, stop=True)
                return mm
            ct_S = B.ev("t", t_S, waits=DKQ_all + [(CV, CV_kv_all)],
                        sem=CT, inc=1)

            def v_msk(v, j=j, nch=nch, keys=keys, Sps=Sps, ct=ct_S, pc=pslot):
                v.wait_ge(CT, ct)
                v.tensor_add(Sps[:, (nch - 2) * 128:(nch - 1) * 128],
                             Sps[:, (nch - 2) * 128:(nch - 1) * 128], mpre[:])
                i1 = v.tensor_add(Sps[:, (nch - 1) * 128:nch * 128],
                                  Sps[:, (nch - 1) * 128:nch * 128], mlast[:])
                B.ch("v", v, i1)
                i2 = v.reduce_max(mrow2[:, pc:pc + 1], Sps[:, 0:keys], axis=AX.X)
                B.ch("v", v, i2)
                return v.tensor_scalar_mul(mrow2[:, 2 + pc:3 + pc],
                                           mrow2[:, pc:pc + 1], -ATT_SCALE)
            cv_m = B.ev("v", v_msk, sem=CV, inc=1)

            def s_exp(s, keys=keys, pslot=pslot, cv=cv_m, Sps=Sps,
                      rel=cv_sc_slot[pslot]):
                s.wait_ge(CV, cv)
                if rel:
                    s.wait_ge(CV, rel)
                return s.activation(pexp[pslot][:, 0:keys], Sps[:, 0:keys],
                                    AF.Exp, bias=mrow2[:, 2 + pslot:3 + pslot],
                                    scale=ATT_SCALE,
                                    accum_out=lsum2[:, pslot:pslot + 1])
            cs_exp_prev = B.ev("sc", s_exp, sem=CS, inc=1)
            cs_exp_byreg[reg] = cs_exp_prev

            def v_scale(v, j=j, h=h, keys=keys, pslot=pslot, cs=cs_exp_prev,
                        reldt=dt_p_slot[pslot]):
                v.wait_ge(CS, cs)
                if reldt:
                    v.wait_ge(DP[pslot], reldt)
                i1 = v.reciprocal(mrow2[:, 4 + pslot:5 + pslot],
                                  lsum2[:, pslot:pslot + 1])
                B.ch("v", v, i1)
                i2 = v.tensor_mul(mrow2[:, 4 + pslot:5 + pslot],
                                  mrow2[:, 4 + pslot:5 + pslot],
                                  gtm[:, j, h:h + 1])
                B.ch("v", v, i2)
                return v.tensor_scalar_mul(pbf[pslot][:, 0:keys],
                                           pexp[pslot][:, 0:keys],
                                           mrow2[:, 4 + pslot:5 + pslot])
            cv_sc = B.ev("v", v_scale, sem=CV, inc=1)
            cv_scale_prev = cv_sc
            cv_sc_slot[pslot] = cv_sc

            def s_tpp(s, nch=nch, keys=keys, hh=hh, pslot=pslot, tslot=tslot,
                      cv=cv_sc, relct=pt_rel[tslot]):
                s.wait_ge(CV, cv)
                if relct:
                    s.wait_ge(CT, relct)
                return s.dma_start(out=PT[tslot][:, 0:nch, hh, :],
                                   in_=pbf[pslot][:, 0:keys], transpose=True)
            dt_p[(j, h)] = B.ev("s", s_tpp, sem=DP[pslot], inc=16)
            dt_p_slot[pslot] = dt_p[(j, h)]

            if hh == HKV - 1:
                def t_PV(t, j=j, g=g, nch=nch, tslot=tslot,
                         dtp0=dt_p[(j, h - 1)], dtp1=dt_p[(j, h)],
                         cvp=cv_pv_prev):
                    t.wait_ge(DP[0], dtp0)
                    t.wait_ge(DP[1], dtp1)
                    if cvp:
                        t.wait_ge(CV, cvp)
                    mm = None
                    for c in range(nch):
                        mm = t.matmul(psB[:, 0:512],
                                      vtm[:, c, g * 128:(g + 1) * 128],
                                      PT[tslot][:, c, :, :],
                                      start=(c == 0), stop=(c == nch - 1))
                    return mm
                ct_pv = B.ev("t", t_PV, waits=[(CV, CV_kv_all)], sem=CT, inc=1)
                pt_rel[tslot] = ct_pv

                def v_att(v, j=j, g=g, ct=ct_pv):
                    v.wait_ge(CT, ct)
                    return v.tensor_copy(attn[:, 4 * g:4 * g + 4, j, :],
                                         psB[:, 0:512])
                cv_pv_prev = B.ev("v", v_att, sem=CV, inc=1)

    B.dbg("attn", attn)
    CV_attn_done = B.val(CV)

    barrier("pe0")

    def psm(m):
        src = psA if m < 4 else psB
        return src[:, (m % 4) * 512:(m % 4 + 1) * 512]

    cv_evac_prev = 0
    ct_o = 0
    woT_rel = [0, 0]
    for n in range(4):
        oslot = n % 2
        dwo = wload(d_wo[:, n * 8192:(n + 1) * 8192],
                    woT[oslot][:].rearrange("p a b c -> p (a b c)"),
                    waits=[(CT, woT_rel[oslot])])

        for mg in range(2):
            def t_wo(t, n=n, mg=mg, oslot=oslot, dt=dwo, cvp=cv_evac_prev):
                t.wait_ge(dt[0], dt[1])
                if cvp:
                    t.wait_ge(CV, cvp)
                mm = None
                for m in range(4 * mg, 4 * mg + 4):
                    for k in range(16):
                        mm = t.matmul(psm(m), attn[:, k, m, :],
                                      woT[oslot][:, k, :, :],
                                      start=(k == 0), stop=(k == 15))
                return mm
            ct_o = B.ev("t", t_wo, waits=[(CV, CV_attn_done), (CS, B.val(CS))],
                        sem=CT, inc=1)
            woT_rel[oslot] = ct_o

            def v_evo(v, n=n, mg=mg, ct=ct_o):
                v.wait_ge(CT, ct)
                mm = None
                for m in range(4 * mg, 4 * mg + 4):
                    mm = v.tensor_copy(oacc[:, m, n * 512:(n + 1) * 512], psm(m))
                return mm
            cv_evac_prev = B.ev("v", v_evo, sem=CV, inc=1)

    barrier("p45")

    xrelQ = [(cv_evac_prev, B.val(CS))] * 2
    relQ_dt = [B.val(DQ[0]), B.val(DQ[1])]
    xnf_rel = [0, 0]
    cv_q_prev = 0
    for j in range(NO):
        slot = j % 2
        fslot = j % 2

        def s_ldxq(s, j=j, slot=slot, rel=xrelQ[slot], dxp=B.val(DXA[slot])):
            s.wait_ge(DXA[slot], dxp)
            if rel[0]:
                s.wait_ge(CV, rel[0])
            if rel[1]:
                s.wait_ge(CS, rel[1])
            return s.dma_start(out=xsB[slot][:], in_=d_xown[j * 128:(j + 1) * 128, :])
        dx = B.ev("s", s_ldxq, sem=DXA[slot], inc=16)

        def s_xn3(s, j=j, slot=slot, fslot=fslot, dx=dx, rel=xnf_rel[fslot]):
            s.wait_ge(DXA[slot], dx)
            if rel:
                s.wait_ge(CV, rel)
            return s.activation(xnf[fslot][:], xsB[slot][:], AF.Identity,
                                bias=bias_[:, 16 + j:17 + j],
                                scale=rstd[:, 16 + j:17 + j])
        cs_xn = B.ev("sc", s_xn3, sem=CS, inc=1)
        xrelQ[slot] = (0, cs_xn)

        def v_quant(v, j=j, fslot=fslot, cs=cs_xn, reldt=relQ_dt[fslot]):
            v.wait_ge(CS, cs)
            if reldt:
                v.wait_ge(DQ[fslot], reldt)
            xn = xnf[fslot][:]
            i1 = v.reduce_max(amax[:, j:j + 1], xn, axis=AX.X,
                              apply_absolute_value=True)
            B.ch("v", v, i1)
            i2 = v.tensor_mul(amax[:, j:j + 1], amax[:, j:j + 1],
                              rstd2[:, j:j + 1])
            B.ch("v", v, i2)
            i3 = v.reciprocal(n8[:, j:j + 1], amax[:, j:j + 1])
            B.ch("v", v, i3)
            i4 = v.tensor_scalar_mul(n8[:, j:j + 1], n8[:, j:j + 1], 127.0)
            B.ch("v", v, i4)
            v.reciprocal(ninvs[:, j:j + 1], n8[:, j:j + 1])
            i5 = v.tensor_mul(zsc[:, j:j + 1], rstd2[:, j:j + 1], n8[:, j:j + 1])
            B.ch("v", v, i5)
            v.tensor_scalar_mul(ninvs[:, j:j + 1], ninvs[:, j:j + 1], -1.0)
            i6 = v.tensor_scalar_mul(zb[:], xn, zsc[:, j:j + 1])
            B.ch("v", v, i6)
            i7 = v.tensor_copy(zi[:], zb[:])
            B.ch("v", v, i7)
            return v.tensor_copy(zk[:], zi[:])
        cv_q1 = B.ev("v", v_quant, sem=CV, inc=1)
        xnf_rel[fslot] = cv_q1

        def g_quant(g, cv=cv_q1):
            g.wait_ge(CV, cv)
            return g.tensor_sub(zd[:], zb[:], zk[:])
        cg_q = B.ev("g", g_quant, sem=CG, inc=1)

        def v_quant2(v, j=j, fslot=fslot, cg=cg_q):
            v.wait_ge(CG, cg)
            i1 = v.scalar_tensor_tensor(zk[:], zd[:], 0.5, zk[:],
                                        ALU.is_ge, ALU.add)
            B.ch("v", v, i1)
            return v.scalar_tensor_tensor(xqtm[fslot][:], zd[:], -0.5, zk[:],
                                          ALU.is_lt, ALU.subtract)
        cv_q_prev = B.ev("v", v_quant2, sem=CV, inc=1)

        def s_tpxq(s, j=j, fslot=fslot, cv=cv_q_prev):
            s.wait_ge(CV, cv)
            return s.dma_start(out=xqfm[:, :, j, :], in_=xqtm[fslot][:],
                               transpose=True)
        relQ_dt[fslot] = B.ev("s", s_tpxq, sem=DQ[fslot], inc=16)
    B.dbg("xqfm", xqfm)
    B.dbg("ninvs", ninvs)
    B.dbg("zsc", zsc)
    cv_q_all = cv_q_prev

    def t_invT(t):
        mm = None
        for m in range(8):
            mm = t.transpose(psB[0:1, m * 128:(m + 1) * 128], ninvs[:, m:m + 1],
                             ident[:])
        return mm
    ct_it = B.ev("t", t_invT, waits=[(CV, max(cv_q_all, cv_evac_prev))],
                 sem=CT, inc=1)

    def v_row(v):
        v.wait_ge(CT, ct_it)
        return v.tensor_copy(rowb[:], psB[0:1, 0:1024])
    cv_row = B.ev("v", v_row, sem=CV, inc=1)

    def t_bc(t):
        t.wait_ge(CV, cv_row)
        t.matmul(psA[:, 0:512], ones_row[:], rowb[0:1, 0:512], start=True, stop=True)
        return t.matmul(psA[:, 512:1024], ones_row[:], rowb[0:1, 512:1024],
                        start=True, stop=True)
    ct_bc = B.ev("t", t_bc, sem=CT, inc=1)

    def v_bc(v):
        v.wait_ge(CT, ct_bc)
        return v.tensor_copy(nb_full[:], psA[:, 0:1024])
    cv_bc = B.ev("v", v_bc, sem=CV, inc=1)
    B.dbg("nb_full", nb_full)

    barrier("pe")

    cs_tanh_prev = 0
    cv_up_evac = 0
    up_evac_par = [0, 0]
    cv_dn_evac = cv_bc
    wgn_rel = [B.val(CT)] * 4
    for qt in range(4):
        for m in range(16):
            mh = qt * 16 + m
            for (wi, dsrc) in ((0, d_wg), (1, d_wu)):
                rslot = (mh * 2 + wi) % 4
                dt_w = wload(
                    dsrc[mh * 128:(mh + 1) * 128, :],
                    wgn[rslot][:].rearrange("p a b -> p (a b)"),
                    waits=[(CT, wgn_rel[rslot])], sign=True)

                upps = psA if m % 2 == 0 else psB

                def t_up(t, wi=wi, rslot=rslot, dt=dt_w, upps=upps,
                         cvp=max(up_evac_par[m % 2], cv_dn_evac)):
                    t.wait_ge(dt[0], dt[1])
                    if cvp:
                        t.wait_ge(CV, cvp)
                    off = 0 if wi == 0 else 1024
                    mm = None
                    for k in range(16):
                        for nn2 in range(2):
                            mm = t.matmul(
                                upps[:, off + nn2 * 512:off + (nn2 + 1) * 512],
                                wgn[rslot][:, k, :],
                                xqfm[:, k, nn2 * 4:(nn2 + 1) * 4, :],
                                start=(k == 0), stop=(k == 15))
                    return mm
                wgn_rel[rslot] = B.ev("t", t_up, sem=CT, inc=1)
            ct_up = wgn_rel[(mh * 2 + 1) % 4]

            def v_eps1(v, ct=ct_up, upps=upps, csrel=cs_tanh_prev,
                       cvrel=cv_up_evac):
                v.wait_ge(CT, ct)
                if csrel:
                    v.wait_ge(CS, csrel)
                if cvrel:
                    v.wait_ge(CV, cvrel)
                v.tensor_mul(ffa[:], upps[:, 0:1024], nb_full[:])
                return v.tensor_mul(ffu[:], upps[:, 1024:2048], nb_full[:])
            cv_e1 = B.ev("v", v_eps1, waits=[(CV, cv_bc)], sem=CV, inc=1)

            def s_tanh2(s, cv=cv_e1, cvrel=cv_up_evac):
                s.wait_ge(CV, cv)
                if cvrel:
                    s.wait_ge(CV, cvrel)
                return s.activation(fft[:], ffa[:], AF.Tanh, scale=0.5)
            cs_tanh_prev = B.ev("sc", s_tanh2, sem=CS, inc=1)

            def v_eps2(v, m=m, cs=cs_tanh_prev):
                v.wait_ge(CS, cs)
                v.tensor_mul(ffu[:], ffa[:], ffu[:])
                i1 = v.tensor_scalar(fft[:], fft[:], 0.5, 0.5, ALU.mult, ALU.add)
                B.ch("v", v, i1)
                return v.tensor_mul(pq[:, m, :, :], fft[:], ffu[:])
            cv_up_evac = B.ev("v", v_eps2, sem=CV, inc=1)
            up_evac_par[m % 2] = cv_up_evac
        cv_p_done = cv_up_evac

        wdT_rel = [B.val(CT)] * 2
        for n in range(4):
            dslot = n % 2
            sl = (qt * 4 + n) * 128
            dts = []
            for hh2 in range(2):
                dts.append(wload(
                    d_wd[sl:sl + 128, hh2 * 4096:(hh2 + 1) * 4096],
                    wdT[dslot][:, 8 * hh2:8 * hh2 + 8, :, :].rearrange(
                        "p a b c -> p (a b c)"),
                    waits=[(CT, wdT_rel[dslot])], sign=True))

            for mg in range(2):
                def t_dn(t, qt=qt, n=n, dslot=dslot, mg=mg, dt=tuple(dts),
                         cvpd=cv_p_done, cvev=cv_dn_evac):
                    for (dsem, dval) in dt:
                        t.wait_ge(dsem, dval)
                    t.wait_ge(CV, max(cvpd, cvev))
                    mm = None
                    for m in range(4 * mg, 4 * mg + 4):
                        for k in range(16):
                            mm = t.matmul(psm(m), pq[:, k, m, :],
                                          wdT[dslot][:, k, :, :],
                                          start=(k == 0), stop=(k == 15))
                    return mm
                ct_dn = B.ev("t", t_dn, sem=CT, inc=1)
                if mg == 1:
                    wdT_rel[dslot] = ct_dn

                def v_evd(v, n=n, mg=mg, ct=ct_dn):
                    v.wait_ge(CT, ct)
                    mm = None
                    for m in range(4 * mg, 4 * mg + 4):
                        mm = v.tensor_add(oacc[:, m, n * 512:(n + 1) * 512],
                                          oacc[:, m, n * 512:(n + 1) * 512],
                                          psm(m))
                    return mm
                cv_dn_evac = B.ev("v", v_evd, sem=CV, inc=1)

    barrier("fin")

    cs_all_pre_final = B.val(CS)
    xrelF = [(cv_dn_evac, cs_all_pre_final)] * 2
    ost_rel = [0, 0]
    for m in range(NO):
        slot = m % 2
        oslot = m % 2

        def s_ldxf(s, m=m, slot=slot, rel=xrelF[slot], dxp=B.val(DXA[slot])):
            s.wait_ge(DXA[slot], dxp)
            if rel[0]:
                s.wait_ge(CV, rel[0])
            if rel[1]:
                s.wait_ge(CS, rel[1])
            return s.dma_start(out=xsC[slot][:], in_=d_xown[m * 128:(m + 1) * 128, :])
        dx = B.ev("s", s_ldxf, sem=DXA[slot], inc=16)

        def v_fin(v, m=m, slot=slot, oslot=oslot, dx=dx, rel=ost_rel[oslot]):
            v.wait_ge(DXA[slot], dx)
            if rel:
                v.wait_ge(DS[oslot], rel)
            return v.tensor_add(ost[oslot][:], oacc[:, m, :], xsC[slot][:])
        cv_f = B.ev("v", v_fin, waits=[(CV, cv_dn_evac)], sem=CV, inc=1)
        xrelF[slot] = (cv_f, 0)

        def s_st(s, m=m, oslot=oslot, cv=cv_f):
            s.wait_ge(CV, cv)
            return s.dma_start(out=d_out[m * 128:(m + 1) * 128, :], in_=ost[oslot][:])
        ost_rel[oslot] = B.ev("s", s_st, sem=DS[oslot], inc=16)

    ds_fin = (B.val(DS[0]), B.val(DS[1]))
    if B.dbg_bufs:
        allw = [(CV, B.val(CV)), (CS, B.val(CS)), (CT, B.val(CT)),
                (CG, B.val(CG)), (DS[0], ds_fin[0]), (DS[1], ds_fin[1])]

        def s_dbg(s):
            for (name, buf, d) in B.dbg_bufs:
                s.dma_start(out=d, in_=buf[:]).then_inc(DDB, 16)
            return None
        B.ev("s", s_dbg, waits=allw)
        B.cnt[DDB] = B.cnt.get(DDB, 0) + 16 * len(B.dbg_bufs)

    def s_end(s, v0=ds_fin[0], v1=ds_fin[1], vd=B.val(DDB)):
        s.wait_ge(DS[0], v0)
        s.wait_ge(DS[1], v1)
        if vd:
            s.wait_ge(DDB, vd)
        return None
    B.ev("s", s_end)

    total, remain = B.pack()
    B.pack_info = (total, remain)
    B.emit()
    return B


ZMAP = [[2 * j for j in range(NO)], [2 * j + 1 for j in range(NO)]]


def stage_weights(wq, wk, wv, wo, wgs, wgw, wg, wu, wd):
    f = np.float32
    return {
        "wq": np.ascontiguousarray(np.asarray(wq, f).T),
        "wk": np.ascontiguousarray(
            np.asarray(wk, f).T.reshape(16, 128, 4, 128)
            .transpose(1, 0, 2, 3).reshape(128, 8192)),
        "wv": np.ascontiguousarray(
            np.asarray(wv, f).T.reshape(16, 128, 4, 128)
            .transpose(1, 0, 2, 3).reshape(128, 8192)),
        "wo": np.ascontiguousarray(
            np.asarray(wo, f).T.reshape(16, 128, 4, 4, 128)
            .transpose(1, 2, 0, 3, 4).reshape(128, 32768)),
        "wgs": np.ascontiguousarray(
            np.concatenate([np.asarray(wgs, f).T, np.asarray(wgw, f).T],
                           axis=1)
            .reshape(16, 128, 32).transpose(1, 0, 2).reshape(128, 512)),
        "wg": np.ascontiguousarray(
            np.asarray(wg, f).T.reshape(16, 128, 64, 128)
            .transpose(2, 1, 0, 3).reshape(8192, 2048)),
        "wu": np.ascontiguousarray(
            np.asarray(wu, f).T.reshape(16, 128, 64, 128)
            .transpose(2, 1, 0, 3).reshape(8192, 2048)),
        "wd": np.ascontiguousarray(
            np.asarray(wd, f).T.reshape(4, 16, 128, 4, 4, 128)
            .transpose(0, 3, 2, 1, 4, 5).reshape(2048, 8192)),
    }


def kernel(L0_attn_hidden_states, L0_attn_Wq, L0_attn_Wk, L0_attn_Wv,
           L0_attn_Wo, L0_attn_Wkc, L0_attn_Wgslc, L0_attn_Wgswa,
           L0_ffn_W_g, L0_ffn_W_u, L0_ffn_W_d, _dbg_names=(), _cores=8):
    x = np.ascontiguousarray(np.asarray(L0_attn_hidden_states, dtype=np.float32))
    B = build(_dbg_names)
    wmap = stage_weights(L0_attn_Wq, L0_attn_Wk, L0_attn_Wv, L0_attn_Wo,
                         L0_attn_Wgslc, L0_attn_Wgswa, L0_ffn_W_g,
                         L0_ffn_W_u, L0_ffn_W_d)
    in_maps = []
    for c in range(_cores):
        b, par = c // 2, c % 2
        rows = ZMAP[par]
        xo = np.ascontiguousarray(
            x[b].reshape(NT, 128, DM)[rows].reshape(NO * 128, DM))
        im = {
            "xseq": np.ascontiguousarray(x[b]),
            "xown": xo,
            "pm": np.full((128, 1), float(par), np.float32),
            "pm1": np.full((128, 1), float(1 - par), np.float32),
            **wmap,
        }
        in_maps.append(im)
    res = run_bass_kernel_spmd(B.nc, in_maps, core_ids=list(range(_cores)))
    out = np.empty((4, SEQ, DM), res.results[0]["oown"].dtype)
    for c in range(_cores):
        b, par = c // 2, c % 2
        o = res.results[c]["oown"].reshape(NO, 128, DM)
        for j, g in enumerate(ZMAP[par]):
            out[b, g * 128:(g + 1) * 128, :] = o[j]
    if _dbg_names:
        return out, res
    return out


# version 14
# speedup vs baseline: 1.1939x; 1.0083x over previous
import numpy as np
import ml_dtypes
import concourse.bass as bass
import concourse.mybir as mybir
from concourse.bass_utils import run_bass_kernel_spmd

F32 = mybir.dt.float32
BF16 = mybir.dt.bfloat16
I32 = mybir.dt.int32
AF = mybir.ActivationFunctionType
ALU = mybir.AluOpType
AX = mybir.AxisListType

SEQ, DM, NT, NO = 2048, 2048, 16, 8
PAIR_TP = True
MASK_MM = True
H, HKV, DH, FH = 16, 4, 128, 8192
NEG = -3.0e38
ATT_SCALE = float(np.float32(1.0) / np.sqrt(np.float32(DH)))

TWO_PI = 2.0 * np.pi
P2_1 = 6.28125
_r = TWO_PI - P2_1
P2_2 = float(np.float32(int(_r * 2 ** 22) / 2 ** 22))
P2_3 = float(np.float32(_r - P2_2))
INV_2PI = float(np.float32(1.0 / TWO_PI))
PIH = np.pi / 2.0
PIH_HI = float(np.float32(int(PIH * 2 ** 20) / 2 ** 20))
PIH_LO = float(np.float32(PIH - PIH_HI))
L64 = float(np.log(10000.0) / 64.0)
L64_HI = float(np.float32(int(L64 * 2 ** 18) / 2 ** 18))
L64_LO = float(np.float32(L64 - L64_HI))


class Builder:
    def __init__(self, dbg_names=()):
        self.nc = bass.Bass("TRN2", target_bir_lowering=False)
        self.events = []
        self.cnt = {}
        self.dbg_names = set(dbg_names)
        self.dbg_bufs = []
        self._ctxs = []
        self._ivbufs = []

    def sb(self, name, shape, dtype, phases=None, **kw):
        if phases is not None and name in self.dbg_names:
            phases = (phases[0], 7)
        if phases is None:
            cm = self.nc.sbuf_tensor(name, shape, dtype, **kw)
            t = cm.__enter__()
            self._ctxs.append(cm)
            return t
        t = self.nc.virtual_sbuf_tensor(name, list(shape), dtype)
        bpp = int(np.prod(shape[1:])) * mybir.dt.size(dtype)
        self._ivbufs.append((t, list(shape), dtype, bpp, phases[0], phases[1]))
        return t

    def pack(self):
        free0 = (self.nc.sbuf_base + 511) & ~511
        cap = self.nc.sbuf_bytes_remaining - (free0 - self.nc.sbuf_base)
        placed = []
        assign = {}
        for (t, shape, dtype, bpp, lo, hi) in sorted(
                self._ivbufs, key=lambda x: -x[3]):
            size = (bpp + 511) & ~511
            cands = sorted({0} | {p[0] + p[1] for p in placed})
            off = None
            for c in cands:
                if c + size > cap:
                    continue
                ok = True
                for (p0, p1, p2, p3) in placed:
                    if not (hi < p2 or lo > p3):
                        if not (c + size <= p0 or c >= p0 + p1):
                            ok = False
                            break
                if ok:
                    off = c
                    break
            assert off is not None, \
                f"pack failed for {t.name} size {size} (cap {cap})"
            placed.append((off, size, lo, hi))
            assign[t.name] = (off, shape, dtype, t)
        total = max(p[0] + p[1] for p in placed)
        for name, (off, shape, dtype, t) in assign.items():
            bk = self.nc.alloc_sbuf_tensor_at(
                name + "_bk", shape, dtype, offset=free0 + off)
            self.nc.back_virtual_tensor_with(t, bk)
        return total, cap

    def ps(self, name, shape):
        cm = self.nc.psum_tensor(name, shape, F32)
        t = cm.__enter__()
        self._ctxs.append(cm)
        return t

    def sem(self, name):
        cm = self.nc.semaphore(name)
        s = cm.__enter__()
        self._ctxs.append(cm)
        return s

    def init_chains(self):
        self.chsem = {}
        self.chcnt = {}

    def mkchain(self, tag):
        self.chsem[tag] = self.sem("CH_" + tag)
        self.chcnt[tag] = 0

    def ch(self, tag, eng, inst):
        self.chcnt[tag] += 1
        inst.then_inc(self.chsem[tag], 1)
        eng.wait_ge(self.chsem[tag], self.chcnt[tag])
        return inst

    def ev(self, tag, fn, waits=(), sem=None, inc=0):
        self.events.append((tag, fn, tuple((s, v) for (s, v) in waits if v > 0),
                            sem, inc))
        if sem is not None and inc:
            self.cnt[sem] = self.cnt.get(sem, 0) + inc
        return self.cnt.get(sem, 0)

    def val(self, sem):
        return self.cnt.get(sem, 0)

    def dbg(self, name, buf):
        if name in self.dbg_names:
            shape = [buf.shape[0], int(np.prod(buf.shape[1:]))]
            d = self.nc.dram_tensor("dbg_" + name, shape, buf.dtype,
                                    kind="ExternalOutput").ap()
            self.dbg_bufs.append((name, buf, d))

    def emit(self):
        P = self
        with self.nc.Block() as block:
            @block.sync
            def _(s):
                run_events(s, P, "s")

            @block.gpsimd
            def _(g):
                run_events(g, P, "g")

            @block.vector
            def _(v):
                run_events(v, P, "v")

            @block.scalar
            def _(sc):
                run_events(sc, P, "sc")

            @block.tensor
            def _(t):
                run_events(t, P, "t")
        for cm in reversed(self._ctxs):
            cm.__exit__(None, None, None)


def run_events(eng, P, tag):
    for (etag, fn, waits, sem, inc) in P.events:
        if etag != tag:
            continue
        for (s, v) in waits:
            eng.wait_ge(s, v)
        inst = fn(eng)
        if sem is not None and inc and inst is not None:
            inst.then_inc(sem, inc)


def build(dbg_names=()):
    B = Builder(dbg_names)
    nc = B.nc

    d_xseq = nc.dram_tensor("xseq", [SEQ, DM], F32, kind="ExternalInput").ap()
    d_xown = nc.dram_tensor("xown", [NO * 128, DM], F32, kind="ExternalInput").ap()
    d_pm = nc.dram_tensor("pm", [128, 1], F32, kind="ExternalInput").ap()
    d_pm1 = nc.dram_tensor("pm1", [128, 1], F32, kind="ExternalInput").ap()
    d_wq = nc.dram_tensor("wq", [DM, DM], F32, kind="ExternalInput").ap()
    d_wk = nc.dram_tensor("wk", [128, 8192], F32, kind="ExternalInput").ap()
    d_wv = nc.dram_tensor("wv", [128, 8192], F32, kind="ExternalInput").ap()
    d_wo = nc.dram_tensor("wo", [128, 4 * 8192], F32, kind="ExternalInput").ap()
    d_wgs = nc.dram_tensor("wgs", [128, 512], F32, kind="ExternalInput").ap()
    d_wg = nc.dram_tensor("wg", [FH, DM], F32, kind="ExternalInput").ap()
    d_wu = nc.dram_tensor("wu", [FH, DM], F32, kind="ExternalInput").ap()
    d_wd = nc.dram_tensor("wd", [2048, FH], F32, kind="ExternalInput").ap()
    d_out = nc.dram_tensor("oown", [NO * 128, DM], F32, kind="ExternalOutput").ap()

    DXA = [B.sem("DXA0"), B.sem("DXA1")]
    DW = [B.sem("DW0"), B.sem("DW1"), B.sem("DW2"), B.sem("DW3")]
    DF = [B.sem("DF0"), B.sem("DF1")]
    DQ = [B.sem("DQ0"), B.sem("DQ1")]
    DK = [B.sem("DK0"), B.sem("DK1")]
    DP = [B.sem("DP0"), B.sem("DP1")]
    DS = [B.sem("DS0"), B.sem("DS1")]
    DDB = B.sem("DDB")
    B.init_chains()
    for _tag in ("v", "sc", "g"):
        B.mkchain(_tag)
    CV = B.sem("CV")
    CS = B.sem("CS")
    CT = B.sem("CT")
    CG = B.sem("CG")

    psA = B.ps("psA", [128, 2048])
    psB = B.ps("psB", [128, 2048])

    ALLSEMS = DXA + DW + DF + DQ + DK + DP + DS + [CV, CS, CT, CG]

    def barrier(name):
        snap = [(sm, B.val(sm)) for sm in ALLSEMS if B.val(sm) > 0]
        for tag in ("s", "g", "v", "sc", "t"):
            B.ev(tag, (lambda e: None), waits=snap)

    ones_row = B.sb("ones_row", [1, 128], F32)
    ident = B.sb("ident", [128, 128], F32)
    idbf = B.sb("idbf", [128, 128], BF16)
    mdiag = B.sb("mdiag", [128, 128], F32)
    mpre = B.sb("mpre", [128, 128], F32)
    mlast = B.sb("mlast", [128, 128], F32)
    bmask2 = B.sb("bmask2", [128, 256], BF16)
    bmask4 = B.sb("bmask4", [128, 512], BF16)
    pmb = B.sb("pmb", [128, 1], F32)
    pm1b = B.sb("pm1b", [128, 1], F32)
    pm1neg = B.sb("pm1neg", [128, 1], F32)
    wrow = B.sb("wrow", [1, 64], F32)
    omega = B.sb("omega", [128, 64], F32)
    tgi = B.sb("tgi", [128, 64], F32)
    rr1 = B.sb("rr1", [128, 64], F32)
    rr2 = B.sb("rr2", [128, 64], F32)
    rri = B.sb("rri", [128, 64], I32)
    rrk = B.sb("rrk", [128, 64], F32)
    zbQ = B.sb("zbQ", [128, 256], F32)
    sums = B.sb("sums", [128, 24], F32)
    sumsq = B.sb("sumsq", [128, 24], F32)
    mean_ = B.sb("mean_", [128, 24], F32)
    var_ = B.sb("var_", [128, 24], F32)
    rstd = B.sb("rstd", [128, 24], F32)
    nrs = B.sb("nrs", [128, 24], F32)
    bias_ = B.sb("bias_", [128, 24], F32)
    var2 = B.sb("var2", [128, 8], F32)
    rstd2 = B.sb("rstd2", [128, 8], F32)
    n8 = B.sb("n8", [128, 8], F32)
    amax = B.sb("amax", [128, 8], F32)
    zsc = B.sb("zsc", [128, 8], F32)
    ninvs = B.sb("ninvs", [128, 8], F32)
    gtm = B.sb("gtm", [128, NO, H], F32)
    gsig = B.sb("gsig", [128, 32], F32)
    lsum = B.sb("lsum", [128, 1], F32)
    mrow = B.sb("mrow", [128, 2], F32)
    lsum2 = B.sb("lsum2", [128, 2], F32)
    mrow2 = B.sb("mrow2", [128, 6], F32)

    costm = B.sb("costm", [128, NT, 64], F32, phases=(0, 2))
    sintm = B.sb("sintm", [128, NT, 64], F32, phases=(0, 2))
    wkT = B.sb("wkT", [128, 16, 4, 128], BF16, phases=(2, 3))
    wvT = B.sb("wvT", [128, 16, 4, 128], BF16, phases=(2, 3))
    wgT = B.sb("wgT", [128, 16, 32], BF16, phases=(0, 2))
    sqs = B.sb("sqs", [128, DM], F32, phases=(0, 0))
    xsA = [B.sb(f"xsA{i}", [128, DM], F32, phases=(0, 2)) for i in range(2)]
    wqTh = B.sb("wqTh", [128, 16, 16, 128], BF16, phases=(0, 1))
    xnbo = [B.sb(f"xnbo{i}", [128, DM], BF16, phases=(1, 1)) for i in range(2)]
    xfm = [B.sb(f"xfm{i}", [128, NT, 128], BF16, phases=(1, 2)) for i in range(2)]
    qtm = [B.sb(f"qtm{i}", [128, 2048], BF16, phases=(1, 1)) for i in range(2)]
    crep = B.sb("crep", [128, 1024], F32, phases=(0, 2))
    srep = B.sb("srep", [128, 1024], F32, phases=(0, 2))
    zbW = B.sb("zbW", [128, 2048], F32, phases=(0, 2))
    zbW2 = B.sb("zbW2", [128, 1024], F32, phases=(0, 0))
    rriW = B.sb("rriW", [128, 1024], I32, phases=(0, 0))
    qfm = B.sb("qfm", [128, NO, H, 128], BF16, phases=(1, 3))
    xnb = [B.sb(f"xnb{i}", [128, DM], BF16, phases=(2, 2)) for i in range(2)]
    ktm = [B.sb(f"ktm{i}", [128, 512], BF16, phases=(2, 2)) for i in range(2)]
    kfm = B.sb("kfm", [128, HKV, NT, 128], BF16, phases=(2, 3))
    vtm = B.sb("vtm", [128, NT, 512], BF16, phases=(2, 3))
    pexp = [B.sb(f"pexp{i}", [128, 2048], F32, phases=(3, 3)) for i in range(2)]
    pbf2 = [B.sb(f"pbf2_{i}", [128, 4096], BF16, phases=(3, 3)) for i in range(2)]
    PT = [B.sb(f"PT{i}", [128, 4, 16, 128], BF16, phases=(3, 3))
          for i in range(2)]
    attn = B.sb("attn", [128, 16, NO, 128], BF16, phases=(3, 4))
    oacc = B.sb("oacc", [128, NO, 2048], F32, phases=(4, 7))
    woT = [B.sb(f"woT{i}", [128, 16, 4, 128], BF16, phases=(4, 4))
           for i in range(2)]
    xsB = [B.sb(f"xsB{i}", [128, DM], F32, phases=(5, 5)) for i in range(2)]
    xnf = [B.sb(f"xnf{i}", [128, DM], F32, phases=(5, 5)) for i in range(2)]
    zb = B.sb("zb", [128, DM], F32, phases=(5, 5))
    zi = B.sb("zi", [128, DM], I32, phases=(5, 5))
    zk = B.sb("zk", [128, DM], F32, phases=(5, 5))
    zd = B.sb("zd", [128, DM], F32, phases=(5, 5))
    xqtm = [B.sb(f"xqtm{i}", [128, DM], BF16, phases=(5, 5)) for i in range(2)]
    rowb = B.sb("rowb", [1, 1024], F32, phases=(5, 5))
    xqfm = B.sb("xqfm", [128, 16, NO, 128], BF16, phases=(5, 6))
    nb_full = B.sb("nb_full", [128, 1024], F32, phases=(5, 6))
    pq = B.sb("pq", [128, 16, NO, 128], BF16, phases=(6, 6))
    wgn = [B.sb(f"wgn{i}", [128, 16, 128], BF16, phases=(6, 6)) for i in range(4)]
    wdT = [B.sb(f"wdT{i}", [128, 16, 4, 128], BF16, phases=(6, 6)) for i in range(2)]
    ffa = B.sb("ffa", [128, 1024], F32, phases=(6, 6))
    fft = B.sb("fft", [128, 1024], F32, phases=(6, 6))
    ffu = B.sb("ffu", [128, 1024], F32, phases=(6, 6))
    xsC = [B.sb(f"xsC{i}", [128, DM], F32, phases=(7, 7)) for i in range(2)]
    ost = [B.sb(f"ost{i}", [128, DM], F32, phases=(7, 7)) for i in range(2)]

    def g_const(g):
        g.memset(ones_row[:], 1.0)
        g.memset(mdiag[:], 0.0)
        g.memset(ident[:], 1.0)
        g.drain()
        g.affine_select(mdiag[:], mdiag[:], [[-1, 128]], ALU.is_ge, NEG,
                        base=0, channel_multiplier=1)
        g.affine_select(ident[:], ident[:], [[-1, 128]], ALU.is_equal, 0.0,
                        base=0, channel_multiplier=1)
        g.iota(wrow[:], [[1, 64]], base=0, channel_multiplier=0,
               allow_small_or_imprecise_dtypes=True)
        g.iota(srep[:, 0:1024].rearrange("p (a b) -> p a b", b=64),
               [[128, 16], [0, 64]], base=0, channel_multiplier=1,
               allow_small_or_imprecise_dtypes=True)
        return g.drain()
    cg_const = B.ev("g", g_const, sem=CG, inc=1)

    def s_pm(s):
        s.dma_start(out=pmb[:], in_=d_pm).then_inc(DXA[0], 16)
        s.dma_start(out=pm1b[:], in_=d_pm1).then_inc(DXA[1], 16)
        return None
    B.ev("s", s_pm)
    B.cnt[DXA[0]] = B.cnt.get(DXA[0], 0) + 16
    B.cnt[DXA[1]] = B.cnt.get(DXA[1], 0) + 16

    def v_masks(v):
        v.tensor_scalar_mul(pm1neg[:], pm1b[:], NEG)
        v.tensor_scalar_mul(mpre[:], mdiag[:], pm1b[:, 0:1])
        v.tensor_scalar_mul(mlast[:], mdiag[:], pmb[:, 0:1])
        v.tensor_copy(idbf[:], ident[:])
        v.drain()
        v.tensor_scalar_add(mlast[:], mlast[:], pm1neg[:, 0:1])
        v.tensor_copy(bmask2[:, 0:128], mpre[:])
        v.memset(bmask4[:, 0:256], 0.0)
        v.tensor_scalar_mul(rr1[0:1, :], wrow[:], -L64_HI)
        v.drain()
        v.tensor_copy(bmask2[:, 128:256], mlast[:])
        v.tensor_copy(bmask4[:, 256:384], mpre[:])
        v.tensor_copy(bmask4[:, 384:512], mlast[:])
        v.scalar_tensor_tensor(rr1[0:1, :], wrow[:], -L64_LO, rr1[0:1, :],
                               ALU.mult, ALU.add)
        return v.drain()
    cv_masks = B.ev("v", v_masks,
                    waits=[(CG, cg_const), (DXA[0], 16), (DXA[1], 16)],
                    sem=CV, inc=1)

    def s_omega(s):
        return s.activation(wrow[:], rr1[0:1, :], AF.Exp)
    cs_omega = B.ev("sc", s_omega, waits=[(CV, cv_masks)], sem=CS, inc=1)

    def t_omega(t):
        return t.matmul(psB[:, 0:64], ones_row[:], wrow[:], start=True, stop=True)
    ct_omega = B.ev("t", t_omega, waits=[(CS, cs_omega)], sem=CT, inc=1)

    def v_omega_ev(v):
        return v.tensor_copy(omega[:], psB[:, 0:64])
    cv_omega = B.ev("v", v_omega_ev, waits=[(CT, ct_omega)], sem=CV, inc=1)

    def rr_chain(v, src):
        for op in (
            lambda: v.tensor_scalar(rr2[:], src, INV_2PI, 0.5, ALU.mult, ALU.add),
            lambda: v.tensor_copy(rri[:], rr2[:]),
            lambda: v.tensor_copy(rrk[:], rri[:]),
            lambda: v.tensor_sub(rr2[:], rr2[:], rrk[:]),
            lambda: v.tensor_scalar(rr2[:], rr2[:], 0.0, None, ALU.is_lt),
            lambda: v.tensor_sub(rrk[:], rrk[:], rr2[:]),
            lambda: v.scalar_tensor_tensor(rr2[:], rrk[:], -P2_1, src,
                                           ALU.mult, ALU.add),
            lambda: v.scalar_tensor_tensor(rr2[:], rrk[:], -P2_2, rr2[:],
                                           ALU.mult, ALU.add),
            lambda: v.scalar_tensor_tensor(rr2[:], rrk[:], -P2_3, rr2[:],
                                           ALU.mult, ALU.add),
        ):
            B.ch("v", v, op())

    def v_tables(v):
        i0 = v.tensor_copy(crep[:, 0:64], omega[:])
        B.ch("v", v, i0)
        for w in (64, 128, 256, 512):
            B.ch("v", v, v.tensor_copy(crep[:, w:2 * w], crep[:, 0:w]))
        B.ch("v", v, v.tensor_mul(zbW[:, 0:1024], srep[:, 0:1024], crep[:, 0:1024]))
        return v.engine_nop()
    cv_tab0 = B.ev("v", v_tables, waits=[(CV, cv_omega), (CG, B.val(CG))],
                   sem=CV, inc=1)

    def rr_chain_w(v, src, uW, riW, rkW):
        for op in (
            lambda: v.tensor_scalar(uW, src, INV_2PI, 0.5, ALU.mult, ALU.add),
            lambda: v.tensor_copy(riW, uW),
            lambda: v.tensor_copy(rkW, riW),
            lambda: v.tensor_sub(uW, uW, rkW),
            lambda: v.tensor_scalar(uW, uW, 0.0, None, ALU.is_lt),
            lambda: v.tensor_sub(rkW, rkW, uW),
            lambda: v.scalar_tensor_tensor(uW, rkW, -P2_1, src, ALU.mult, ALU.add),
            lambda: v.scalar_tensor_tensor(uW, rkW, -P2_2, uW, ALU.mult, ALU.add),
            lambda: v.scalar_tensor_tensor(uW, rkW, -P2_3, uW, ALU.mult, ALU.add),
        ):
            B.ch("v", v, op())

    def v_tab_sin(v):
        rr_chain_w(v, zbW[:, 0:1024], zbW[:, 1024:2048], rriW[:], zbW2[:])
        return v.engine_nop()
    cv_tsin = B.ev("v", v_tab_sin, sem=CV, inc=1)

    def s_tab_sin(s):
        return s.activation(sintm[:].rearrange("p a b -> p (a b)"),
                            zbW[:, 1024:2048], AF.Sin)
    cs_tsin = B.ev("sc", s_tab_sin, waits=[(CV, cv_tsin)], sem=CS, inc=1)

    def v_tab_cos(v):
        B.ch("v", v, v.tensor_scalar_add(zbW[:, 0:1024], zbW[:, 0:1024], PIH_HI))
        B.ch("v", v, v.tensor_scalar_add(zbW[:, 0:1024], zbW[:, 0:1024], PIH_LO))
        rr_chain_w(v, zbW[:, 0:1024], zbW[:, 1024:2048], rriW[:], zbW2[:])
        return v.engine_nop()
    cv_tcos = B.ev("v", v_tab_cos, waits=[(CS, cs_tsin)], sem=CV, inc=1)

    def s_tab_cos(s):
        return s.activation(costm[:].rearrange("p a b -> p (a b)"),
                            zbW[:, 1024:2048], AF.Sin)
    B.ev("sc", s_tab_cos, waits=[(CV, cv_tcos)], sem=CS, inc=1)

    B.dbg("sintm", sintm)
    B.dbg("costm", costm)

    wl_prev = [0, 0, 0, 0]

    class WS:
        idx = 0

    def wload(src, dst, sdst=None, waits=(), sign=False):
        slot = WS.idx % 4
        WS.idx += 1

        def g_w(g, src=src, dst=dst, slot=slot, prev=wl_prev[slot],
                waits=tuple(waits)):
            g.wait_ge(DW[slot], prev)
            for (sm, vv) in waits:
                if vv:
                    g.wait_ge(sm, vv)
            return g.dma_start(out=dst, in_=src)
        dw = B.ev("g", g_w, sem=DW[slot], inc=16)
        wl_prev[slot] = dw
        if sign:
            def s_sg(s, d2=(sdst if sdst is not None else dst), slot=slot,
                     dw=dw):
                s.wait_ge(DW[slot], dw)
                return s.activation(d2, d2, AF.Sign)
            return (CS, B.ev("sc", s_sg, sem=CS, inc=1))
        return (DW[slot], dw)

    wload(d_wgs, wgT[:].rearrange("p a b -> p (a b)"))
    for k in range(16):
        wload(d_wq[k * 128:(k + 1) * 128, :],
              wqTh[:, k, :, :].rearrange("p a b -> p (a b)"))
    dw_wq_all = tuple(B.val(x) for x in DW)
    B.dbg("wkT", wkT)

    xsrc = [(d_xseq, t, t) for t in range(NT)] + \
           [(d_xown, t, NT + t) for t in range(NO)]
    xrel = [(0, 0)] * 2
    cs_sq_prev = 0
    for idx, (src, t, col) in enumerate(xsrc):
        slot = idx % 2

        def s_ldx(s, src=src, t=t, slot=slot, rel=xrel[slot],
                  dxp=B.val(DXA[slot])):
            s.wait_ge(DXA[slot], dxp)
            if rel[0]:
                s.wait_ge(CV, rel[0])
            if rel[1]:
                s.wait_ge(CS, rel[1])
            return s.dma_start(out=xsA[slot][:], in_=src[t * 128:(t + 1) * 128, :])
        dx = B.ev("s", s_ldx, sem=DXA[slot], inc=16)

        def v_sum(v, slot=slot, col=col):
            return v.reduce_sum(sums[:, col:col + 1], xsA[slot][:], axis=AX.X)
        cv_h = B.ev("v", v_sum, waits=[(DXA[slot], dx)], sem=CV, inc=1)

        def s_sq(s, slot=slot, col=col, relsq=cs_sq_prev):
            if relsq:
                s.wait_ge(CS, relsq)
            return s.activation(sqs[:], xsA[slot][:], AF.Square,
                                accum_out=sumsq[:, col:col + 1])
        cs_h = B.ev("sc", s_sq, waits=[(DXA[slot], dx)], sem=CS, inc=1)
        cs_sq_prev = cs_h
        xrel[slot] = (cv_h, cs_h)

    def nr_rsqrt(v, r, vv, scratch, iters=6):
        v.memset(r, 1.0)
        v.drain()
        for _ in range(iters):
            v.tensor_mul(scratch, r, r)
            v.drain()
            v.tensor_mul(scratch, scratch, vv)
            v.drain()
            v.tensor_scalar(scratch, scratch, -0.5, 1.5, ALU.mult, ALU.add)
            v.drain()
            v.tensor_mul(r, r, scratch)
            v.drain()

    def v_stats(v):
        v.tensor_scalar_mul(mean_[:], sums[:], 1.0 / DM)
        v.tensor_scalar_mul(var_[:], sumsq[:], 1.0 / DM)
        v.drain()
        v.tensor_mul(nrs[:], mean_[:], mean_[:])
        v.drain()
        v.tensor_sub(var_[:], var_[:], nrs[:])
        v.drain()
        v.tensor_scalar_add(var_[:], var_[:], 1e-5)
        v.drain()
        nr_rsqrt(v, rstd[:], var_[:], nrs[:])
        v.tensor_mul(bias_[:], mean_[:], rstd[:])
        v.drain()
        v.tensor_scalar_mul(bias_[:], bias_[:], -1.0)
        v.tensor_scalar_add(var2[:], var_[:, 16:24], -1e-5)
        v.drain()
        v.tensor_mul(n8[:], rstd[:, 16:24], rstd[:, 16:24])
        v.drain()
        v.tensor_mul(var2[:], var2[:], n8[:])
        v.drain()
        v.tensor_scalar_add(var2[:], var2[:], 1e-8)
        v.drain()
        nr_rsqrt(v, rstd2[:], var2[:], n8[:])
        return v.drain()
    cv_stats = B.ev("v", v_stats,
                    waits=[(CV, B.val(CV)), (CS, B.val(CS))], sem=CV, inc=1)
    B.dbg("rstd", rstd)
    B.dbg("rstd2", rstd2)
    B.dbg("bias_", bias_)

    barrier("pc")
    relC_ct = [0, 0]
    relC_dtq = [0, 0]
    rope_hist = []
    cs_gate_prev = 0
    xrelC = list(xrel)
    cs_xnbo = {}
    xnbo_rel = [0, 0]
    for j in range(NO):
        fslot = j % 2
        slot = j % 2
        cv_rope_m2 = rope_hist[-2] if len(rope_hist) >= 2 else 0

        def s_ldxo(s, j=j, slot=slot, rel=xrelC[slot], dxp=B.val(DXA[slot])):
            s.wait_ge(DXA[slot], dxp)
            if rel[0]:
                s.wait_ge(CV, rel[0])
            if rel[1]:
                s.wait_ge(CS, rel[1])
            return s.dma_start(out=xsA[slot][:],
                               in_=d_xown[j * 128:(j + 1) * 128, :])
        dx = B.ev("s", s_ldxo, sem=DXA[slot], inc=16)

        def s_xn(s, j=j, slot=slot, fslot=fslot, dx=dx,
                 relb=xnbo_rel[fslot]):
            s.wait_ge(DXA[slot], dx)
            if relb:
                s.wait_ge(DF[fslot], relb)
            return s.activation(xnbo[fslot][:], xsA[slot][:], AF.Identity,
                                bias=bias_[:, 16 + j:17 + j],
                                scale=rstd[:, 16 + j:17 + j])
        cs_xn = B.ev("sc", s_xn, waits=[(CV, cv_stats)], sem=CS, inc=1)
        cs_xnbo[j] = cs_xn
        xrelC[slot] = (0, cs_xn)

        def s_tpxc(s, j=j, fslot=fslot, relct=relC_ct[fslot]):
            s.wait_ge(CS, cs_xnbo[j])
            if relct:
                s.wait_ge(CT, relct)
            return s.dma_start(out=xfm[fslot][:, 0:16, :], in_=xnbo[fslot][:],
                               transpose=True)
        dt_x = B.ev("s", s_tpxc, sem=DF[fslot], inc=16)
        xnbo_rel[fslot] = dt_x

        def t_q(t, j=j, fslot=fslot, dt=dt_x, cvprev=cv_rope_m2,
                csg_prev=cs_gate_prev, dtw=dw_wq_all):
            t.wait_ge(DF[fslot], dt)
            for _i in range(4):
                t.wait_ge(DW[_i], dtw[_i])
            if cvprev:
                t.wait_ge(CV, cvprev)
            if csg_prev:
                t.wait_ge(CS, csg_prev)
            mm = None
            for k in range(16):
                lhsT = xfm[fslot][:, k, :]
                for nn in range(4):
                    mm = t.matmul(psA[:, nn * 512:(nn + 1) * 512],
                                  lhsT, wqTh[:, k, nn * 4:(nn + 1) * 4, :],
                                  start=(k == 0), stop=(k == 15))
                mm = t.matmul(psB[:, 1792:1824], lhsT, wgT[:, k, :],
                              start=(k == 0), stop=(k == 15))
            return mm
        ct_q = B.ev("t", t_q, sem=CT, inc=1)
        relC_ct[fslot] = ct_q

        def s_gate(s, j=j, ct=ct_q):
            s.wait_ge(CT, ct)
            return s.activation(gsig[:], psB[:, 1792:1824], AF.Tanh, scale=0.5)
        cs_gate = B.ev("sc", s_gate, sem=CS, inc=1)
        cs_gate_prev = cs_gate

        def v_rope_q(v, j=j, fslot=fslot, ct=ct_q, csg=cs_gate,
                     reldq=relC_dtq[fslot]):
            if reldq:
                v.wait_ge(DQ[fslot], reldq)
            v.wait_ge(CT, ct)
            v.tensor_scalar_mul(crep[:, 0:64], costm[:, 2 * j, :], pm1b[:, 0:1])
            i1 = v.tensor_scalar_mul(srep[:, 0:64], sintm[:, 2 * j, :],
                                     pm1b[:, 0:1])
            B.ch("v", v, i1)
            v.scalar_tensor_tensor(crep[:, 0:64], costm[:, 2 * j + 1, :],
                                   pmb[:, 0:1], crep[:, 0:64], ALU.mult, ALU.add)
            i2 = v.scalar_tensor_tensor(srep[:, 0:64], sintm[:, 2 * j + 1, :],
                                        pmb[:, 0:1], srep[:, 0:64],
                                        ALU.mult, ALU.add)
            B.ch("v", v, i2)
            for w in (64, 128, 256, 512):
                v.tensor_copy(crep[:, w:2 * w], crep[:, 0:w])
                i3 = v.tensor_copy(srep[:, w:2 * w], srep[:, 0:w])
                B.ch("v", v, i3)
            ps3 = psA[:].rearrange("p (h t) -> p h t", t=128)
            x1 = ps3[:, :, 0:64]
            x2 = ps3[:, :, 64:128]
            c3 = crep[:].rearrange("p (h t) -> p h t", t=64)
            s3 = srep[:].rearrange("p (h t) -> p h t", t=64)
            o3 = qtm[fslot][:].rearrange("p (h t) -> p h t", t=128)
            v.tensor_mul(zbW[:, 0:1024], x1, c3)
            i4 = v.tensor_mul(zbW[:, 1024:2048], x2, s3)
            B.ch("v", v, i4)
            i4b = v.tensor_sub(
                o3[:, :, 0:64],
                zbW[:, 0:1024].rearrange("p (h t) -> p h t", t=64),
                zbW[:, 1024:2048].rearrange("p (h t) -> p h t", t=64))
            B.ch("v", v, i4b)
            v.tensor_mul(zbW[:, 0:1024], x2, c3)
            i5 = v.tensor_mul(zbW[:, 1024:2048], x1, s3)
            B.ch("v", v, i5)
            out = v.tensor_add(
                o3[:, :, 64:128],
                zbW[:, 0:1024].rearrange("p (h t) -> p h t", t=64),
                zbW[:, 1024:2048].rearrange("p (h t) -> p h t", t=64))
            v.wait_ge(CS, csg)
            i6 = v.tensor_add(gsig[:, 0:16], gsig[:, 0:16], gsig[:, 16:32])
            B.ch("v", v, i6)
            out = v.tensor_scalar(gtm[:, j, :], gsig[:, 0:16], 0.5, 1.0,
                                  ALU.mult, ALU.add)
            return out
        cv_rope = B.ev("v", v_rope_q, sem=CV, inc=1)
        rope_hist.append(cv_rope)

        def s_tpq(s, j=j, fslot=fslot, cv=cv_rope):
            s.wait_ge(CV, cv)
            return s.dma_start(out=qfm[:, j, :, :], in_=qtm[fslot][:],
                               transpose=True)
        relC_dtq[fslot] = B.ev("s", s_tpq, sem=DQ[fslot], inc=16)

    B.dbg("qfm", qfm)
    B.dbg("gtm", gtm)

    barrier("pb")

    for h in range(2):
        wload(d_wk[:, h * 4096:(h + 1) * 4096],
              wkT[:, 8 * h:8 * h + 8, :, :].rearrange("p a b c -> p (a b c)"))
        wload(d_wv[:, h * 4096:(h + 1) * 4096],
              wvT[:, 8 * h:8 * h + 8, :, :].rearrange("p a b c -> p (a b c)"))
    DW_attw = tuple(B.val(x) for x in DW)
    xrelB = list(xrelC)
    cv_kv_par = [0, 0]
    relB_dtx = [0, 0]
    relB_ctx = list(relC_ct)
    relB_dtk = [0, 0]
    for t_i in range(NT):
        slot = t_i % 2
        fslot = t_i % 2

        def s_ldxs(s, t_i=t_i, slot=slot, rel=xrelB[slot], dxp=B.val(DXA[slot])):
            s.wait_ge(DXA[slot], dxp)
            if rel[0]:
                s.wait_ge(CV, rel[0])
            if rel[1]:
                s.wait_ge(CS, rel[1])
            return s.dma_start(out=xsA[slot][:],
                               in_=d_xseq[t_i * 128:(t_i + 1) * 128, :])
        dx = B.ev("s", s_ldxs, sem=DXA[slot], inc=16)

        def s_xn2(s, t_i=t_i, slot=slot, fslot=fslot, dx=dx, rel=relB_dtx[fslot]):
            s.wait_ge(DXA[slot], dx)
            if rel:
                s.wait_ge(DF[fslot], rel)
            return s.activation(xnb[fslot][:], xsA[slot][:], AF.Identity,
                                bias=bias_[:, t_i:t_i + 1],
                                scale=rstd[:, t_i:t_i + 1])
        cs_xn = B.ev("sc", s_xn2, waits=[(CV, cv_stats)], sem=CS, inc=1)
        xrelB[slot] = (0, cs_xn)

        def s_tpx2(s, fslot=fslot, cs=cs_xn, relct=relB_ctx[fslot]):
            s.wait_ge(CS, cs)
            if relct:
                s.wait_ge(CT, relct)
            return s.dma_start(out=xfm[fslot][:], in_=xnb[fslot][:], transpose=True)
        dt_x = B.ev("s", s_tpx2, sem=DF[fslot], inc=16)
        relB_dtx[fslot] = dt_x

        def t_kv(t, t_i=t_i, fslot=fslot, dt=dt_x, cvprev=cv_kv_par[t_i % 2]):
            t.wait_ge(DF[fslot], dt)
            for _i in range(4):
                t.wait_ge(DW[_i], DW_attw[_i])
            if cvprev:
                t.wait_ge(CV, cvprev)
            off = 0 if (t_i % 2 == 0) else 1024
            mm = None
            for k in range(16):
                lhsT = xfm[fslot][:, k, :]
                t.matmul(psB[:, off:off + 512], lhsT, wkT[:, k, :, :],
                         start=(k == 0), stop=(k == 15))
                mm = t.matmul(psB[:, off + 512:off + 1024], lhsT, wvT[:, k, :, :],
                              start=(k == 0), stop=(k == 15))
            return mm
        ct_kv = B.ev("t", t_kv, sem=CT, inc=1)
        relB_ctx[fslot] = ct_kv

        def v_kv(v, t_i=t_i, fslot=fslot, ct=ct_kv, rel=relB_dtk[fslot]):
            v.wait_ge(CT, ct)
            if rel:
                v.wait_ge(DK[fslot], rel)
            off = 0 if (t_i % 2 == 0) else 1024
            v.tensor_copy(crep[:, 0:64], costm[:, t_i, :])
            i1 = v.tensor_copy(srep[:, 0:64], sintm[:, t_i, :])
            B.ch("v", v, i1)
            for w in (64, 128):
                v.tensor_copy(crep[:, w:2 * w], crep[:, 0:w])
                i2 = v.tensor_copy(srep[:, w:2 * w], srep[:, 0:w])
                B.ch("v", v, i2)
            ps3 = psB[:, off:off + 512].rearrange("p (h t) -> p h t", t=128)
            x1 = ps3[:, :, 0:64]
            x2 = ps3[:, :, 64:128]
            c3 = crep[:, 0:256].rearrange("p (h t) -> p h t", t=64)
            s3 = srep[:, 0:256].rearrange("p (h t) -> p h t", t=64)
            o3 = ktm[fslot][:].rearrange("p (h t) -> p h t", t=128)
            v.tensor_mul(zbW[:, 0:256], x1, c3)
            v.tensor_mul(zbW[:, 512:768], x2, s3)
            v.tensor_mul(zbW[:, 1024:1280], x2, c3)
            i3 = v.tensor_mul(zbW[:, 1536:1792], x1, s3)
            B.ch("v", v, i3)
            v.tensor_sub(o3[:, :, 0:64],
                         zbW[:, 0:256].rearrange("p (h t) -> p h t", t=64),
                         zbW[:, 512:768].rearrange("p (h t) -> p h t", t=64))
            v.tensor_add(o3[:, :, 64:128],
                         zbW[:, 1024:1280].rearrange("p (h t) -> p h t", t=64),
                         zbW[:, 1536:1792].rearrange("p (h t) -> p h t", t=64))
            return v.tensor_copy(vtm[:, t_i, :], psB[:, off + 512:off + 1024])
        cv_kv = B.ev("v", v_kv, sem=CV, inc=1)
        cv_kv_par[t_i % 2] = cv_kv

        def s_tpk(s, t_i=t_i, fslot=fslot, cv=cv_kv):
            s.wait_ge(CV, cv)
            return s.dma_start(out=kfm[:, :, t_i, :], in_=ktm[fslot][:],
                               transpose=True)
        relB_dtk[fslot] = B.ev("s", s_tpk, sem=DK[fslot], inc=16)

    B.dbg("kfm", kfm)
    B.dbg("vtm", vtm)
    DKQ_all = [(DK[0], B.val(DK[0])), (DK[1], B.val(DK[1])),
               (DQ[0], B.val(DQ[0])), (DQ[1], B.val(DQ[1]))]
    CV_kv_all = B.val(CV)

    barrier("pd")

    def v_pdinit(v):
        v.memset(pbf2[0][:], 0.0)
        return v.memset(pbf2[1][:], 0.0)
    cv_pdinit = B.ev("v", v_pdinit, sem=CV, inc=1)

    cs_exp_prev = 0
    cs_exp_byreg = [0, 0]
    cv_sc_slot = [0, 0]
    dp_slot = [0, 0]
    pt_rel = [0, 0]
    cv_pv_prev = 0
    for j in range(NO):
        nch = 2 * j + 2
        keys = 128 * nch
        for g in range(HKV):
            tslot = (j * HKV + g) % 2
            dtp_pair = []
            for hp in range(2):
                pslot2 = (j * 8 + g * 2 + hp) % 2
                cv_sc_pair = 0
                for h2 in range(2):
                    hh = hp * 2 + h2
                    h = HKV * g + hh
                    pslot = h % 2
                    reg = 1 if (h % 2 == 1 and j <= 5) else 0
                    Sps = (psA[:, 0:keys] if reg == 0
                           else psB[:, 512:512 + keys])

                    def t_S(t, j=j, h=h, g=g, nch=nch, Sps=Sps,
                            cse=cs_exp_byreg[reg]):
                        if cse:
                            t.wait_ge(CS, cse)
                        lhsT = qfm[:, j, h, :]
                        for c0 in range(0, nch, 4):
                            cw = min(4, nch - c0)
                            if c0 + cw < nch or not MASK_MM:
                                mm = t.matmul(
                                    Sps[:, c0 * 128:(c0 + cw) * 128],
                                    lhsT, kfm[:, g, c0:c0 + cw, :],
                                    start=True, stop=True)
                            else:
                                t.matmul(Sps[:, c0 * 128:nch * 128],
                                         lhsT, kfm[:, g, c0:nch, :],
                                         start=True, stop=False)
                                msk = bmask4 if cw > 2 else bmask2
                                mm = t.matmul(
                                    Sps[:, c0 * 128:nch * 128],
                                    idbf[:], msk[:],
                                    start=False, stop=True)
                        return mm
                    ct_S = B.ev("t", t_S, waits=DKQ_all + [(CV, CV_kv_all)],
                                sem=CT, inc=1)

                    def v_msk(v, nch=nch, keys=keys, Sps=Sps, ct=ct_S,
                              pc=pslot):
                        v.wait_ge(CT, ct)
                        if not MASK_MM:
                            v.tensor_add(
                                Sps[:, (nch - 2) * 128:(nch - 1) * 128],
                                Sps[:, (nch - 2) * 128:(nch - 1) * 128],
                                mpre[:])
                            i1 = v.tensor_add(
                                Sps[:, (nch - 1) * 128:nch * 128],
                                Sps[:, (nch - 1) * 128:nch * 128], mlast[:])
                            B.ch("v", v, i1)
                        i2 = v.reduce_max(mrow2[:, pc:pc + 1], Sps[:, 0:keys],
                                          axis=AX.X)
                        B.ch("v", v, i2)
                        return v.tensor_scalar_mul(mrow2[:, 2 + pc:3 + pc],
                                                   mrow2[:, pc:pc + 1],
                                                   -ATT_SCALE)
                    cv_m = B.ev("v", v_msk, sem=CV, inc=1)

                    def s_exp(s, keys=keys, pslot=pslot, cv=cv_m, Sps=Sps,
                              rel=cv_sc_slot[pslot]):
                        s.wait_ge(CV, cv)
                        if rel:
                            s.wait_ge(CV, rel)
                        return s.activation(
                            pexp[pslot][:, 0:keys],
                            Sps[:, 0:keys], AF.Exp,
                            bias=mrow2[:, 2 + pslot:3 + pslot],
                            scale=ATT_SCALE,
                            accum_out=lsum2[:, pslot:pslot + 1])
                    cs_exp_prev = B.ev("sc", s_exp, sem=CS, inc=1)
                    cs_exp_byreg[reg] = cs_exp_prev

                    def v_scale(v, j=j, h=h, h2=h2, keys=keys, pslot=pslot,
                                pslot2=pslot2, cs=cs_exp_prev,
                                reldp=dp_slot[pslot2]):
                        v.wait_ge(CS, cs)
                        if reldp:
                            v.wait_ge(DP[pslot2], reldp)
                        else:
                            v.wait_ge(CV, cv_pdinit)
                        i1 = v.reciprocal(mrow2[:, 4 + pslot:5 + pslot],
                                          lsum2[:, pslot:pslot + 1])
                        B.ch("v", v, i1)
                        i2 = v.tensor_mul(mrow2[:, 4 + pslot:5 + pslot],
                                          mrow2[:, 4 + pslot:5 + pslot],
                                          gtm[:, j, h:h + 1])
                        B.ch("v", v, i2)
                        return v.tensor_scalar_mul(
                            pbf2[pslot2][:, h2 * 2048:h2 * 2048 + keys],
                            pexp[pslot][:, 0:keys],
                            mrow2[:, 4 + pslot:5 + pslot])
                    cv_sc_pair = B.ev("v", v_scale, sem=CV, inc=1)
                    cv_sc_slot[pslot] = cv_sc_pair

                if PAIR_TP:
                    def s_tpp(s, hp=hp, pslot2=pslot2,
                              tslot=tslot, cv=cv_sc_pair,
                              relct=pt_rel[tslot]):
                        s.wait_ge(CV, cv)
                        if relct:
                            s.wait_ge(CT, relct)
                        return s.dma_start(
                            out=PT[tslot][:, 2 * hp:2 * hp + 2, :, :],
                            in_=pbf2[pslot2][:], transpose=True)
                    dt_p = B.ev("s", s_tpp, sem=DP[pslot2], inc=16)
                else:
                    B.cnt[DP[pslot2]] = B.cnt.get(DP[pslot2], 0) + 16

                    def s_tpp(s, nch=nch, keys=keys, hp=hp, pslot2=pslot2,
                              tslot=tslot, cv=cv_sc_pair,
                              relct=pt_rel[tslot]):
                        s.wait_ge(CV, cv)
                        if relct:
                            s.wait_ge(CT, relct)
                        s.dma_start(
                            out=PT[tslot][:, 2 * hp, 0:nch, :],
                            in_=pbf2[pslot2][:, 0:keys],
                            transpose=True).then_inc(DP[pslot2], 16)
                        return s.dma_start(
                            out=PT[tslot][:, 2 * hp + 1, 0:nch, :],
                            in_=pbf2[pslot2][:, 2048:2048 + keys],
                            transpose=True)
                    dt_p = B.ev("s", s_tpp, sem=DP[pslot2], inc=16)
                dp_slot[pslot2] = dt_p
                dtp_pair.append((pslot2, dt_p))

            def t_PV(t, j=j, g=g, nch=nch, keys=keys, tslot=tslot,
                     dtp=tuple(dtp_pair), cvp=cv_pv_prev):
                for (ps2, dv) in dtp:
                    t.wait_ge(DP[ps2], dv)
                if cvp:
                    t.wait_ge(CV, cvp)
                mm = None
                for c in range(nch):
                    mm = t.matmul(psB[:, 0:512],
                                  vtm[:, c, g * 128:(g + 1) * 128],
                                  PT[tslot][:, :, c, :],
                                  start=(c == 0), stop=(c == nch - 1))
                return mm
            ct_pv = B.ev("t", t_PV, waits=[(CV, CV_kv_all)], sem=CT, inc=1)
            pt_rel[tslot] = ct_pv

            def v_att(v, j=j, g=g, ct=ct_pv):
                v.wait_ge(CT, ct)
                return v.tensor_copy(attn[:, 4 * g:4 * g + 4, j, :],
                                     psB[:, 0:512])
            cv_pv_prev = B.ev("v", v_att, sem=CV, inc=1)

    B.dbg("attn", attn)
    CV_attn_done = B.val(CV)

    barrier("pe0")

    def psm(m):
        src = psA if m < 4 else psB
        return src[:, (m % 4) * 512:(m % 4 + 1) * 512]

    cv_evac_prev = 0
    ct_o = 0
    woT_rel = [0, 0]
    for n in range(4):
        oslot = n % 2
        dwo = wload(d_wo[:, n * 8192:(n + 1) * 8192],
                    woT[oslot][:].rearrange("p a b c -> p (a b c)"),
                    waits=[(CT, woT_rel[oslot])])

        for mg in range(2):
            def t_wo(t, n=n, mg=mg, oslot=oslot, dt=dwo, cvp=cv_evac_prev):
                t.wait_ge(dt[0], dt[1])
                if cvp:
                    t.wait_ge(CV, cvp)
                mm = None
                for m in range(4 * mg, 4 * mg + 4):
                    for k in range(16):
                        mm = t.matmul(psm(m), attn[:, k, m, :],
                                      woT[oslot][:, k, :, :],
                                      start=(k == 0), stop=(k == 15))
                return mm
            ct_o = B.ev("t", t_wo, waits=[(CV, CV_attn_done), (CS, B.val(CS))],
                        sem=CT, inc=1)
            woT_rel[oslot] = ct_o

            def v_evo(v, n=n, mg=mg, ct=ct_o):
                v.wait_ge(CT, ct)
                mm = None
                for m in range(4 * mg, 4 * mg + 4):
                    mm = v.tensor_copy(oacc[:, m, n * 512:(n + 1) * 512], psm(m))
                return mm
            cv_evac_prev = B.ev("v", v_evo, sem=CV, inc=1)

    barrier("p45")

    xrelQ = [(cv_evac_prev, B.val(CS))] * 2
    relQ_dt = [B.val(DQ[0]), B.val(DQ[1])]
    xnf_rel = [0, 0]
    cv_q_prev = 0
    for j in range(NO):
        slot = j % 2
        fslot = j % 2

        def s_ldxq(s, j=j, slot=slot, rel=xrelQ[slot], dxp=B.val(DXA[slot])):
            s.wait_ge(DXA[slot], dxp)
            if rel[0]:
                s.wait_ge(CV, rel[0])
            if rel[1]:
                s.wait_ge(CS, rel[1])
            return s.dma_start(out=xsB[slot][:], in_=d_xown[j * 128:(j + 1) * 128, :])
        dx = B.ev("s", s_ldxq, sem=DXA[slot], inc=16)

        def s_xn3(s, j=j, slot=slot, fslot=fslot, dx=dx, rel=xnf_rel[fslot]):
            s.wait_ge(DXA[slot], dx)
            if rel:
                s.wait_ge(CV, rel)
            return s.activation(xnf[fslot][:], xsB[slot][:], AF.Identity,
                                bias=bias_[:, 16 + j:17 + j],
                                scale=rstd[:, 16 + j:17 + j])
        cs_xn = B.ev("sc", s_xn3, sem=CS, inc=1)
        xrelQ[slot] = (0, cs_xn)

        def v_quant(v, j=j, fslot=fslot, cs=cs_xn, reldt=relQ_dt[fslot]):
            v.wait_ge(CS, cs)
            if reldt:
                v.wait_ge(DQ[fslot], reldt)
            xn = xnf[fslot][:]
            i1 = v.reduce_max(amax[:, j:j + 1], xn, axis=AX.X,
                              apply_absolute_value=True)
            B.ch("v", v, i1)
            i2 = v.tensor_mul(amax[:, j:j + 1], amax[:, j:j + 1],
                              rstd2[:, j:j + 1])
            B.ch("v", v, i2)
            i3 = v.reciprocal(n8[:, j:j + 1], amax[:, j:j + 1])
            B.ch("v", v, i3)
            i4 = v.tensor_scalar_mul(n8[:, j:j + 1], n8[:, j:j + 1], 127.0)
            B.ch("v", v, i4)
            v.reciprocal(ninvs[:, j:j + 1], n8[:, j:j + 1])
            i5 = v.tensor_mul(zsc[:, j:j + 1], rstd2[:, j:j + 1], n8[:, j:j + 1])
            B.ch("v", v, i5)
            v.tensor_scalar_mul(ninvs[:, j:j + 1], ninvs[:, j:j + 1], -1.0)
            i6 = v.tensor_scalar_mul(zb[:], xn, zsc[:, j:j + 1])
            B.ch("v", v, i6)
            i7 = v.tensor_copy(zi[:], zb[:])
            B.ch("v", v, i7)
            return v.tensor_copy(zk[:], zi[:])
        cv_q1 = B.ev("v", v_quant, sem=CV, inc=1)
        xnf_rel[fslot] = cv_q1

        def g_quant(g, cv=cv_q1):
            g.wait_ge(CV, cv)
            return g.tensor_sub(zd[:], zb[:], zk[:])
        cg_q = B.ev("g", g_quant, sem=CG, inc=1)

        def v_quant2(v, j=j, fslot=fslot, cg=cg_q):
            v.wait_ge(CG, cg)
            i1 = v.scalar_tensor_tensor(zk[:], zd[:], 0.5, zk[:],
                                        ALU.is_ge, ALU.add)
            B.ch("v", v, i1)
            return v.scalar_tensor_tensor(xqtm[fslot][:], zd[:], -0.5, zk[:],
                                          ALU.is_lt, ALU.subtract)
        cv_q_prev = B.ev("v", v_quant2, sem=CV, inc=1)

        def s_tpxq(s, j=j, fslot=fslot, cv=cv_q_prev):
            s.wait_ge(CV, cv)
            return s.dma_start(out=xqfm[:, :, j, :], in_=xqtm[fslot][:],
                               transpose=True)
        relQ_dt[fslot] = B.ev("s", s_tpxq, sem=DQ[fslot], inc=16)
    B.dbg("xqfm", xqfm)
    B.dbg("ninvs", ninvs)
    B.dbg("zsc", zsc)
    cv_q_all = cv_q_prev

    def t_invT(t):
        mm = None
        for m in range(8):
            mm = t.transpose(psB[0:1, m * 128:(m + 1) * 128], ninvs[:, m:m + 1],
                             ident[:])
        return mm
    ct_it = B.ev("t", t_invT, waits=[(CV, max(cv_q_all, cv_evac_prev))],
                 sem=CT, inc=1)

    def v_row(v):
        v.wait_ge(CT, ct_it)
        return v.tensor_copy(rowb[:], psB[0:1, 0:1024])
    cv_row = B.ev("v", v_row, sem=CV, inc=1)

    def t_bc(t):
        t.wait_ge(CV, cv_row)
        t.matmul(psA[:, 0:512], ones_row[:], rowb[0:1, 0:512], start=True, stop=True)
        return t.matmul(psA[:, 512:1024], ones_row[:], rowb[0:1, 512:1024],
                        start=True, stop=True)
    ct_bc = B.ev("t", t_bc, sem=CT, inc=1)

    def v_bc(v):
        v.wait_ge(CT, ct_bc)
        return v.tensor_copy(nb_full[:], psA[:, 0:1024])
    cv_bc = B.ev("v", v_bc, sem=CV, inc=1)
    B.dbg("nb_full", nb_full)

    barrier("pe")

    cs_tanh_prev = 0
    cv_up_evac = 0
    up_evac_par = [0, 0]
    cv_dn_evac = cv_bc
    wgn_rel = [B.val(CT)] * 4
    for qt in range(4):
        for m in range(16):
            mh = qt * 16 + m
            for (wi, dsrc) in ((0, d_wg), (1, d_wu)):
                rslot = (mh * 2 + wi) % 4
                dt_w = wload(
                    dsrc[mh * 128:(mh + 1) * 128, :],
                    wgn[rslot][:].rearrange("p a b -> p (a b)"),
                    waits=[(CT, wgn_rel[rslot])], sign=True)

                upps = psA if m % 2 == 0 else psB

                def t_up(t, wi=wi, rslot=rslot, dt=dt_w, upps=upps,
                         cvp=max(up_evac_par[m % 2], cv_dn_evac)):
                    t.wait_ge(dt[0], dt[1])
                    if cvp:
                        t.wait_ge(CV, cvp)
                    off = 0 if wi == 0 else 1024
                    mm = None
                    for k in range(16):
                        for nn2 in range(2):
                            mm = t.matmul(
                                upps[:, off + nn2 * 512:off + (nn2 + 1) * 512],
                                wgn[rslot][:, k, :],
                                xqfm[:, k, nn2 * 4:(nn2 + 1) * 4, :],
                                start=(k == 0), stop=(k == 15))
                    return mm
                wgn_rel[rslot] = B.ev("t", t_up, sem=CT, inc=1)
            ct_up = wgn_rel[(mh * 2 + 1) % 4]

            def v_eps1(v, ct=ct_up, upps=upps, csrel=cs_tanh_prev,
                       cvrel=cv_up_evac):
                v.wait_ge(CT, ct)
                if csrel:
                    v.wait_ge(CS, csrel)
                if cvrel:
                    v.wait_ge(CV, cvrel)
                v.tensor_mul(ffa[:], upps[:, 0:1024], nb_full[:])
                return v.tensor_mul(ffu[:], upps[:, 1024:2048], nb_full[:])
            cv_e1 = B.ev("v", v_eps1, waits=[(CV, cv_bc)], sem=CV, inc=1)

            def s_tanh2(s, cv=cv_e1, cvrel=cv_up_evac):
                s.wait_ge(CV, cv)
                if cvrel:
                    s.wait_ge(CV, cvrel)
                return s.activation(fft[:], ffa[:], AF.Tanh, scale=0.5)
            cs_tanh_prev = B.ev("sc", s_tanh2, sem=CS, inc=1)

            def v_eps2(v, m=m, cs=cs_tanh_prev):
                v.wait_ge(CS, cs)
                v.tensor_mul(ffu[:], ffa[:], ffu[:])
                i1 = v.tensor_scalar(fft[:], fft[:], 0.5, 0.5, ALU.mult, ALU.add)
                B.ch("v", v, i1)
                return v.tensor_mul(pq[:, m, :, :], fft[:], ffu[:])
            cv_up_evac = B.ev("v", v_eps2, sem=CV, inc=1)
            up_evac_par[m % 2] = cv_up_evac
        cv_p_done = cv_up_evac

        wdT_rel = [B.val(CT)] * 2
        for n in range(4):
            dslot = n % 2
            sl = (qt * 4 + n) * 128
            dts = []
            for hh2 in range(2):
                dts.append(wload(
                    d_wd[sl:sl + 128, hh2 * 4096:(hh2 + 1) * 4096],
                    wdT[dslot][:, 8 * hh2:8 * hh2 + 8, :, :].rearrange(
                        "p a b c -> p (a b c)"),
                    waits=[(CT, wdT_rel[dslot])], sign=True))

            for mg in range(2):
                def t_dn(t, qt=qt, n=n, dslot=dslot, mg=mg, dt=tuple(dts),
                         cvpd=cv_p_done, cvev=cv_dn_evac):
                    for (dsem, dval) in dt:
                        t.wait_ge(dsem, dval)
                    t.wait_ge(CV, max(cvpd, cvev))
                    mm = None
                    for m in range(4 * mg, 4 * mg + 4):
                        for k in range(16):
                            mm = t.matmul(psm(m), pq[:, k, m, :],
                                          wdT[dslot][:, k, :, :],
                                          start=(k == 0), stop=(k == 15))
                    return mm
                ct_dn = B.ev("t", t_dn, sem=CT, inc=1)
                if mg == 1:
                    wdT_rel[dslot] = ct_dn

                def v_evd(v, n=n, mg=mg, ct=ct_dn):
                    v.wait_ge(CT, ct)
                    mm = None
                    for m in range(4 * mg, 4 * mg + 4):
                        mm = v.tensor_add(oacc[:, m, n * 512:(n + 1) * 512],
                                          oacc[:, m, n * 512:(n + 1) * 512],
                                          psm(m))
                    return mm
                cv_dn_evac = B.ev("v", v_evd, sem=CV, inc=1)

    barrier("fin")

    cs_all_pre_final = B.val(CS)
    xrelF = [(cv_dn_evac, cs_all_pre_final)] * 2
    ost_rel = [0, 0]
    for m in range(NO):
        slot = m % 2
        oslot = m % 2

        def s_ldxf(s, m=m, slot=slot, rel=xrelF[slot], dxp=B.val(DXA[slot])):
            s.wait_ge(DXA[slot], dxp)
            if rel[0]:
                s.wait_ge(CV, rel[0])
            if rel[1]:
                s.wait_ge(CS, rel[1])
            return s.dma_start(out=xsC[slot][:], in_=d_xown[m * 128:(m + 1) * 128, :])
        dx = B.ev("s", s_ldxf, sem=DXA[slot], inc=16)

        def v_fin(v, m=m, slot=slot, oslot=oslot, dx=dx, rel=ost_rel[oslot]):
            v.wait_ge(DXA[slot], dx)
            if rel:
                v.wait_ge(DS[oslot], rel)
            return v.tensor_add(ost[oslot][:], oacc[:, m, :], xsC[slot][:])
        cv_f = B.ev("v", v_fin, waits=[(CV, cv_dn_evac)], sem=CV, inc=1)
        xrelF[slot] = (cv_f, 0)

        def s_st(s, m=m, oslot=oslot, cv=cv_f):
            s.wait_ge(CV, cv)
            return s.dma_start(out=d_out[m * 128:(m + 1) * 128, :], in_=ost[oslot][:])
        ost_rel[oslot] = B.ev("s", s_st, sem=DS[oslot], inc=16)

    ds_fin = (B.val(DS[0]), B.val(DS[1]))
    if B.dbg_bufs:
        allw = [(CV, B.val(CV)), (CS, B.val(CS)), (CT, B.val(CT)),
                (CG, B.val(CG)), (DS[0], ds_fin[0]), (DS[1], ds_fin[1])]

        def s_dbg(s):
            for (name, buf, d) in B.dbg_bufs:
                s.dma_start(out=d, in_=buf[:]).then_inc(DDB, 16)
            return None
        B.ev("s", s_dbg, waits=allw)
        B.cnt[DDB] = B.cnt.get(DDB, 0) + 16 * len(B.dbg_bufs)

    def s_end(s, v0=ds_fin[0], v1=ds_fin[1], vd=B.val(DDB)):
        s.wait_ge(DS[0], v0)
        s.wait_ge(DS[1], v1)
        if vd:
            s.wait_ge(DDB, vd)
        return None
    B.ev("s", s_end)

    total, remain = B.pack()
    B.pack_info = (total, remain)
    B.emit()
    return B


ZMAP = [[2 * j for j in range(NO)], [2 * j + 1 for j in range(NO)]]


def stage_weights(wq, wk, wv, wo, wgs, wgw, wg, wu, wd):
    f = np.float32
    return {
        "wq": np.ascontiguousarray(np.asarray(wq, f).T),
        "wk": np.ascontiguousarray(
            np.asarray(wk, f).T.reshape(16, 128, 4, 128)
            .transpose(1, 0, 2, 3).reshape(128, 8192)),
        "wv": np.ascontiguousarray(
            np.asarray(wv, f).T.reshape(16, 128, 4, 128)
            .transpose(1, 0, 2, 3).reshape(128, 8192)),
        "wo": np.ascontiguousarray(
            np.asarray(wo, f).T.reshape(16, 128, 4, 4, 128)
            .transpose(1, 2, 0, 3, 4).reshape(128, 32768)),
        "wgs": np.ascontiguousarray(
            np.concatenate([np.asarray(wgs, f).T, np.asarray(wgw, f).T],
                           axis=1)
            .reshape(16, 128, 32).transpose(1, 0, 2).reshape(128, 512)),
        "wg": np.ascontiguousarray(
            np.asarray(wg, f).T.reshape(16, 128, 64, 128)
            .transpose(2, 1, 0, 3).reshape(8192, 2048)),
        "wu": np.ascontiguousarray(
            np.asarray(wu, f).T.reshape(16, 128, 64, 128)
            .transpose(2, 1, 0, 3).reshape(8192, 2048)),
        "wd": np.ascontiguousarray(
            np.asarray(wd, f).T.reshape(4, 16, 128, 4, 4, 128)
            .transpose(0, 3, 2, 1, 4, 5).reshape(2048, 8192)),
    }


def kernel(L0_attn_hidden_states, L0_attn_Wq, L0_attn_Wk, L0_attn_Wv,
           L0_attn_Wo, L0_attn_Wkc, L0_attn_Wgslc, L0_attn_Wgswa,
           L0_ffn_W_g, L0_ffn_W_u, L0_ffn_W_d, _dbg_names=(), _cores=8):
    x = np.ascontiguousarray(np.asarray(L0_attn_hidden_states, dtype=np.float32))
    B = build(_dbg_names)
    wmap = stage_weights(L0_attn_Wq, L0_attn_Wk, L0_attn_Wv, L0_attn_Wo,
                         L0_attn_Wgslc, L0_attn_Wgswa, L0_ffn_W_g,
                         L0_ffn_W_u, L0_ffn_W_d)
    in_maps = []
    for c in range(_cores):
        b, par = c // 2, c % 2
        rows = ZMAP[par]
        xo = np.ascontiguousarray(
            x[b].reshape(NT, 128, DM)[rows].reshape(NO * 128, DM))
        im = {
            "xseq": np.ascontiguousarray(x[b]),
            "xown": xo,
            "pm": np.full((128, 1), float(par), np.float32),
            "pm1": np.full((128, 1), float(1 - par), np.float32),
            **wmap,
        }
        in_maps.append(im)
    res = run_bass_kernel_spmd(B.nc, in_maps, core_ids=list(range(_cores)))
    out = np.empty((4, SEQ, DM), res.results[0]["oown"].dtype)
    for c in range(_cores):
        b, par = c // 2, c % 2
        o = res.results[c]["oown"].reshape(NO, 128, DM)
        for j, g in enumerate(ZMAP[par]):
            out[b, g * 128:(g + 1) * 128, :] = o[j]
    if _dbg_names:
        return out, res
    return out


# version 15 (final)
# speedup vs baseline: 1.1976x; 1.0031x over previous
import numpy as np
import ml_dtypes
import concourse.bass as bass
import concourse.mybir as mybir
from concourse.bass_utils import run_bass_kernel_spmd

F32 = mybir.dt.float32
BF16 = mybir.dt.bfloat16
I32 = mybir.dt.int32
AF = mybir.ActivationFunctionType
ALU = mybir.AluOpType
AX = mybir.AxisListType

SEQ, DM, NT, NO = 2048, 2048, 16, 8
PAIR_TP = True
MASK_MM = True
H, HKV, DH, FH = 16, 4, 128, 8192
NEG = -3.0e38
ATT_SCALE = float(np.float32(1.0) / np.sqrt(np.float32(DH)))

TWO_PI = 2.0 * np.pi
P2_1 = 6.28125
_r = TWO_PI - P2_1
P2_2 = float(np.float32(int(_r * 2 ** 22) / 2 ** 22))
P2_3 = float(np.float32(_r - P2_2))
INV_2PI = float(np.float32(1.0 / TWO_PI))
PIH = np.pi / 2.0
PIH_HI = float(np.float32(int(PIH * 2 ** 20) / 2 ** 20))
PIH_LO = float(np.float32(PIH - PIH_HI))
L64 = float(np.log(10000.0) / 64.0)
L64_HI = float(np.float32(int(L64 * 2 ** 18) / 2 ** 18))
L64_LO = float(np.float32(L64 - L64_HI))


class Builder:
    def __init__(self, dbg_names=()):
        self.nc = bass.Bass("TRN2", target_bir_lowering=False)
        self.events = []
        self.cnt = {}
        self.dbg_names = set(dbg_names)
        self.dbg_bufs = []
        self._ctxs = []
        self._ivbufs = []

    def sb(self, name, shape, dtype, phases=None, **kw):
        if phases is not None and name in self.dbg_names:
            phases = (phases[0], 7)
        if phases is None:
            cm = self.nc.sbuf_tensor(name, shape, dtype, **kw)
            t = cm.__enter__()
            self._ctxs.append(cm)
            return t
        t = self.nc.virtual_sbuf_tensor(name, list(shape), dtype)
        bpp = int(np.prod(shape[1:])) * mybir.dt.size(dtype)
        self._ivbufs.append((t, list(shape), dtype, bpp, phases[0], phases[1]))
        return t

    def pack(self):
        free0 = (self.nc.sbuf_base + 511) & ~511
        cap = self.nc.sbuf_bytes_remaining - (free0 - self.nc.sbuf_base)
        placed = []
        assign = {}
        for (t, shape, dtype, bpp, lo, hi) in sorted(
                self._ivbufs, key=lambda x: -x[3]):
            size = (bpp + 511) & ~511
            cands = sorted({0} | {p[0] + p[1] for p in placed})
            off = None
            for c in cands:
                if c + size > cap:
                    continue
                ok = True
                for (p0, p1, p2, p3) in placed:
                    if not (hi < p2 or lo > p3):
                        if not (c + size <= p0 or c >= p0 + p1):
                            ok = False
                            break
                if ok:
                    off = c
                    break
            assert off is not None, \
                f"pack failed for {t.name} size {size} (cap {cap})"
            placed.append((off, size, lo, hi))
            assign[t.name] = (off, shape, dtype, t)
        total = max(p[0] + p[1] for p in placed)
        for name, (off, shape, dtype, t) in assign.items():
            bk = self.nc.alloc_sbuf_tensor_at(
                name + "_bk", shape, dtype, offset=free0 + off)
            self.nc.back_virtual_tensor_with(t, bk)
        return total, cap

    def ps(self, name, shape):
        cm = self.nc.psum_tensor(name, shape, F32)
        t = cm.__enter__()
        self._ctxs.append(cm)
        return t

    def sem(self, name):
        cm = self.nc.semaphore(name)
        s = cm.__enter__()
        self._ctxs.append(cm)
        return s

    def init_chains(self):
        self.chsem = {}
        self.chcnt = {}

    def mkchain(self, tag):
        self.chsem[tag] = self.sem("CH_" + tag)
        self.chcnt[tag] = 0

    def ch(self, tag, eng, inst):
        self.chcnt[tag] += 1
        inst.then_inc(self.chsem[tag], 1)
        eng.wait_ge(self.chsem[tag], self.chcnt[tag])
        return inst

    def ev(self, tag, fn, waits=(), sem=None, inc=0):
        self.events.append((tag, fn, tuple((s, v) for (s, v) in waits if v > 0),
                            sem, inc))
        if sem is not None and inc:
            self.cnt[sem] = self.cnt.get(sem, 0) + inc
        return self.cnt.get(sem, 0)

    def val(self, sem):
        return self.cnt.get(sem, 0)

    def dbg(self, name, buf):
        if name in self.dbg_names:
            shape = [buf.shape[0], int(np.prod(buf.shape[1:]))]
            d = self.nc.dram_tensor("dbg_" + name, shape, buf.dtype,
                                    kind="ExternalOutput").ap()
            self.dbg_bufs.append((name, buf, d))

    def emit(self):
        P = self
        with self.nc.Block() as block:
            @block.sync
            def _(s):
                run_events(s, P, "s")

            @block.gpsimd
            def _(g):
                run_events(g, P, "g")

            @block.vector
            def _(v):
                run_events(v, P, "v")

            @block.scalar
            def _(sc):
                run_events(sc, P, "sc")

            @block.tensor
            def _(t):
                run_events(t, P, "t")
        for cm in reversed(self._ctxs):
            cm.__exit__(None, None, None)


def run_events(eng, P, tag):
    for (etag, fn, waits, sem, inc) in P.events:
        if etag != tag:
            continue
        for (s, v) in waits:
            eng.wait_ge(s, v)
        inst = fn(eng)
        if sem is not None and inc and inst is not None:
            inst.then_inc(sem, inc)


def build(dbg_names=()):
    B = Builder(dbg_names)
    nc = B.nc

    d_xseq = nc.dram_tensor("xseq", [SEQ, DM], F32, kind="ExternalInput").ap()
    d_xown = nc.dram_tensor("xown", [NO * 128, DM], F32, kind="ExternalInput").ap()
    d_pm = nc.dram_tensor("pm", [128, 1], F32, kind="ExternalInput").ap()
    d_pm1 = nc.dram_tensor("pm1", [128, 1], F32, kind="ExternalInput").ap()
    d_wq = nc.dram_tensor("wq", [DM, DM], F32, kind="ExternalInput").ap()
    d_wk = nc.dram_tensor("wk", [128, 8192], F32, kind="ExternalInput").ap()
    d_wv = nc.dram_tensor("wv", [128, 8192], F32, kind="ExternalInput").ap()
    d_wo = nc.dram_tensor("wo", [128, 4 * 8192], F32, kind="ExternalInput").ap()
    d_wgs = nc.dram_tensor("wgs", [128, 512], F32, kind="ExternalInput").ap()
    d_wg = nc.dram_tensor("wg", [FH, DM], F32, kind="ExternalInput").ap()
    d_wu = nc.dram_tensor("wu", [FH, DM], F32, kind="ExternalInput").ap()
    d_wd = nc.dram_tensor("wd", [2048, FH], F32, kind="ExternalInput").ap()
    d_out = nc.dram_tensor("oown", [NO * 128, DM], F32, kind="ExternalOutput").ap()

    DXA = [B.sem("DXA0"), B.sem("DXA1")]
    DW = [B.sem("DW0"), B.sem("DW1"), B.sem("DW2"), B.sem("DW3")]
    DF = [B.sem("DF0"), B.sem("DF1"), B.sem("DF2")]
    DQ = [B.sem("DQ0"), B.sem("DQ1")]
    DK = [B.sem("DK0"), B.sem("DK1")]
    DP = [B.sem("DP0"), B.sem("DP1")]
    DS = [B.sem("DS0"), B.sem("DS1")]
    DDB = B.sem("DDB")
    B.init_chains()
    for _tag in ("v", "sc", "g"):
        B.mkchain(_tag)
    CV = B.sem("CV")
    CS = B.sem("CS")
    CT = B.sem("CT")
    CG = B.sem("CG")

    psA = B.ps("psA", [128, 2048])
    psB = B.ps("psB", [128, 2048])

    ALLSEMS = DXA + DW + DF + DQ + DK + DP + DS + [CV, CS, CT, CG]

    def barrier(name):
        snap = [(sm, B.val(sm)) for sm in ALLSEMS if B.val(sm) > 0]
        for tag in ("s", "g", "v", "sc", "t"):
            B.ev(tag, (lambda e: None), waits=snap)

    ones_row = B.sb("ones_row", [1, 128], F32)
    ident = B.sb("ident", [128, 128], F32)
    idbf = B.sb("idbf", [128, 128], BF16)
    mdiag = B.sb("mdiag", [128, 128], F32)
    mpre = B.sb("mpre", [128, 128], F32)
    mlast = B.sb("mlast", [128, 128], F32)
    bmask2 = B.sb("bmask2", [128, 256], BF16)
    bmask4 = B.sb("bmask4", [128, 512], BF16)
    pmb = B.sb("pmb", [128, 1], F32)
    pm1b = B.sb("pm1b", [128, 1], F32)
    pm1neg = B.sb("pm1neg", [128, 1], F32)
    wrow = B.sb("wrow", [1, 64], F32)
    omega = B.sb("omega", [128, 64], F32)
    tgi = B.sb("tgi", [128, 64], F32)
    rr1 = B.sb("rr1", [128, 64], F32)
    rr2 = B.sb("rr2", [128, 64], F32)
    rri = B.sb("rri", [128, 64], I32)
    rrk = B.sb("rrk", [128, 64], F32)
    zbQ = B.sb("zbQ", [128, 256], F32)
    sums = B.sb("sums", [128, 24], F32)
    sumsq = B.sb("sumsq", [128, 24], F32)
    mean_ = B.sb("mean_", [128, 24], F32)
    var_ = B.sb("var_", [128, 24], F32)
    rstd = B.sb("rstd", [128, 24], F32)
    nrs = B.sb("nrs", [128, 24], F32)
    bias_ = B.sb("bias_", [128, 24], F32)
    var2 = B.sb("var2", [128, 8], F32)
    rstd2 = B.sb("rstd2", [128, 8], F32)
    n8 = B.sb("n8", [128, 8], F32)
    amax = B.sb("amax", [128, 8], F32)
    zsc = B.sb("zsc", [128, 8], F32)
    ninvs = B.sb("ninvs", [128, 8], F32)
    gtm = B.sb("gtm", [128, NO, H], F32)
    gsig = B.sb("gsig", [128, 32], F32)
    lsum = B.sb("lsum", [128, 1], F32)
    mrow = B.sb("mrow", [128, 2], F32)
    lsum2 = B.sb("lsum2", [128, 2], F32)
    mrow2 = B.sb("mrow2", [128, 6], F32)

    costm = B.sb("costm", [128, NT, 64], F32, phases=(0, 2))
    sintm = B.sb("sintm", [128, NT, 64], F32, phases=(0, 2))
    wkT = B.sb("wkT", [128, 16, 4, 128], BF16, phases=(0, 3))
    wvT = B.sb("wvT", [128, 16, 4, 128], BF16, phases=(0, 3))
    wgT = B.sb("wgT", [128, 16, 32], BF16, phases=(0, 2))
    sqs = B.sb("sqs", [128, DM], F32, phases=(0, 0))
    xsA = [B.sb(f"xsA{i}", [128, DM], F32, phases=(0, 2)) for i in range(2)]
    wqTh = B.sb("wqTh", [128, 16, 16, 128], BF16, phases=(0, 1))
    xnbo = [B.sb(f"xnbo{i}", [128, DM], BF16, phases=(1, 1)) for i in range(2)]
    xfm = [B.sb(f"xfm{i}", [128, NT, 128], BF16, phases=(1, 2)) for i in range(3)]
    qtm = [B.sb(f"qtm{i}", [128, 2048], BF16, phases=(1, 1)) for i in range(2)]
    crep = B.sb("crep", [128, 1024], F32, phases=(0, 2))
    srep = B.sb("srep", [128, 1024], F32, phases=(0, 2))
    zbW = B.sb("zbW", [128, 2048], F32, phases=(0, 2))
    zbW2 = B.sb("zbW2", [128, 1024], F32, phases=(0, 0))
    rriW = B.sb("rriW", [128, 1024], I32, phases=(0, 0))
    qfm = B.sb("qfm", [128, NO, H, 128], BF16, phases=(1, 3))
    xnb = [B.sb(f"xnb{i}", [128, DM], BF16, phases=(2, 2)) for i in range(3)]
    ktm = [B.sb(f"ktm{i}", [128, 512], BF16, phases=(2, 2)) for i in range(2)]
    kfm = B.sb("kfm", [128, HKV, NT, 128], BF16, phases=(2, 3))
    vtm = B.sb("vtm", [128, NT, 512], BF16, phases=(2, 3))
    pexp = [B.sb(f"pexp{i}", [128, 2048], F32, phases=(3, 3)) for i in range(2)]
    pbf2 = [B.sb(f"pbf2_{i}", [128, 4096], BF16, phases=(3, 3)) for i in range(2)]
    PT = [B.sb(f"PT{i}", [128, 4, 16, 128], BF16, phases=(3, 3))
          for i in range(2)]
    attn = B.sb("attn", [128, 16, NO, 128], BF16, phases=(3, 4))
    oacc = B.sb("oacc", [128, NO, 2048], F32, phases=(4, 7))
    woT = [B.sb(f"woT{i}", [128, 16, 4, 128], BF16, phases=(4, 4))
           for i in range(2)]
    xsB = [B.sb(f"xsB{i}", [128, DM], F32, phases=(5, 5)) for i in range(2)]
    xnf = [B.sb(f"xnf{i}", [128, DM], F32, phases=(5, 5)) for i in range(2)]
    zb = B.sb("zb", [128, DM], F32, phases=(5, 5))
    zi = B.sb("zi", [128, DM], I32, phases=(5, 5))
    zk = B.sb("zk", [128, DM], F32, phases=(5, 5))
    zd = B.sb("zd", [128, DM], F32, phases=(5, 5))
    xqtm = [B.sb(f"xqtm{i}", [128, DM], BF16, phases=(5, 5)) for i in range(2)]
    rowb = B.sb("rowb", [1, 1024], F32, phases=(5, 5))
    xqfm = B.sb("xqfm", [128, 16, NO, 128], BF16, phases=(5, 6))
    nb_full = B.sb("nb_full", [128, 1024], F32, phases=(5, 6))
    pq = B.sb("pq", [128, 16, NO, 128], BF16, phases=(6, 6))
    wgn = [B.sb(f"wgn{i}", [128, 16, 128], BF16, phases=(6, 6)) for i in range(4)]
    wdT = [B.sb(f"wdT{i}", [128, 16, 4, 128], BF16, phases=(6, 6)) for i in range(2)]
    ffa = B.sb("ffa", [128, 1024], F32, phases=(6, 6))
    fft = B.sb("fft", [128, 1024], F32, phases=(6, 6))
    ffu = B.sb("ffu", [128, 1024], F32, phases=(6, 6))
    xsC = [B.sb(f"xsC{i}", [128, DM], F32, phases=(7, 7)) for i in range(2)]
    ost = [B.sb(f"ost{i}", [128, DM], F32, phases=(7, 7)) for i in range(2)]

    def g_const(g):
        g.memset(ones_row[:], 1.0)
        g.memset(mdiag[:], 0.0)
        g.memset(ident[:], 1.0)
        g.drain()
        g.affine_select(mdiag[:], mdiag[:], [[-1, 128]], ALU.is_ge, NEG,
                        base=0, channel_multiplier=1)
        g.affine_select(ident[:], ident[:], [[-1, 128]], ALU.is_equal, 0.0,
                        base=0, channel_multiplier=1)
        g.iota(wrow[:], [[1, 64]], base=0, channel_multiplier=0,
               allow_small_or_imprecise_dtypes=True)
        g.iota(srep[:, 0:1024].rearrange("p (a b) -> p a b", b=64),
               [[128, 16], [0, 64]], base=0, channel_multiplier=1,
               allow_small_or_imprecise_dtypes=True)
        return g.drain()
    cg_const = B.ev("g", g_const, sem=CG, inc=1)

    def s_pm(s):
        s.dma_start(out=pmb[:], in_=d_pm).then_inc(DXA[0], 16)
        s.dma_start(out=pm1b[:], in_=d_pm1).then_inc(DXA[1], 16)
        return None
    B.ev("s", s_pm)
    B.cnt[DXA[0]] = B.cnt.get(DXA[0], 0) + 16
    B.cnt[DXA[1]] = B.cnt.get(DXA[1], 0) + 16

    def v_masks(v):
        v.tensor_scalar_mul(pm1neg[:], pm1b[:], NEG)
        v.tensor_scalar_mul(mpre[:], mdiag[:], pm1b[:, 0:1])
        v.tensor_scalar_mul(mlast[:], mdiag[:], pmb[:, 0:1])
        v.tensor_copy(idbf[:], ident[:])
        v.drain()
        v.tensor_scalar_add(mlast[:], mlast[:], pm1neg[:, 0:1])
        v.tensor_copy(bmask2[:, 0:128], mpre[:])
        v.memset(bmask4[:, 0:256], 0.0)
        v.tensor_scalar_mul(rr1[0:1, :], wrow[:], -L64_HI)
        v.drain()
        v.tensor_copy(bmask2[:, 128:256], mlast[:])
        v.tensor_copy(bmask4[:, 256:384], mpre[:])
        v.tensor_copy(bmask4[:, 384:512], mlast[:])
        v.scalar_tensor_tensor(rr1[0:1, :], wrow[:], -L64_LO, rr1[0:1, :],
                               ALU.mult, ALU.add)
        return v.drain()
    cv_masks = B.ev("v", v_masks,
                    waits=[(CG, cg_const), (DXA[0], 16), (DXA[1], 16)],
                    sem=CV, inc=1)

    def s_omega(s):
        return s.activation(wrow[:], rr1[0:1, :], AF.Exp)
    cs_omega = B.ev("sc", s_omega, waits=[(CV, cv_masks)], sem=CS, inc=1)

    def t_omega(t):
        return t.matmul(psB[:, 0:64], ones_row[:], wrow[:], start=True, stop=True)
    ct_omega = B.ev("t", t_omega, waits=[(CS, cs_omega)], sem=CT, inc=1)

    def v_omega_ev(v):
        return v.tensor_copy(omega[:], psB[:, 0:64])
    cv_omega = B.ev("v", v_omega_ev, waits=[(CT, ct_omega)], sem=CV, inc=1)

    def rr_chain(v, src):
        for op in (
            lambda: v.tensor_scalar(rr2[:], src, INV_2PI, 0.5, ALU.mult, ALU.add),
            lambda: v.tensor_copy(rri[:], rr2[:]),
            lambda: v.tensor_copy(rrk[:], rri[:]),
            lambda: v.tensor_sub(rr2[:], rr2[:], rrk[:]),
            lambda: v.tensor_scalar(rr2[:], rr2[:], 0.0, None, ALU.is_lt),
            lambda: v.tensor_sub(rrk[:], rrk[:], rr2[:]),
            lambda: v.scalar_tensor_tensor(rr2[:], rrk[:], -P2_1, src,
                                           ALU.mult, ALU.add),
            lambda: v.scalar_tensor_tensor(rr2[:], rrk[:], -P2_2, rr2[:],
                                           ALU.mult, ALU.add),
            lambda: v.scalar_tensor_tensor(rr2[:], rrk[:], -P2_3, rr2[:],
                                           ALU.mult, ALU.add),
        ):
            B.ch("v", v, op())

    def v_tables(v):
        i0 = v.tensor_copy(crep[:, 0:64], omega[:])
        B.ch("v", v, i0)
        for w in (64, 128, 256, 512):
            B.ch("v", v, v.tensor_copy(crep[:, w:2 * w], crep[:, 0:w]))
        B.ch("v", v, v.tensor_mul(zbW[:, 0:1024], srep[:, 0:1024], crep[:, 0:1024]))
        return v.engine_nop()
    cv_tab0 = B.ev("v", v_tables, waits=[(CV, cv_omega), (CG, B.val(CG))],
                   sem=CV, inc=1)

    def rr_chain_w(v, src, uW, riW, rkW):
        for op in (
            lambda: v.tensor_scalar(uW, src, INV_2PI, 0.5, ALU.mult, ALU.add),
            lambda: v.tensor_copy(riW, uW),
            lambda: v.tensor_copy(rkW, riW),
            lambda: v.tensor_sub(uW, uW, rkW),
            lambda: v.tensor_scalar(uW, uW, 0.0, None, ALU.is_lt),
            lambda: v.tensor_sub(rkW, rkW, uW),
            lambda: v.scalar_tensor_tensor(uW, rkW, -P2_1, src, ALU.mult, ALU.add),
            lambda: v.scalar_tensor_tensor(uW, rkW, -P2_2, uW, ALU.mult, ALU.add),
            lambda: v.scalar_tensor_tensor(uW, rkW, -P2_3, uW, ALU.mult, ALU.add),
        ):
            B.ch("v", v, op())

    def v_tab_sin(v):
        rr_chain_w(v, zbW[:, 0:1024], zbW[:, 1024:2048], rriW[:], zbW2[:])
        return v.engine_nop()
    cv_tsin = B.ev("v", v_tab_sin, sem=CV, inc=1)

    def s_tab_sin(s):
        return s.activation(sintm[:].rearrange("p a b -> p (a b)"),
                            zbW[:, 1024:2048], AF.Sin)
    cs_tsin = B.ev("sc", s_tab_sin, waits=[(CV, cv_tsin)], sem=CS, inc=1)

    def v_tab_cos(v):
        B.ch("v", v, v.tensor_scalar_add(zbW[:, 0:1024], zbW[:, 0:1024], PIH_HI))
        B.ch("v", v, v.tensor_scalar_add(zbW[:, 0:1024], zbW[:, 0:1024], PIH_LO))
        rr_chain_w(v, zbW[:, 0:1024], zbW[:, 1024:2048], rriW[:], zbW2[:])
        return v.engine_nop()
    cv_tcos = B.ev("v", v_tab_cos, waits=[(CS, cs_tsin)], sem=CV, inc=1)

    def s_tab_cos(s):
        return s.activation(costm[:].rearrange("p a b -> p (a b)"),
                            zbW[:, 1024:2048], AF.Sin)
    B.ev("sc", s_tab_cos, waits=[(CV, cv_tcos)], sem=CS, inc=1)

    B.dbg("sintm", sintm)
    B.dbg("costm", costm)

    wl_prev = [0, 0, 0, 0]

    class WS:
        idx = 0

    def wload(src, dst, sdst=None, waits=(), sign=False):
        slot = WS.idx % 4
        WS.idx += 1

        def g_w(g, src=src, dst=dst, slot=slot, prev=wl_prev[slot],
                waits=tuple(waits)):
            g.wait_ge(DW[slot], prev)
            for (sm, vv) in waits:
                if vv:
                    g.wait_ge(sm, vv)
            return g.dma_start(out=dst, in_=src)
        dw = B.ev("g", g_w, sem=DW[slot], inc=16)
        wl_prev[slot] = dw
        if sign:
            def s_sg(s, d2=(sdst if sdst is not None else dst), slot=slot,
                     dw=dw):
                s.wait_ge(DW[slot], dw)
                return s.activation(d2, d2, AF.Sign)
            return (CS, B.ev("sc", s_sg, sem=CS, inc=1))
        return (DW[slot], dw)

    wload(d_wgs, wgT[:].rearrange("p a b -> p (a b)"))
    for k in range(16):
        wload(d_wq[k * 128:(k + 1) * 128, :],
              wqTh[:, k, :, :].rearrange("p a b -> p (a b)"))
    dw_wq_all = tuple(B.val(x) for x in DW)
    for h in range(2):
        wload(d_wk[:, h * 4096:(h + 1) * 4096],
              wkT[:, 8 * h:8 * h + 8, :, :].rearrange("p a b c -> p (a b c)"))
        wload(d_wv[:, h * 4096:(h + 1) * 4096],
              wvT[:, 8 * h:8 * h + 8, :, :].rearrange("p a b c -> p (a b c)"))
    DW_attw = tuple(B.val(x) for x in DW)
    B.dbg("wkT", wkT)

    xsrc = [(d_xseq, t, t) for t in range(NT)] + \
           [(d_xown, t, NT + t) for t in range(NO)]
    xrel = [(0, 0)] * 2
    cs_sq_prev = 0
    for idx, (src, t, col) in enumerate(xsrc):
        slot = idx % 2

        def s_ldx(s, src=src, t=t, slot=slot, rel=xrel[slot],
                  dxp=B.val(DXA[slot])):
            s.wait_ge(DXA[slot], dxp)
            if rel[0]:
                s.wait_ge(CV, rel[0])
            if rel[1]:
                s.wait_ge(CS, rel[1])
            return s.dma_start(out=xsA[slot][:], in_=src[t * 128:(t + 1) * 128, :])
        dx = B.ev("s", s_ldx, sem=DXA[slot], inc=16)

        def v_sum(v, slot=slot, col=col):
            return v.reduce_sum(sums[:, col:col + 1], xsA[slot][:], axis=AX.X)
        cv_h = B.ev("v", v_sum, waits=[(DXA[slot], dx)], sem=CV, inc=1)

        def s_sq(s, slot=slot, col=col, relsq=cs_sq_prev):
            if relsq:
                s.wait_ge(CS, relsq)
            return s.activation(sqs[:], xsA[slot][:], AF.Square,
                                accum_out=sumsq[:, col:col + 1])
        cs_h = B.ev("sc", s_sq, waits=[(DXA[slot], dx)], sem=CS, inc=1)
        cs_sq_prev = cs_h
        xrel[slot] = (cv_h, cs_h)

    def nr_rsqrt(v, r, vv, scratch, iters=6):
        v.memset(r, 1.0)
        v.drain()
        for _ in range(iters):
            v.tensor_mul(scratch, r, r)
            v.drain()
            v.tensor_mul(scratch, scratch, vv)
            v.drain()
            v.tensor_scalar(scratch, scratch, -0.5, 1.5, ALU.mult, ALU.add)
            v.drain()
            v.tensor_mul(r, r, scratch)
            v.drain()

    def v_stats(v):
        v.tensor_scalar_mul(mean_[:], sums[:], 1.0 / DM)
        v.tensor_scalar_mul(var_[:], sumsq[:], 1.0 / DM)
        v.drain()
        v.tensor_mul(nrs[:], mean_[:], mean_[:])
        v.drain()
        v.tensor_sub(var_[:], var_[:], nrs[:])
        v.drain()
        v.tensor_scalar_add(var_[:], var_[:], 1e-5)
        v.drain()
        nr_rsqrt(v, rstd[:], var_[:], nrs[:])
        v.tensor_mul(bias_[:], mean_[:], rstd[:])
        v.drain()
        v.tensor_scalar_mul(bias_[:], bias_[:], -1.0)
        v.tensor_scalar_add(var2[:], var_[:, 16:24], -1e-5)
        v.drain()
        v.tensor_mul(n8[:], rstd[:, 16:24], rstd[:, 16:24])
        v.drain()
        v.tensor_mul(var2[:], var2[:], n8[:])
        v.drain()
        v.tensor_scalar_add(var2[:], var2[:], 1e-8)
        v.drain()
        nr_rsqrt(v, rstd2[:], var2[:], n8[:])
        return v.drain()
    cv_stats = B.ev("v", v_stats,
                    waits=[(CV, B.val(CV)), (CS, B.val(CS))], sem=CV, inc=1)
    B.dbg("rstd", rstd)
    B.dbg("rstd2", rstd2)
    B.dbg("bias_", bias_)

    barrier("pc")
    relC_ct = [0, 0]
    relC_dtq = [0, 0]
    rope_hist = []
    cs_gate_prev = 0
    xrelC = list(xrel)
    cs_xnbo = {}
    xnbo_rel = [0, 0]
    for j in range(NO):
        fslot = j % 2
        slot = j % 2
        cv_rope_m2 = rope_hist[-2] if len(rope_hist) >= 2 else 0

        def s_ldxo(s, j=j, slot=slot, rel=xrelC[slot], dxp=B.val(DXA[slot])):
            s.wait_ge(DXA[slot], dxp)
            if rel[0]:
                s.wait_ge(CV, rel[0])
            if rel[1]:
                s.wait_ge(CS, rel[1])
            return s.dma_start(out=xsA[slot][:],
                               in_=d_xown[j * 128:(j + 1) * 128, :])
        dx = B.ev("s", s_ldxo, sem=DXA[slot], inc=16)

        def s_xn(s, j=j, slot=slot, fslot=fslot, dx=dx,
                 relb=xnbo_rel[fslot]):
            s.wait_ge(DXA[slot], dx)
            if relb:
                s.wait_ge(DF[fslot], relb)
            return s.activation(xnbo[fslot][:], xsA[slot][:], AF.Identity,
                                bias=bias_[:, 16 + j:17 + j],
                                scale=rstd[:, 16 + j:17 + j])
        cs_xn = B.ev("sc", s_xn, waits=[(CV, cv_stats)], sem=CS, inc=1)
        cs_xnbo[j] = cs_xn
        xrelC[slot] = (0, cs_xn)

        def s_tpxc(s, j=j, fslot=fslot, relct=relC_ct[fslot]):
            s.wait_ge(CS, cs_xnbo[j])
            if relct:
                s.wait_ge(CT, relct)
            return s.dma_start(out=xfm[fslot][:, 0:16, :], in_=xnbo[fslot][:],
                               transpose=True)
        dt_x = B.ev("s", s_tpxc, sem=DF[fslot], inc=16)
        xnbo_rel[fslot] = dt_x

        def t_q(t, j=j, fslot=fslot, dt=dt_x, cvprev=cv_rope_m2,
                csg_prev=cs_gate_prev, dtw=dw_wq_all):
            t.wait_ge(DF[fslot], dt)
            for _i in range(4):
                t.wait_ge(DW[_i], dtw[_i])
            if cvprev:
                t.wait_ge(CV, cvprev)
            if csg_prev:
                t.wait_ge(CS, csg_prev)
            mm = None
            for k in range(16):
                lhsT = xfm[fslot][:, k, :]
                for nn in range(4):
                    mm = t.matmul(psA[:, nn * 512:(nn + 1) * 512],
                                  lhsT, wqTh[:, k, nn * 4:(nn + 1) * 4, :],
                                  start=(k == 0), stop=(k == 15))
                mm = t.matmul(psB[:, 1792:1824], lhsT, wgT[:, k, :],
                              start=(k == 0), stop=(k == 15))
            return mm
        ct_q = B.ev("t", t_q, sem=CT, inc=1)
        relC_ct[fslot] = ct_q

        def s_gate(s, j=j, ct=ct_q):
            s.wait_ge(CT, ct)
            return s.activation(gsig[:], psB[:, 1792:1824], AF.Tanh, scale=0.5)
        cs_gate = B.ev("sc", s_gate, sem=CS, inc=1)
        cs_gate_prev = cs_gate

        def v_rope_q(v, j=j, fslot=fslot, ct=ct_q, csg=cs_gate,
                     reldq=relC_dtq[fslot]):
            if reldq:
                v.wait_ge(DQ[fslot], reldq)
            v.wait_ge(CT, ct)
            v.tensor_scalar_mul(crep[:, 0:64], costm[:, 2 * j, :], pm1b[:, 0:1])
            i1 = v.tensor_scalar_mul(srep[:, 0:64], sintm[:, 2 * j, :],
                                     pm1b[:, 0:1])
            B.ch("v", v, i1)
            v.scalar_tensor_tensor(crep[:, 0:64], costm[:, 2 * j + 1, :],
                                   pmb[:, 0:1], crep[:, 0:64], ALU.mult, ALU.add)
            i2 = v.scalar_tensor_tensor(srep[:, 0:64], sintm[:, 2 * j + 1, :],
                                        pmb[:, 0:1], srep[:, 0:64],
                                        ALU.mult, ALU.add)
            B.ch("v", v, i2)
            for w in (64, 128, 256, 512):
                v.tensor_copy(crep[:, w:2 * w], crep[:, 0:w])
                i3 = v.tensor_copy(srep[:, w:2 * w], srep[:, 0:w])
                B.ch("v", v, i3)
            ps3 = psA[:].rearrange("p (h t) -> p h t", t=128)
            x1 = ps3[:, :, 0:64]
            x2 = ps3[:, :, 64:128]
            c3 = crep[:].rearrange("p (h t) -> p h t", t=64)
            s3 = srep[:].rearrange("p (h t) -> p h t", t=64)
            o3 = qtm[fslot][:].rearrange("p (h t) -> p h t", t=128)
            v.tensor_mul(zbW[:, 0:1024], x1, c3)
            i4 = v.tensor_mul(zbW[:, 1024:2048], x2, s3)
            B.ch("v", v, i4)
            i4b = v.tensor_sub(
                o3[:, :, 0:64],
                zbW[:, 0:1024].rearrange("p (h t) -> p h t", t=64),
                zbW[:, 1024:2048].rearrange("p (h t) -> p h t", t=64))
            B.ch("v", v, i4b)
            v.tensor_mul(zbW[:, 0:1024], x2, c3)
            i5 = v.tensor_mul(zbW[:, 1024:2048], x1, s3)
            B.ch("v", v, i5)
            out = v.tensor_add(
                o3[:, :, 64:128],
                zbW[:, 0:1024].rearrange("p (h t) -> p h t", t=64),
                zbW[:, 1024:2048].rearrange("p (h t) -> p h t", t=64))
            v.wait_ge(CS, csg)
            i6 = v.tensor_add(gsig[:, 0:16], gsig[:, 0:16], gsig[:, 16:32])
            B.ch("v", v, i6)
            out = v.tensor_scalar(gtm[:, j, :], gsig[:, 0:16], 0.5, 1.0,
                                  ALU.mult, ALU.add)
            return out
        cv_rope = B.ev("v", v_rope_q, sem=CV, inc=1)
        rope_hist.append(cv_rope)

        def s_tpq(s, j=j, fslot=fslot, cv=cv_rope):
            s.wait_ge(CV, cv)
            return s.dma_start(out=qfm[:, j, :, :], in_=qtm[fslot][:],
                               transpose=True)
        relC_dtq[fslot] = B.ev("s", s_tpq, sem=DQ[fslot], inc=16)

    B.dbg("qfm", qfm)
    B.dbg("gtm", gtm)

    barrier("pb")

    xrelB = list(xrelC)
    cv_kv_par = [0, 0]
    relB_dtx = [0, 0, 0]
    relB_ctx = list(relC_ct) + [0]
    relB_dtk = [0, 0]
    for t_i in range(NT):
        slot = t_i % 2
        fslot = t_i % 3

        def s_ldxs(s, t_i=t_i, slot=slot, rel=xrelB[slot], dxp=B.val(DXA[slot])):
            s.wait_ge(DXA[slot], dxp)
            if rel[0]:
                s.wait_ge(CV, rel[0])
            if rel[1]:
                s.wait_ge(CS, rel[1])
            return s.dma_start(out=xsA[slot][:],
                               in_=d_xseq[t_i * 128:(t_i + 1) * 128, :])
        dx = B.ev("s", s_ldxs, sem=DXA[slot], inc=16)

        def s_xn2(s, t_i=t_i, slot=slot, fslot=fslot, dx=dx, rel=relB_dtx[fslot]):
            s.wait_ge(DXA[slot], dx)
            if rel:
                s.wait_ge(DF[fslot], rel)
            return s.activation(xnb[fslot][:], xsA[slot][:], AF.Identity,
                                bias=bias_[:, t_i:t_i + 1],
                                scale=rstd[:, t_i:t_i + 1])
        cs_xn = B.ev("sc", s_xn2, waits=[(CV, cv_stats)], sem=CS, inc=1)
        xrelB[slot] = (0, cs_xn)

        def s_tpx2(s, fslot=fslot, cs=cs_xn, relct=relB_ctx[fslot]):
            s.wait_ge(CS, cs)
            if relct:
                s.wait_ge(CT, relct)
            return s.dma_start(out=xfm[fslot][:], in_=xnb[fslot][:], transpose=True)
        dt_x = B.ev("s", s_tpx2, sem=DF[fslot], inc=16)
        relB_dtx[fslot] = dt_x

        def t_kv(t, t_i=t_i, fslot=fslot, dt=dt_x, cvprev=cv_kv_par[t_i % 2]):
            t.wait_ge(DF[fslot], dt)
            for _i in range(4):
                t.wait_ge(DW[_i], DW_attw[_i])
            if cvprev:
                t.wait_ge(CV, cvprev)
            off = 0 if (t_i % 2 == 0) else 1024
            mm = None
            for k in range(16):
                lhsT = xfm[fslot][:, k, :]
                t.matmul(psB[:, off:off + 512], lhsT, wkT[:, k, :, :],
                         start=(k == 0), stop=(k == 15))
                mm = t.matmul(psB[:, off + 512:off + 1024], lhsT, wvT[:, k, :, :],
                              start=(k == 0), stop=(k == 15))
            return mm
        ct_kv = B.ev("t", t_kv, sem=CT, inc=1)
        relB_ctx[fslot] = ct_kv

        kslot = t_i % 2

        def v_kv(v, t_i=t_i, kslot=kslot, ct=ct_kv, rel=relB_dtk[kslot]):
            v.wait_ge(CT, ct)
            if rel:
                v.wait_ge(DK[kslot], rel)
            off = 0 if (t_i % 2 == 0) else 1024
            v.tensor_copy(crep[:, 0:64], costm[:, t_i, :])
            i1 = v.tensor_copy(srep[:, 0:64], sintm[:, t_i, :])
            B.ch("v", v, i1)
            for w in (64, 128):
                v.tensor_copy(crep[:, w:2 * w], crep[:, 0:w])
                i2 = v.tensor_copy(srep[:, w:2 * w], srep[:, 0:w])
                B.ch("v", v, i2)
            ps3 = psB[:, off:off + 512].rearrange("p (h t) -> p h t", t=128)
            x1 = ps3[:, :, 0:64]
            x2 = ps3[:, :, 64:128]
            c3 = crep[:, 0:256].rearrange("p (h t) -> p h t", t=64)
            s3 = srep[:, 0:256].rearrange("p (h t) -> p h t", t=64)
            o3 = ktm[kslot][:].rearrange("p (h t) -> p h t", t=128)
            v.tensor_mul(zbW[:, 0:256], x1, c3)
            v.tensor_mul(zbW[:, 512:768], x2, s3)
            v.tensor_mul(zbW[:, 1024:1280], x2, c3)
            i3 = v.tensor_mul(zbW[:, 1536:1792], x1, s3)
            B.ch("v", v, i3)
            v.tensor_sub(o3[:, :, 0:64],
                         zbW[:, 0:256].rearrange("p (h t) -> p h t", t=64),
                         zbW[:, 512:768].rearrange("p (h t) -> p h t", t=64))
            v.tensor_add(o3[:, :, 64:128],
                         zbW[:, 1024:1280].rearrange("p (h t) -> p h t", t=64),
                         zbW[:, 1536:1792].rearrange("p (h t) -> p h t", t=64))
            return v.tensor_copy(vtm[:, t_i, :], psB[:, off + 512:off + 1024])
        cv_kv = B.ev("v", v_kv, sem=CV, inc=1)
        cv_kv_par[t_i % 2] = cv_kv

        def s_tpk(s, t_i=t_i, kslot=kslot, cv=cv_kv):
            s.wait_ge(CV, cv)
            return s.dma_start(out=kfm[:, :, t_i, :], in_=ktm[kslot][:],
                               transpose=True)
        relB_dtk[kslot] = B.ev("s", s_tpk, sem=DK[kslot], inc=16)

    B.dbg("kfm", kfm)
    B.dbg("vtm", vtm)
    DKQ_all = [(DK[0], B.val(DK[0])), (DK[1], B.val(DK[1])),
               (DQ[0], B.val(DQ[0])), (DQ[1], B.val(DQ[1]))]
    CV_kv_all = B.val(CV)

    barrier("pd")

    def v_pdinit(v):
        v.memset(pbf2[0][:], 0.0)
        return v.memset(pbf2[1][:], 0.0)
    cv_pdinit = B.ev("v", v_pdinit, sem=CV, inc=1)

    cs_exp_prev = 0
    cs_exp_byreg = [0, 0]
    cv_sc_slot = [0, 0]
    dp_slot = [0, 0]
    pt_rel = [0, 0]
    cv_pv_prev = 0
    for j in range(NO):
        nch = 2 * j + 2
        keys = 128 * nch
        for g in range(HKV):
            tslot = (j * HKV + g) % 2
            dtp_pair = []
            for hp in range(2):
                pslot2 = (j * 8 + g * 2 + hp) % 2
                cv_sc_pair = 0
                for h2 in range(2):
                    hh = hp * 2 + h2
                    h = HKV * g + hh
                    pslot = h % 2
                    reg = 1 if (h % 2 == 1 and j <= 5) else 0
                    Sps = (psA[:, 0:keys] if reg == 0
                           else psB[:, 512:512 + keys])

                    def t_S(t, j=j, h=h, g=g, nch=nch, Sps=Sps,
                            cse=cs_exp_byreg[reg]):
                        if cse:
                            t.wait_ge(CS, cse)
                        lhsT = qfm[:, j, h, :]
                        for c0 in range(0, nch, 4):
                            cw = min(4, nch - c0)
                            if c0 + cw < nch or not MASK_MM:
                                mm = t.matmul(
                                    Sps[:, c0 * 128:(c0 + cw) * 128],
                                    lhsT, kfm[:, g, c0:c0 + cw, :],
                                    start=True, stop=True)
                            else:
                                t.matmul(Sps[:, c0 * 128:nch * 128],
                                         lhsT, kfm[:, g, c0:nch, :],
                                         start=True, stop=False)
                                msk = bmask4 if cw > 2 else bmask2
                                mm = t.matmul(
                                    Sps[:, c0 * 128:nch * 128],
                                    idbf[:], msk[:],
                                    start=False, stop=True)
                        return mm
                    ct_S = B.ev("t", t_S, waits=DKQ_all + [(CV, CV_kv_all)],
                                sem=CT, inc=1)

                    def v_msk(v, nch=nch, keys=keys, Sps=Sps, ct=ct_S,
                              pc=pslot):
                        v.wait_ge(CT, ct)
                        if not MASK_MM:
                            v.tensor_add(
                                Sps[:, (nch - 2) * 128:(nch - 1) * 128],
                                Sps[:, (nch - 2) * 128:(nch - 1) * 128],
                                mpre[:])
                            i1 = v.tensor_add(
                                Sps[:, (nch - 1) * 128:nch * 128],
                                Sps[:, (nch - 1) * 128:nch * 128], mlast[:])
                            B.ch("v", v, i1)
                        i2 = v.reduce_max(mrow2[:, pc:pc + 1], Sps[:, 0:keys],
                                          axis=AX.X)
                        B.ch("v", v, i2)
                        return v.tensor_scalar_mul(mrow2[:, 2 + pc:3 + pc],
                                                   mrow2[:, pc:pc + 1],
                                                   -ATT_SCALE)
                    cv_m = B.ev("v", v_msk, sem=CV, inc=1)

                    def s_exp(s, keys=keys, pslot=pslot, cv=cv_m, Sps=Sps,
                              rel=cv_sc_slot[pslot]):
                        s.wait_ge(CV, cv)
                        if rel:
                            s.wait_ge(CV, rel)
                        return s.activation(
                            pexp[pslot][:, 0:keys],
                            Sps[:, 0:keys], AF.Exp,
                            bias=mrow2[:, 2 + pslot:3 + pslot],
                            scale=ATT_SCALE,
                            accum_out=lsum2[:, pslot:pslot + 1])
                    cs_exp_prev = B.ev("sc", s_exp, sem=CS, inc=1)
                    cs_exp_byreg[reg] = cs_exp_prev

                    def v_scale(v, j=j, h=h, h2=h2, keys=keys, pslot=pslot,
                                pslot2=pslot2, cs=cs_exp_prev,
                                reldp=dp_slot[pslot2]):
                        v.wait_ge(CS, cs)
                        if reldp:
                            v.wait_ge(DP[pslot2], reldp)
                        else:
                            v.wait_ge(CV, cv_pdinit)
                        i1 = v.reciprocal(mrow2[:, 4 + pslot:5 + pslot],
                                          lsum2[:, pslot:pslot + 1])
                        B.ch("v", v, i1)
                        i2 = v.tensor_mul(mrow2[:, 4 + pslot:5 + pslot],
                                          mrow2[:, 4 + pslot:5 + pslot],
                                          gtm[:, j, h:h + 1])
                        B.ch("v", v, i2)
                        return v.tensor_scalar_mul(
                            pbf2[pslot2][:, h2 * 2048:h2 * 2048 + keys],
                            pexp[pslot][:, 0:keys],
                            mrow2[:, 4 + pslot:5 + pslot])
                    cv_sc_pair = B.ev("v", v_scale, sem=CV, inc=1)
                    cv_sc_slot[pslot] = cv_sc_pair

                if PAIR_TP:
                    def s_tpp(s, hp=hp, pslot2=pslot2,
                              tslot=tslot, cv=cv_sc_pair,
                              relct=pt_rel[tslot]):
                        s.wait_ge(CV, cv)
                        if relct:
                            s.wait_ge(CT, relct)
                        return s.dma_start(
                            out=PT[tslot][:, 2 * hp:2 * hp + 2, :, :],
                            in_=pbf2[pslot2][:], transpose=True)
                    dt_p = B.ev("s", s_tpp, sem=DP[pslot2], inc=16)
                else:
                    B.cnt[DP[pslot2]] = B.cnt.get(DP[pslot2], 0) + 16

                    def s_tpp(s, nch=nch, keys=keys, hp=hp, pslot2=pslot2,
                              tslot=tslot, cv=cv_sc_pair,
                              relct=pt_rel[tslot]):
                        s.wait_ge(CV, cv)
                        if relct:
                            s.wait_ge(CT, relct)
                        s.dma_start(
                            out=PT[tslot][:, 2 * hp, 0:nch, :],
                            in_=pbf2[pslot2][:, 0:keys],
                            transpose=True).then_inc(DP[pslot2], 16)
                        return s.dma_start(
                            out=PT[tslot][:, 2 * hp + 1, 0:nch, :],
                            in_=pbf2[pslot2][:, 2048:2048 + keys],
                            transpose=True)
                    dt_p = B.ev("s", s_tpp, sem=DP[pslot2], inc=16)
                dp_slot[pslot2] = dt_p
                dtp_pair.append((pslot2, dt_p))

            def t_PV(t, j=j, g=g, nch=nch, keys=keys, tslot=tslot,
                     dtp=tuple(dtp_pair), cvp=cv_pv_prev):
                for (ps2, dv) in dtp:
                    t.wait_ge(DP[ps2], dv)
                if cvp:
                    t.wait_ge(CV, cvp)
                mm = None
                for c in range(nch):
                    mm = t.matmul(psB[:, 0:512],
                                  vtm[:, c, g * 128:(g + 1) * 128],
                                  PT[tslot][:, :, c, :],
                                  start=(c == 0), stop=(c == nch - 1))
                return mm
            ct_pv = B.ev("t", t_PV, waits=[(CV, CV_kv_all)], sem=CT, inc=1)
            pt_rel[tslot] = ct_pv

            def v_att(v, j=j, g=g, ct=ct_pv):
                v.wait_ge(CT, ct)
                return v.tensor_copy(attn[:, 4 * g:4 * g + 4, j, :],
                                     psB[:, 0:512])
            cv_pv_prev = B.ev("v", v_att, sem=CV, inc=1)

    B.dbg("attn", attn)
    CV_attn_done = B.val(CV)

    barrier("pe0")

    def psm(m):
        src = psA if m < 4 else psB
        return src[:, (m % 4) * 512:(m % 4 + 1) * 512]

    cv_evac_prev = 0
    ct_o = 0
    woT_rel = [0, 0]
    for n in range(4):
        oslot = n % 2
        dwo = wload(d_wo[:, n * 8192:(n + 1) * 8192],
                    woT[oslot][:].rearrange("p a b c -> p (a b c)"),
                    waits=[(CT, woT_rel[oslot])])

        for mg in range(2):
            def t_wo(t, n=n, mg=mg, oslot=oslot, dt=dwo, cvp=cv_evac_prev):
                t.wait_ge(dt[0], dt[1])
                if cvp:
                    t.wait_ge(CV, cvp)
                mm = None
                for m in range(4 * mg, 4 * mg + 4):
                    for k in range(16):
                        mm = t.matmul(psm(m), attn[:, k, m, :],
                                      woT[oslot][:, k, :, :],
                                      start=(k == 0), stop=(k == 15))
                return mm
            ct_o = B.ev("t", t_wo, waits=[(CV, CV_attn_done), (CS, B.val(CS))],
                        sem=CT, inc=1)
            woT_rel[oslot] = ct_o

            def v_evo(v, n=n, mg=mg, ct=ct_o):
                v.wait_ge(CT, ct)
                mm = None
                for m in range(4 * mg, 4 * mg + 4):
                    mm = v.tensor_copy(oacc[:, m, n * 512:(n + 1) * 512], psm(m))
                return mm
            cv_evac_prev = B.ev("v", v_evo, sem=CV, inc=1)

    barrier("p45")

    xrelQ = [(cv_evac_prev, B.val(CS))] * 2
    relQ_dt = [B.val(DQ[0]), B.val(DQ[1])]
    xnf_rel = [0, 0]
    cv_q_prev = 0
    for j in range(NO):
        slot = j % 2
        fslot = j % 2

        def s_ldxq(s, j=j, slot=slot, rel=xrelQ[slot], dxp=B.val(DXA[slot])):
            s.wait_ge(DXA[slot], dxp)
            if rel[0]:
                s.wait_ge(CV, rel[0])
            if rel[1]:
                s.wait_ge(CS, rel[1])
            return s.dma_start(out=xsB[slot][:], in_=d_xown[j * 128:(j + 1) * 128, :])
        dx = B.ev("s", s_ldxq, sem=DXA[slot], inc=16)

        def s_xn3(s, j=j, slot=slot, fslot=fslot, dx=dx, rel=xnf_rel[fslot]):
            s.wait_ge(DXA[slot], dx)
            if rel:
                s.wait_ge(CV, rel)
            return s.activation(xnf[fslot][:], xsB[slot][:], AF.Identity,
                                bias=bias_[:, 16 + j:17 + j],
                                scale=rstd[:, 16 + j:17 + j])
        cs_xn = B.ev("sc", s_xn3, sem=CS, inc=1)
        xrelQ[slot] = (0, cs_xn)

        def v_quant(v, j=j, fslot=fslot, cs=cs_xn, reldt=relQ_dt[fslot]):
            v.wait_ge(CS, cs)
            if reldt:
                v.wait_ge(DQ[fslot], reldt)
            xn = xnf[fslot][:]
            i1 = v.reduce_max(amax[:, j:j + 1], xn, axis=AX.X,
                              apply_absolute_value=True)
            B.ch("v", v, i1)
            i2 = v.tensor_mul(amax[:, j:j + 1], amax[:, j:j + 1],
                              rstd2[:, j:j + 1])
            B.ch("v", v, i2)
            i3 = v.reciprocal(n8[:, j:j + 1], amax[:, j:j + 1])
            B.ch("v", v, i3)
            i4 = v.tensor_scalar_mul(n8[:, j:j + 1], n8[:, j:j + 1], 127.0)
            B.ch("v", v, i4)
            v.reciprocal(ninvs[:, j:j + 1], n8[:, j:j + 1])
            i5 = v.tensor_mul(zsc[:, j:j + 1], rstd2[:, j:j + 1], n8[:, j:j + 1])
            B.ch("v", v, i5)
            v.tensor_scalar_mul(ninvs[:, j:j + 1], ninvs[:, j:j + 1], -1.0)
            i6 = v.tensor_scalar_mul(zb[:], xn, zsc[:, j:j + 1])
            B.ch("v", v, i6)
            i7 = v.tensor_copy(zi[:], zb[:])
            B.ch("v", v, i7)
            return v.tensor_copy(zk[:], zi[:])
        cv_q1 = B.ev("v", v_quant, sem=CV, inc=1)
        xnf_rel[fslot] = cv_q1

        def v_quant2(v, j=j, fslot=fslot, cv=cv_q1):
            v.wait_ge(CV, cv)
            i0 = v.tensor_sub(zd[:], zb[:], zk[:])
            B.ch("v", v, i0)
            i1 = v.scalar_tensor_tensor(zk[:], zd[:], 0.5, zk[:],
                                        ALU.is_ge, ALU.add)
            B.ch("v", v, i1)
            return v.scalar_tensor_tensor(xqtm[fslot][:], zd[:], -0.5, zk[:],
                                          ALU.is_lt, ALU.subtract)
        cv_q_prev = B.ev("v", v_quant2, sem=CV, inc=1)

        def s_tpxq(s, j=j, fslot=fslot, cv=cv_q_prev):
            s.wait_ge(CV, cv)
            return s.dma_start(out=xqfm[:, :, j, :], in_=xqtm[fslot][:],
                               transpose=True)
        relQ_dt[fslot] = B.ev("s", s_tpxq, sem=DQ[fslot], inc=16)
    B.dbg("xqfm", xqfm)
    B.dbg("ninvs", ninvs)
    B.dbg("zsc", zsc)
    cv_q_all = cv_q_prev

    def t_invT(t):
        mm = None
        for m in range(8):
            mm = t.transpose(psB[0:1, m * 128:(m + 1) * 128], ninvs[:, m:m + 1],
                             ident[:])
        return mm
    ct_it = B.ev("t", t_invT, waits=[(CV, max(cv_q_all, cv_evac_prev))],
                 sem=CT, inc=1)

    def v_row(v):
        v.wait_ge(CT, ct_it)
        return v.tensor_copy(rowb[:], psB[0:1, 0:1024])
    cv_row = B.ev("v", v_row, sem=CV, inc=1)

    def t_bc(t):
        t.wait_ge(CV, cv_row)
        t.matmul(psA[:, 0:512], ones_row[:], rowb[0:1, 0:512], start=True, stop=True)
        return t.matmul(psA[:, 512:1024], ones_row[:], rowb[0:1, 512:1024],
                        start=True, stop=True)
    ct_bc = B.ev("t", t_bc, sem=CT, inc=1)

    def v_bc(v):
        v.wait_ge(CT, ct_bc)
        return v.tensor_copy(nb_full[:], psA[:, 0:1024])
    cv_bc = B.ev("v", v_bc, sem=CV, inc=1)
    B.dbg("nb_full", nb_full)

    barrier("pe")

    cs_tanh_prev = 0
    cv_up_evac = 0
    up_evac_par = [0, 0]
    cv_dn_evac = cv_bc
    wgn_rel = [B.val(CT)] * 4
    for qt in range(4):
        for m in range(16):
            mh = qt * 16 + m
            for (wi, dsrc) in ((0, d_wg), (1, d_wu)):
                rslot = (mh * 2 + wi) % 4
                dt_w = wload(
                    dsrc[mh * 128:(mh + 1) * 128, :],
                    wgn[rslot][:].rearrange("p a b -> p (a b)"),
                    waits=[(CT, wgn_rel[rslot])], sign=True)

                upps = psA if m % 2 == 0 else psB

                def t_up(t, wi=wi, rslot=rslot, dt=dt_w, upps=upps,
                         cvp=max(up_evac_par[m % 2], cv_dn_evac)):
                    t.wait_ge(dt[0], dt[1])
                    if cvp:
                        t.wait_ge(CV, cvp)
                    off = 0 if wi == 0 else 1024
                    mm = None
                    for k in range(16):
                        for nn2 in range(2):
                            mm = t.matmul(
                                upps[:, off + nn2 * 512:off + (nn2 + 1) * 512],
                                wgn[rslot][:, k, :],
                                xqfm[:, k, nn2 * 4:(nn2 + 1) * 4, :],
                                start=(k == 0), stop=(k == 15))
                    return mm
                wgn_rel[rslot] = B.ev("t", t_up, sem=CT, inc=1)
            ct_up = wgn_rel[(mh * 2 + 1) % 4]

            def v_eps1(v, ct=ct_up, upps=upps, csrel=cs_tanh_prev,
                       cvrel=cv_up_evac):
                v.wait_ge(CT, ct)
                if csrel:
                    v.wait_ge(CS, csrel)
                if cvrel:
                    v.wait_ge(CV, cvrel)
                v.tensor_mul(ffa[:], upps[:, 0:1024], nb_full[:])
                return v.tensor_mul(ffu[:], upps[:, 1024:2048], nb_full[:])
            cv_e1 = B.ev("v", v_eps1, waits=[(CV, cv_bc)], sem=CV, inc=1)

            def s_tanh2(s, cv=cv_e1, cvrel=cv_up_evac):
                s.wait_ge(CV, cv)
                if cvrel:
                    s.wait_ge(CV, cvrel)
                return s.activation(fft[:], ffa[:], AF.Tanh, scale=0.5)
            cs_tanh_prev = B.ev("sc", s_tanh2, sem=CS, inc=1)

            def v_eps2(v, m=m, cs=cs_tanh_prev):
                v.wait_ge(CS, cs)
                v.tensor_mul(ffu[:], ffa[:], ffu[:])
                i1 = v.tensor_scalar(fft[:], fft[:], 0.5, 0.5, ALU.mult, ALU.add)
                B.ch("v", v, i1)
                return v.tensor_mul(pq[:, m, :, :], fft[:], ffu[:])
            cv_up_evac = B.ev("v", v_eps2, sem=CV, inc=1)
            up_evac_par[m % 2] = cv_up_evac
        cv_p_done = cv_up_evac

        wdT_rel = [B.val(CT)] * 2
        for n in range(4):
            dslot = n % 2
            sl = (qt * 4 + n) * 128
            dts = []
            for hh2 in range(2):
                dts.append(wload(
                    d_wd[sl:sl + 128, hh2 * 4096:(hh2 + 1) * 4096],
                    wdT[dslot][:, 8 * hh2:8 * hh2 + 8, :, :].rearrange(
                        "p a b c -> p (a b c)"),
                    waits=[(CT, wdT_rel[dslot])], sign=True))

            for mg in range(2):
                def t_dn(t, qt=qt, n=n, dslot=dslot, mg=mg, dt=tuple(dts),
                         cvpd=cv_p_done, cvev=cv_dn_evac):
                    for (dsem, dval) in dt:
                        t.wait_ge(dsem, dval)
                    t.wait_ge(CV, max(cvpd, cvev))
                    mm = None
                    for m in range(4 * mg, 4 * mg + 4):
                        for k in range(16):
                            mm = t.matmul(psm(m), pq[:, k, m, :],
                                          wdT[dslot][:, k, :, :],
                                          start=(k == 0), stop=(k == 15))
                    return mm
                ct_dn = B.ev("t", t_dn, sem=CT, inc=1)
                if mg == 1:
                    wdT_rel[dslot] = ct_dn

                def v_evd(v, n=n, mg=mg, ct=ct_dn):
                    v.wait_ge(CT, ct)
                    mm = None
                    for m in range(4 * mg, 4 * mg + 4):
                        mm = v.tensor_add(oacc[:, m, n * 512:(n + 1) * 512],
                                          oacc[:, m, n * 512:(n + 1) * 512],
                                          psm(m))
                    return mm
                cv_dn_evac = B.ev("v", v_evd, sem=CV, inc=1)

    barrier("fin")

    cs_all_pre_final = B.val(CS)
    xrelF = [(cv_dn_evac, cs_all_pre_final)] * 2
    ost_rel = [0, 0]
    for m in range(NO):
        slot = m % 2
        oslot = m % 2

        def s_ldxf(s, m=m, slot=slot, rel=xrelF[slot], dxp=B.val(DXA[slot])):
            s.wait_ge(DXA[slot], dxp)
            if rel[0]:
                s.wait_ge(CV, rel[0])
            if rel[1]:
                s.wait_ge(CS, rel[1])
            return s.dma_start(out=xsC[slot][:], in_=d_xown[m * 128:(m + 1) * 128, :])
        dx = B.ev("s", s_ldxf, sem=DXA[slot], inc=16)

        def v_fin(v, m=m, slot=slot, oslot=oslot, dx=dx, rel=ost_rel[oslot]):
            v.wait_ge(DXA[slot], dx)
            if rel:
                v.wait_ge(DS[oslot], rel)
            return v.tensor_add(ost[oslot][:], oacc[:, m, :], xsC[slot][:])
        cv_f = B.ev("v", v_fin, waits=[(CV, cv_dn_evac)], sem=CV, inc=1)
        xrelF[slot] = (cv_f, 0)

        def s_st(s, m=m, oslot=oslot, cv=cv_f):
            s.wait_ge(CV, cv)
            return s.dma_start(out=d_out[m * 128:(m + 1) * 128, :], in_=ost[oslot][:])
        ost_rel[oslot] = B.ev("s", s_st, sem=DS[oslot], inc=16)

    ds_fin = (B.val(DS[0]), B.val(DS[1]))
    if B.dbg_bufs:
        allw = [(CV, B.val(CV)), (CS, B.val(CS)), (CT, B.val(CT)),
                (CG, B.val(CG)), (DS[0], ds_fin[0]), (DS[1], ds_fin[1])]

        def s_dbg(s):
            for (name, buf, d) in B.dbg_bufs:
                s.dma_start(out=d, in_=buf[:]).then_inc(DDB, 16)
            return None
        B.ev("s", s_dbg, waits=allw)
        B.cnt[DDB] = B.cnt.get(DDB, 0) + 16 * len(B.dbg_bufs)

    def s_end(s, v0=ds_fin[0], v1=ds_fin[1], vd=B.val(DDB)):
        s.wait_ge(DS[0], v0)
        s.wait_ge(DS[1], v1)
        if vd:
            s.wait_ge(DDB, vd)
        return None
    B.ev("s", s_end)

    total, remain = B.pack()
    B.pack_info = (total, remain)
    B.emit()
    return B


ZMAP = [[2 * j for j in range(NO)], [2 * j + 1 for j in range(NO)]]


def stage_weights(wq, wk, wv, wo, wgs, wgw, wg, wu, wd):
    f = np.float32
    return {
        "wq": np.ascontiguousarray(np.asarray(wq, f).T),
        "wk": np.ascontiguousarray(
            np.asarray(wk, f).T.reshape(16, 128, 4, 128)
            .transpose(1, 0, 2, 3).reshape(128, 8192)),
        "wv": np.ascontiguousarray(
            np.asarray(wv, f).T.reshape(16, 128, 4, 128)
            .transpose(1, 0, 2, 3).reshape(128, 8192)),
        "wo": np.ascontiguousarray(
            np.asarray(wo, f).T.reshape(16, 128, 4, 4, 128)
            .transpose(1, 2, 0, 3, 4).reshape(128, 32768)),
        "wgs": np.ascontiguousarray(
            np.concatenate([np.asarray(wgs, f).T, np.asarray(wgw, f).T],
                           axis=1)
            .reshape(16, 128, 32).transpose(1, 0, 2).reshape(128, 512)),
        "wg": np.ascontiguousarray(
            np.asarray(wg, f).T.reshape(16, 128, 64, 128)
            .transpose(2, 1, 0, 3).reshape(8192, 2048)),
        "wu": np.ascontiguousarray(
            np.asarray(wu, f).T.reshape(16, 128, 64, 128)
            .transpose(2, 1, 0, 3).reshape(8192, 2048)),
        "wd": np.ascontiguousarray(
            np.asarray(wd, f).T.reshape(4, 16, 128, 4, 4, 128)
            .transpose(0, 3, 2, 1, 4, 5).reshape(2048, 8192)),
    }


def kernel(L0_attn_hidden_states, L0_attn_Wq, L0_attn_Wk, L0_attn_Wv,
           L0_attn_Wo, L0_attn_Wkc, L0_attn_Wgslc, L0_attn_Wgswa,
           L0_ffn_W_g, L0_ffn_W_u, L0_ffn_W_d, _dbg_names=(), _cores=8):
    x = np.ascontiguousarray(np.asarray(L0_attn_hidden_states, dtype=np.float32))
    B = build(_dbg_names)
    wmap = stage_weights(L0_attn_Wq, L0_attn_Wk, L0_attn_Wv, L0_attn_Wo,
                         L0_attn_Wgslc, L0_attn_Wgswa, L0_ffn_W_g,
                         L0_ffn_W_u, L0_ffn_W_d)
    in_maps = []
    for c in range(_cores):
        b, par = c // 2, c % 2
        rows = ZMAP[par]
        xo = np.ascontiguousarray(
            x[b].reshape(NT, 128, DM)[rows].reshape(NO * 128, DM))
        im = {
            "xseq": np.ascontiguousarray(x[b]),
            "xown": xo,
            "pm": np.full((128, 1), float(par), np.float32),
            "pm1": np.full((128, 1), float(1 - par), np.float32),
            **wmap,
        }
        in_maps.append(im)
    res = run_bass_kernel_spmd(B.nc, in_maps, core_ids=list(range(_cores)))
    out = np.empty((4, SEQ, DM), res.results[0]["oown"].dtype)
    for c in range(_cores):
        b, par = c // 2, c % 2
        o = res.results[c]["oown"].reshape(NO, 128, DM)
        for j, g in enumerate(ZMAP[par]):
            out[b, g * 128:(g + 1) * 128, :] = o[j]
    if _dbg_names:
        return out, res
    return out
